# version 04
# baseline (speedup 1.0000x reference)
import contextlib
import math

import numpy as np
import ml_dtypes
import concourse.bass as bass
import concourse.mybir as mybir
from concourse.bass_utils import run_bass_kernel_spmd

F32 = mybir.dt.float32
BF16 = mybir.dt.bfloat16
AF = mybir.ActivationFunctionType
ALU = mybir.AluOpType
AX = mybir.AxisListType
BFNP = ml_dtypes.bfloat16

NCORES = 8
B = 2
S = 4096
E = 2048
OWN = 512
TOK = B * OWN
WIN = 1024
WROWS = OWN + 2 * WIN
NT_W = WROWS // 128
NQC = OWN // 128
NKC = WROWS // 128
EC = E // 128
L0_KV = 4
L0_H = 16
CAP = 50.0
L1_H = 16
LAMBDA_INIT = 0.8 - 0.6 * float(np.exp(-0.3 * 12))
FFN_H = 8192
HC = FFN_H // 128
MOE_E = 8

ENGINES = ("pe", "dve", "act", "pool", "sync")
CCSEMS = ("cc0", "cc1")
DEBUG = False


class Em:
    def __init__(self, nc):
        self.nc = nc
        self.ops = {e: [] for e in ENGINES}
        self.cnt = {e: 0 for e in ENGINES + CCSEMS}
        self.wm = {e: {s: 0 for s in ENGINES} for e in ENGINES}
        self.pend = {e: {} for e in ENGINES}

    def op(self, eng, fn, deps=(), inc=1, sem=None, emit_inc=None):
        for ps, val in self.pend[eng].items():
            if self.wm[eng].get(ps, 0) < val:
                self.ops[eng].append(("w", ps, val))
                self.wm[eng][ps] = val
        self.pend[eng] = {}
        for dep in deps:
            if dep is None:
                continue
            ds, val = dep
            if ds == eng:
                if val > self.cnt[eng]:
                    continue
            if self.wm[eng].get(ds, 0) >= val or val <= 0:
                continue
            self.wm[eng][ds] = val
            self.ops[eng].append(("w", ds, val))
        key = sem if sem is not None else eng
        self.cnt[key] += inc
        self.ops[eng].append(("i", fn, inc if emit_inc is None else emit_inc,
                              sem))
        return (key, self.cnt[key])

    def dma(self, eng, out, in_, deps=(), **kw):
        return self.op(eng, lambda q: q.dma_start(out=out, in_=in_, **kw),
                       deps=deps, inc=16)

    def barrier(self, engines=ENGINES, on=ENGINES):
        snap = {e: self.cnt[e] for e in on}
        for e in engines:
            for s, v in snap.items():
                if v > 0 and s != e and self.pend[e].get(s, 0) < v:
                    self.pend[e][s] = v
        return snap

    def token(self, eng):
        return (eng, self.cnt[eng])

    def replay(self):
        nc = self.nc
        with contextlib.ExitStack() as st:
            sems = {e: st.enter_context(nc.semaphore("sem_" + e))
                    for e in ENGINES + CCSEMS}
            block = st.enter_context(nc.Block())
            eng_obj = {"pe": nc.tensor, "dve": nc.vector, "act": nc.scalar,
                       "pool": nc.gpsimd, "sync": nc.sync}
            deco = {"pe": block.tensor, "dve": block.vector, "act": block.scalar,
                    "pool": block.gpsimd, "sync": block.sync}

            def mk(ename):
                ops = self.ops[ename]
                my = eng_obj[ename]
                sem = sems[ename]

                def body(_q):
                    for rec in ops:
                        if rec[0] == "w":
                            my.wait_ge(sems[rec[1]], rec[2])
                        else:
                            inst = rec[1](my)
                            if rec[2]:
                                tgt = sems[rec[3]] if rec[3] else sem
                                inst.then_inc(tgt, rec[2])
                return body

            for e in ENGINES:
                deco[e](mk(e))


def ts(i, size=128):
    return slice(i * size, (i + 1) * size)


class Ctx:
    def __init__(self, nc, em):
        self.nc = nc
        self.em = em
        self._n = 0
        self.pt = {}
        self._handles = {}
        self._order = {}

    def sb(self, shape, dtype=F32, name=None):
        self._n += 1
        h = self.nc.sbuf_tensor(f"{name or 't'}_{self._n}", list(shape), dtype)
        t = h.__enter__()
        self._handles[id(t)] = h
        self._order[id(t)] = self._n
        return t

    def ps(self, shape, dtype=F32, name=None):
        self._n += 1
        h = self.nc.psum_tensor(f"{name or 'p'}_{self._n}", list(shape), dtype)
        t = h.__enter__()
        self._handles[id(t)] = h
        return t

    def free(self, *tensors):
        self.em.barrier()
        for t in sorted(tensors, key=lambda x: -self._order[id(x)]):
            self._handles.pop(id(t)).__exit__(None, None, None)

    def pdep(self, pt):
        return self.pt.get(id(pt))

    def puse(self, pt, tok):
        self.pt[id(pt)] = tok


def emit_consts(cx, AP):
    em = cx.em
    cx.ident = cx.sb([128, 128], BF16, "ident")
    cx.ones_cb = cx.sb([128, 1], BF16, "ones_cb")
    cx.ones_cf = cx.sb([128, 1], F32, "ones_cf")
    cx.ones_rf = cx.sb([1, 128], F32, "ones_rf")
    cx.t_ident = em.dma("sync", cx.ident[:], AP["c_ident"])
    m1 = em.op("dve", lambda v: v.memset(cx.ones_cb[:], 1.0), inc=0)
    m2 = em.op("dve", lambda v: v.memset(cx.ones_cf[:], 1.0), inc=0)
    cx.t_ones = em.op("dve", lambda v: v.memset(cx.ones_rf[:], 1.0), inc=1)
    cx.prot = cx.sb([128, 128], F32, "prot")
    cx.t_prot = em.dma("sync", cx.prot[:], AP["c_prot"])
    cx.bias_ln = cx.sb([128, 1], F32, "bias_ln")
    cx.bias_eps = cx.sb([128, 1], F32, "bias_eps")
    cx.bias_ncap = cx.sb([128, 1], F32, "bias_ncap")
    em.op("dve", lambda v: v.memset(cx.bias_ln[:], 1e-5), inc=0)
    em.op("dve", lambda v: v.memset(cx.bias_eps[:], 1.1920929e-07), inc=0)
    cx.t_bias = em.op("dve", lambda v: v.memset(cx.bias_ncap[:], -CAP), inc=1)


def layernorm_tile(cx, xt, out_bf, scr, stats, deps):
    em = cx.em
    s1 = stats[:, 0:1]; s2 = stats[:, 1:2]; mcol = stats[:, 2:3]
    vcol = stats[:, 3:4]; sdcol = stats[:, 4:5]; rcol = stats[:, 5:6]
    deps = list(deps)
    t1 = em.op("dve", lambda v: v.tensor_reduce(s1, xt[:], AX.X, ALU.add), deps=deps)
    t2 = em.op("act", lambda a: a.activation(scr[:], xt[:], AF.Square), deps=deps)
    t3 = em.op("dve", lambda v: v.tensor_reduce(s2, scr[:], AX.X, ALU.add), deps=[t2, t1])
    t4 = em.op("dve", lambda v: v.tensor_scalar(mcol, s1, 1.0 / E, None, ALU.mult), deps=[t3])
    t5 = em.op("dve", lambda v: v.tensor_scalar(vcol, s2, 1.0 / E, None, ALU.mult), deps=[t4])
    t6 = em.op("dve", lambda v: v.tensor_tensor(sdcol, mcol, mcol, ALU.mult), deps=[t5])
    t7 = em.op("dve", lambda v: v.tensor_tensor(vcol, vcol, sdcol, ALU.subtract), deps=[t6])
    t8 = em.op("act", lambda a: a.activation(sdcol, vcol, AF.Sqrt, bias=cx.bias_ln[:]),
               deps=[t7, cx.t_bias])
    t9 = em.op("dve", lambda v: v.reciprocal(rcol, sdcol), deps=[t8])
    tA = em.op("dve", lambda v: v.tensor_scalar(out_bf[:], xt[:], mcol, rcol,
                                                ALU.subtract, ALU.mult), deps=[t9])
    return tA




def _rope_tables_np(positions):
    half = 64
    inv = (1.0 / 10000.0 ** (np.arange(0, half, dtype=np.float32) / half))
    ang = np.outer(positions.astype(np.float32), inv)
    ang = np.concatenate([ang, ang], -1)
    cos = np.cos(ang).astype(np.float32)
    sin = np.sin(ang).astype(np.float32)
    sinA = sin.copy()
    sinA[:, :half] *= -1.0
    return np.ascontiguousarray(cos.T), np.ascontiguousarray(sinA.T)


def l0_mask_pairs():
    pairs = []
    for kc in range(NKC):
        qlo, qhi = max(0, kc - 16), min(NQC - 1, kc)
        for qc in range(qlo, qhi + 1):
            if kc - qc in (0, 16) or kc < WIN // 128 or kc >= (WIN + OWN) // 128:
                pairs.append((kc, qc))
    return pairs


_L0_PAIRS = l0_mask_pairs()
_L0_PAIR_IDX = {p: i for i, p in enumerate(_L0_PAIRS)}


def l0_mask_values(core):
    q0 = core * OWN
    out = np.zeros((len(_L0_PAIRS), 128, 128), np.float32)
    for i, (kc, qc) in enumerate(_L0_PAIRS):
        kpos = q0 - WIN + kc * 128 + np.arange(128)
        qpos = q0 + qc * 128 + np.arange(128)
        valid = (np.abs(qpos[None, :] - kpos[:, None]) <= WIN) & \
                (kpos[:, None] >= 0) & (kpos[:, None] < S)
        out[i] = valid.astype(np.float32)
    return np.ascontiguousarray(out.astype(BFNP))


def declare_l0_inputs(nc):
    ap = {}
    def inp(n, sh, dt=F32):
        ap[n] = nc.dram_tensor(n, sh, dt, kind="ExternalInput").ap()
    inp("xw", [B, WROWS, E])
    inp("c_ident", [128, 128], BF16)
    inp("c_prot", [128, 128])
    inp("c_cosw", [128, WROWS]); inp("c_sinw", [128, WROWS])
    inp("c_l0mask", [len(_L0_PAIRS), 128, 128], BF16)
    inp("qscale", [128, 1]); inp("kscale", [128, 1])
    inp("L0_attn_wq", [E, E]); inp("L0_attn_wk", [E, 512])
    inp("L0_attn_wv", [E, 512]); inp("L0_attn_wo", [E, E])
    inp("L0_ffn_W_fc", [FFN_H, E]); inp("L0_ffn_W_proj", [E, FFN_H])
    return ap




def transpose128(cx, dst_ap, src_ap, deps):
    em = cx.em
    bank = cx.tr_rot % 2
    cx.tr_rot += 1
    pb = cx.pp["tr0" if bank == 0 else "tr1"]
    tt = em.op("pe", lambda t: t.transpose(pb[:, 0:128], src_ap, cx.ident[:]),
               deps=list(deps) + [cx.t_ident, cx.pdep(pb)])
    te = em.op("dve", lambda v: v.tensor_copy(dst_ap, pb[:, 0:128]), deps=[tt])
    cx.puse(pb, te)
    return te


def rope_norm(cx, raw, dst_bf, width, scale_col, rc, rs, sc1, sc2, sqs, rows,
              deps):
    em = cx.em
    rowp = cx.pp["row"]
    bc = cx.pp["c"]
    rotp = cx.pp["o"]
    deps = list(deps)
    a1 = em.op("dve", lambda v: v.tensor_tensor(sc1[:, 0:width], raw[:, 0:width],
                                                rc[:, 0:width], ALU.mult), deps=deps)
    a2 = em.op("pe", lambda t: t.matmul(rotp[:, 0:width], cx.prot[:], raw[:, 0:width],
                                        start=True, stop=True),
               deps=deps + [cx.t_prot, cx.pdep(rotp)])
    a3 = em.op("dve", lambda v: v.tensor_tensor(sc2[:, 0:width], rotp[:, 0:width],
                                                rs[:, 0:width], ALU.mult), deps=[a2])
    cx.puse(rotp, a3)
    a4 = em.op("dve", lambda v: v.tensor_tensor(sc1[:, 0:width], sc1[:, 0:width],
                                                sc2[:, 0:width], ALU.add), deps=[a1, a3])
    a5 = em.op("act", lambda a: a.activation(sqs[:, 0:width], sc1[:, 0:width], AF.Square),
               deps=[a4])
    np_ = (width + 511) // 512
    sq_toks = []
    for p in range(np_):
        w0, w1 = p * 512, min(width, (p + 1) * 512)
        tm = em.op("pe", lambda t, w0=w0, w1=w1: t.matmul(
            rowp[0:1, 0:w1 - w0], cx.ones_cb[:], sqs[:, w0:w1], start=True, stop=True),
            deps=[a5, cx.t_ones, cx.pdep(rowp)])
        tq = em.op("act", lambda a, w0=w0, w1=w1: a.activation(
            rows[:, w0:w1], rowp[0:1, 0:w1 - w0], AF.Sqrt), deps=[tm])
        cx.puse(rowp, tq)
        sq_toks.append(tq)
    a7 = em.op("dve", lambda v: v.tensor_scalar(rows[:, 0:width], rows[:, 0:width],
                                                1e-6, None, ALU.add), deps=sq_toks)
    a8 = em.op("dve", lambda v: v.reciprocal(rows[:, 0:width], rows[:, 0:width]), deps=[a7])
    out_toks = []
    for p in range(np_):
        w0, w1 = p * 512, min(width, (p + 1) * 512)
        tb = em.op("pe", lambda t, w0=w0, w1=w1: t.matmul(
            bc[:, 0:w1 - w0], cx.ones_rf[:], rows[:, w0:w1], start=True, stop=True),
            deps=[a8, cx.t_ones, cx.pdep(bc)])
        tm = em.op("dve", lambda v, w0=w0, w1=w1: v.tensor_tensor(
            sc2[:, w0:w1], sc1[:, w0:w1], bc[:, 0:w1 - w0], ALU.mult), deps=[tb])
        cx.puse(bc, tm)
        tc = em.op("act", lambda a, w0=w0, w1=w1: a.activation(
            dst_bf[:, w0:w1], sc2[:, w0:w1], AF.Copy, scale=scale_col), deps=[tm])
        out_toks.append(tc)
    return out_toks


L0_D_CONST = 128


def xn_own_cols(xn_own, e, b):
    return xn_own[e][:, b * OWN:(b + 1) * OWN]


def emit_l0(cx, AP, xn_own, u_dram, acc_dram):
    em = cx.em

    rope_cos = cx.sb([128, WROWS], BF16, "ropec")
    rope_sin = cx.sb([128, WROWS], BF16, "ropes")
    qscale = cx.sb([128, 1], F32, "qscale")
    kscale = cx.sb([128, 1], F32, "kscale")
    t_rc = em.dma("pool", rope_cos[:], AP["c_cosw"])
    t_rs = em.dma("pool", rope_sin[:], AP["c_sinw"])
    t_qs = em.dma("sync", qscale[:], AP["qscale"])
    t_ks = em.dma("sync", kscale[:], AP["kscale"])

    pA = pB = cx.pp["a"]

    sc1 = cx.sb([128, 512], F32, "sc1")
    sc2 = cx.sb([128, 512], F32, "sc2")
    sqs = cx.sb([128, 512], BF16, "sqs")
    rows = cx.sb([1, 512], F32, "rows")
    kraw = cx.sb([128, 512], F32, "kraw")
    vpc = cx.sb([128, 512], BF16, "vpc")
    wblk = [cx.sb([128, EC * 128], BF16, f"wblk{i}") for i in range(2)]
    npieces = WROWS // 512

    for b in range(B):
        K = [cx.sb([128, WROWS], BF16, f"K_{h}") for h in range(L0_KV)]
        Vt = [cx.sb([128, NKC * 128], BF16, f"V_{h}") for h in range(L0_KV)]
        xnw = [cx.sb([128, WROWS], BF16, f"xnw_{e}") for e in range(EC)]
        xt = [cx.sb([128, E], F32, "lnx0")]
        scr = cx.sb([128, E], F32, "lnscr")
        xbf = cx.sb([128, E], BF16, "lnbf")
        stats = cx.sb([128, 8], F32, "lnstat")
        tile_done = []
        prev_use = [None]
        for t in range(NT_W):
            buf = xt[0]
            tl = em.dma("sync", buf[:], AP["xw"][b, ts(t), :], deps=[prev_use[0]])
            ln = layernorm_tile(cx, buf, xbf, scr, stats, [tl])
            last = None
            for e in range(EC):
                last = transpose128(cx, xnw[e][:, ts(t)], xbf[:, ts(e)], [ln])
            tile_done.append(last)
            prev_use[0] = last
        if DEBUG and b == 0 and hasattr(cx.nc, "_dbg_xnw"):
            em.dma("pool", cx.nc._dbg_xnw[0:128, :], xnw[4][:], deps=tile_done)
            em.dma("pool", cx.nc._dbg_xnw[128:256, :], xnw[5][:], deps=tile_done)
            em.barrier()
        for e in range(EC):
            em.op("dve", lambda v, e=e, b=b: v.tensor_copy(
                xn_own[e][:, b * OWN:(b + 1) * OWN], xnw[e][:, WIN:WIN + OWN]),
                deps=[tile_done[-1]] if e == 0 else ())
        t_xnown = em.token("dve")
        cx.free(xt[0], scr, xbf, stats)

        wtoks = [None, None]
        kv_done = {}
        q_done = {}
        for kh in range(L0_KV):
            wb = wblk[kh % 2]
            t_wk = em.dma("pool", wb[:].rearrange("p (ec i) -> p ec i", i=128),
                          AP["L0_attn_wk"][:, ts(kh)].rearrange("(ec p) i -> p ec i", p=128),
                          deps=[wtoks[kh % 2]])
            for p in range(npieces):
                pm = pA
                mmlast = None
                for e in range(EC):
                    mmlast = em.op("pe", lambda t, e=e, p=p, wb=wb, pm=pm: t.matmul(
                        pm[:], wb[:, ts(e)],
                        xnw[e][:, ts(p, 512)], start=(e == 0), stop=(e == EC - 1)),
                        deps=([t_wk, cx.pdep(pm)] + tile_done) if e == 0 else ())
                wtoks[kh % 2] = mmlast
                tev = em.op("dve", lambda v, pm=pm: v.tensor_copy(kraw[:], pm[:]),
                            deps=[mmlast])
                cx.puse(pm, tev)
                kv_done.setdefault(kh, []).extend(rope_norm(
                    cx, kraw, K[kh][:, ts(p, 512)], 512, kscale[:],
                    rope_cos[:, ts(p, 512)], rope_sin[:, ts(p, 512)],
                    sc1, sc2, sqs, rows, [tev, t_rc, t_rs, t_ks]))
            t_wv = em.dma("pool", wb[:].rearrange("p (ec i) -> p ec i", i=128),
                          AP["L0_attn_wv"][:, ts(kh)].rearrange("(ec p) i -> p ec i", p=128),
                          deps=[wtoks[kh % 2], em.token("pe")])
            for p in range(npieces):
                pm = pA
                mmlast = None
                for e in range(EC):
                    mmlast = em.op("pe", lambda t, e=e, p=p, wb=wb, pm=pm: t.matmul(
                        pm[:], wb[:, ts(e)],
                        xnw[e][:, ts(p, 512)], start=(e == 0), stop=(e == EC - 1)),
                        deps=[t_wv, cx.pdep(pm)] if e == 0 else ())
                wtoks[kh % 2] = mmlast
                tev = em.op("dve", lambda v, pm=pm: v.tensor_copy(vpc[:], pm[:]),
                            deps=[mmlast])
                cx.puse(pm, tev)
                for j in range(4):
                    transpose128(cx, Vt[kh][:, ts(p * 4 + j)], vpc[:, ts(j)], [tev])
        t_kv = em.token("dve")
        if DEBUG and b == 0 and hasattr(cx.nc, "_dbg_k"):
            em.dma("pool", cx.nc._dbg_k, K[0][:], deps=[t_kv])
            for e in range(EC):
                em.dma("pool", cx.nc._dbg_xn[ts(e), :], xn_own[e][:], deps=[t_xnown])
            em.barrier()
        cx.free(*xnw)

        Qf = [cx.sb([128, OWN], BF16, f"Qf_{h}") for h in range(L0_H)]
        qraw = cx.sb([128, OWN], F32, "qraw")
        for h in range(L0_H):
            wb = wblk[h % 2]
            t_wq = em.dma("pool", wb[:].rearrange("p (ec i) -> p ec i", i=128),
                          AP["L0_attn_wq"][:, ts(h)].rearrange("(ec p) i -> p ec i", p=128),
                          deps=[wtoks[h % 2], em.token("pe")])
            mmlast = None
            for e in range(EC):
                mmlast = em.op("pe", lambda t, e=e, b=b, wb=wb: t.matmul(
                    pA[:], wb[:, ts(e)], xn_own_cols(xn_own, e, b),
                    start=(e == 0), stop=(e == EC - 1)),
                    deps=[t_wq, cx.pdep(pA), t_xnown] if e == 0 else ())
            wtoks[h % 2] = mmlast
            tev = em.op("dve", lambda v: v.tensor_copy(qraw[:], pA[:]), deps=[mmlast])
            cx.puse(pA, tev)
            q_done[h] = rope_norm(cx, qraw, Qf[h], OWN, qscale[:],
                      rope_cos[:, WIN:WIN + OWN], rope_sin[:, WIN:WIN + OWN],
                      sc1, sc2, sqs, rows, [tev, t_qs])

        if DEBUG and b == 0 and hasattr(cx.nc, "_dbg_q"):
            em.dma("pool", cx.nc._dbg_q, Qf[0][:], deps=[em.token("act"), em.token("dve")])
            em.barrier()
        masks = cx.sb([128, len(_L0_PAIRS) * 128], BF16, "l0mask")
        t_mk = em.dma("sync", masks[:].rearrange("p (n k) -> p n k", k=128),
                      AP["c_l0mask"].rearrange("n p k -> p n k"))
        attn = [cx.sb([128, OWN], BF16, f"attn_{h}") for h in range(L0_H)]
        tS = cx.sb([128, 512], F32, "tS")
        pT = cx.sb([128, 512], BF16, "pTl0")
        rl = cx.sb([1, 512], F32, "rl")
        of = cx.sb([128, 512], F32, "of")
        ps_l = cx.pp["row"]
        ps_o = cx.pp["o"]
        ps_bc = cx.pp["c"]
        for h in range(L0_H):
            kh = h // (L0_H // L0_KV)
            t_pT_prev = None
            tl = tv = None
            for kc in range(NKC):
                qlo, qhi = max(0, kc - 16), min(NQC - 1, kc)
                r0, r1 = qlo * 128, (qhi + 1) * 128
                tmm = em.op("pe", lambda t, kc=kc, kh=kh, h=h, r0=r0, r1=r1: t.matmul(
                    pA[:, 0:r1 - r0], K[kh][:, ts(kc)], Qf[h][:, r0:r1],
                    start=True, stop=True),
                    deps=[cx.pdep(pA)] + (kv_done[kh] + q_done[h] if kc == 0 else []))
                tt = em.op("act", lambda a, r0=r0, r1=r1: a.activation(
                    tS[:, 0:r1 - r0], pA[:, 0:r1 - r0], AF.Tanh,
                    scale=1.0 / (CAP * math.sqrt(L0_D_CONST))), deps=[tmm])
                cx.puse(pA, tt)
                tex = em.op("act", lambda a, r0=r0, r1=r1: a.activation(
                    pT[:, 0:r1 - r0], tS[:, 0:r1 - r0], AF.Exp, scale=CAP,
                    bias=cx.bias_ncap[:]), deps=[tt, t_pT_prev, cx.t_bias])
                mtoks = [tex]
                for qc in range(qlo, qhi + 1):
                    pi = _L0_PAIR_IDX.get((kc, qc))
                    if pi is not None:
                        mtoks.append(em.op("pool", lambda g, pi=pi, qc=qc, r0=r0: g.tensor_tensor(
                            pT[:, qc * 128 - r0: qc * 128 - r0 + 128],
                            pT[:, qc * 128 - r0: qc * 128 - r0 + 128],
                            masks[:, ts(pi)], ALU.mult), deps=[tex, t_mk]))
                tl = em.op("pe", lambda t, kc=kc, r0=r0, r1=r1: t.matmul(
                    ps_l[0:1, r0:r1], cx.ones_cb[:], pT[:, 0:r1 - r0],
                    start=(kc == 0), stop=(kc == NKC - 1)),
                    deps=mtoks + ([cx.pdep(ps_l)] if kc == 0 else []))
                tv = em.op("pe", lambda t, kc=kc, kh=kh, r0=r0, r1=r1: t.matmul(
                    ps_o[:, r0:r1], Vt[kh][:, ts(kc)], pT[:, 0:r1 - r0],
                    start=(kc == 0), stop=(kc == NKC - 1)),
                    deps=([t_kv, cx.pdep(ps_o)] if kc == 0 else []))
                t_pT_prev = tv
            trc = em.op("dve", lambda v: v.reciprocal(rl[:], ps_l[0:1, 0:512]), deps=[tl, tv])
            cx.puse(ps_l, trc)
            tb = em.op("pe", lambda t: t.matmul(ps_bc[:], cx.ones_rf[:], rl[:],
                                                start=True, stop=True),
                       deps=[trc, cx.pdep(ps_bc)])
            tof = em.op("dve", lambda v: v.tensor_copy(of[:], ps_o[:, 0:512]), deps=[tv, tb])
            cx.puse(ps_o, tof)
            tat = em.op("dve", lambda v, h=h: v.tensor_tensor(
                attn[h][:], of[:], ps_bc[:], ALU.mult), deps=[tof])
            cx.puse(ps_bc, tat)

        cx.free(tS, pT, rl, of)

        t_attn_all = em.token("dve")
        if DEBUG and b == 0 and hasattr(cx.nc, "_dbg_at"):
            em.dma("pool", cx.nc._dbg_at, attn[0][:], deps=[t_attn_all])
            em.barrier()
        wo_grp = cx.sb([128, 4 * E], BF16, "wo_grp")
        acc = [cx.sb([128, E], F32, f"accw{t}") for t in range(NQC)]
        ps_w = cx.pp["w"]
        tev = None
        for g in range(4):
            t_wo = em.dma("pool", wo_grp[:].rearrange("p (hh e) -> p hh e", e=E),
                          AP["L0_attn_wo"][ts(g, 512), :].rearrange("(hh p) e -> p hh e", p=128),
                          deps=[getattr(cx, "pdep_wo", None)])
            for tc in range(NQC):
                for half in range(2):
                    mmlast = None
                    for hh in range(4):
                        h = g * 4 + hh
                        for pc in range(2):
                            mmlast = em.op("pe", lambda t, h=h, hh=hh, tc=tc, pc=pc, half=half: t.matmul(
                                ps_w[:, ts(pc, 512)], attn[h][:, ts(tc)],
                                wo_grp[:, hh * E + half * 1024 + pc * 512:
                                       hh * E + half * 1024 + (pc + 1) * 512],
                                start=(hh == 0), stop=(hh == 3)),
                                deps=[t_wo, cx.pdep(ps_w), t_attn_all]
                                if (hh == 0 and pc == 0) else ())
                    if g == 0:
                        tev = em.op("dve", lambda v, tc=tc, half=half: v.tensor_copy(
                            acc[tc][:, half * 1024:(half + 1) * 1024], ps_w[:]),
                            deps=[mmlast])
                    else:
                        tev = em.op("dve", lambda v, tc=tc, half=half: v.tensor_tensor(
                            acc[tc][:, half * 1024:(half + 1) * 1024],
                            acc[tc][:, half * 1024:(half + 1) * 1024],
                            ps_w[:], ALU.add), deps=[mmlast])
                    cx.puse(ps_w, tev)
            cx.pdep_wo = tev
        sp_toks = []
        for tc in range(NQC):
            sp_toks.append(em.dma("pool", acc_dram[(b * NQC + tc) * 128:
                                                   (b * NQC + tc + 1) * 128, :],
                                  acc[tc][:], deps=[tev]))
        em.barrier()
        cx.free(*(acc + [wo_grp] + attn + [masks, qraw] + Qf + Vt + K))

    cx.free(sc1, sc2, sqs, rows, kraw, vpc, wblk[0], wblk[1],
            rope_cos, rope_sin, qscale, kscale)
    em.barrier()


def quantize_tile(cx, wt_bf, gscr, width, deps):
    em = cx.em
    ng = width // 64
    s_mean = gscr[:, 0:ng]
    s_meanb = cx.qz_meanb[:, 0:ng]
    s_recip = gscr[:, ng:2 * ng]
    deps = list(deps)
    t1 = em.op("dve", lambda v: v.tensor_reduce(
        s_mean, wt_bf[:, 0:width].rearrange("p (g k) -> p g k", k=64),
        AX.X, ALU.add, apply_absolute_value=True), deps=deps)
    t2 = em.op("act", lambda a: a.activation(s_meanb, s_mean, AF.Copy, scale=1.0 / 64.0),
               deps=[t1])
    t3 = em.op("dve", lambda v: v.tensor_scalar(s_meanb, s_meanb, 1e-8, None, ALU.max),
               deps=[t2])
    t4 = em.op("dve", lambda v: v.reciprocal(s_recip, s_meanb), deps=[t3])
    rb = cx.qz_ratio[:, 0:width]
    qa = cx.qz_qa[:, 0:width]
    t5 = em.op("dve", lambda v: v.tensor_tensor(
        rb.rearrange("p (g k) -> p g k", k=64),
        wt_bf[:, 0:width].rearrange("p (g k) -> p g k", k=64),
        s_recip.unsqueeze(2).broadcast_to((128, ng, 64)), ALU.mult),
        deps=[t4])
    t6 = em.op("pool", lambda g: g.tensor_scalar(qa, rb, 0.5, None, ALU.is_gt), deps=[t5])
    t7 = em.op("dve", lambda v: v.tensor_scalar(rb, rb, -0.5, None, ALU.is_lt), deps=[t6])
    t8 = em.op("pool", lambda g: g.tensor_tensor(qa, qa, rb, ALU.subtract), deps=[t7])
    t9 = em.op("dve", lambda v: v.tensor_tensor(
        wt_bf[:, 0:width].rearrange("p (g k) -> p g k", k=64),
        qa.rearrange("p (g k) -> p g k", k=64),
        s_meanb.unsqueeze(2).broadcast_to((128, ng, 64)), ALU.mult),
        deps=[t8])
    return t9


def emit_l0_ffn(cx, AP, xn_own, u_dram, acc_dram, acc):
    em = cx.em
    cx.qz_meanb = cx.sb([128, 512], BF16, "qz_meanb")
    cx.qz_ratio = cx.sb([128, 8192], BF16, "qz_ratio")
    cx.qz_qa = cx.sb([128, 8192], BF16, "qz_qa")
    gscr = cx.sb([128, 1024], F32, "qz_gscr")

    pA = pB = cx.pp["a"]
    msacc = cx.sb([128, 8], F32, "msacc")
    rstd = cx.sb([128, 8], F32, "rstd")
    t_ms0 = em.op("dve", lambda v: v.memset(msacc[:], 0.0), inc=1)

    wfc_bf = cx.sb([128, 4 * E], BF16, "wfc_bf")
    w1qT = cx.sb([128, EC * 512], BF16, "w1qT")
    rrelu = cx.sb([128, 512], F32, "rrelu")
    u32 = cx.sb([128, 512], F32, "u32")
    ubf = cx.sb([128, 512], BF16, "ubf")
    u2d = cx.sb([128, 512], BF16, "u2dump")
    part = cx.sb([128, 1], F32, "part")
    uT = cx.sb([128, 512], BF16, "uT")

    for grp in range(16):
        qz = []
        for j in range(4):
            tl = em.dma("pool", wfc_bf[:, j * E:(j + 1) * E],
                        AP["L0_ffn_W_fc"][ts(grp * 4 + j), :],
                        deps=[getattr(cx, "pdep_wfc", None)])
            qz.append(quantize_tile(cx, wfc_bf[:, j * E:(j + 1) * E], gscr, E, [tl]))
        ttok = []
        for j in range(4):
            for e in range(EC):
                ttok.append(transpose128(cx, w1qT[:, e * 512 + j * 128: e * 512 + (j + 1) * 128],
                                         wfc_bf[:, j * E + e * 128: j * E + (e + 1) * 128],
                                         [qz[j]]))
        cx.pdep_wfc = ttok[-1]
        for b in range(B):
            for tc in range(NQC):
                pm = pA
                mmlast = None
                for e in range(EC):
                    mmlast = em.op("pe", lambda t, e=e, b=b, tc=tc, pm=pm: t.matmul(
                        pm[:], xn_own[e][:, b * OWN + tc * 128: b * OWN + (tc + 1) * 128],
                        w1qT[:, ts(e, 512)], start=(e == 0), stop=(e == EC - 1)),
                        deps=(ttok + [cx.pdep(pm)]) if e == 0 else ())
                t_r = em.op("act", lambda a, pm=pm: a.activation(rrelu[:], pm[:], AF.Relu),
                            deps=[mmlast])
                cx.puse(pm, t_r)
                t_u = em.op("act", lambda a: a.activation(u32[:], rrelu[:], AF.Square),
                            deps=[t_r])
                t_ub = em.op("dve", lambda v: v.tensor_copy(ubf[:], u32[:]), deps=[t_u])
                t_u2 = em.op("act", lambda a: a.activation(
                    u2d[:], u32[:], AF.Square, accum_out=part[:]), deps=[t_u, t_ub])
                col = b * NQC + tc
                em.op("dve", lambda v, col=col: v.tensor_tensor(
                    msacc[:, col:col + 1], msacc[:, col:col + 1], part[:], ALU.add),
                    deps=[t_u2, t_ms0])
                utok = []
                for j in range(4):
                    utok.append(transpose128(cx, uT[:, ts(j)], ubf[:, ts(j)],
                                             [t_ub, getattr(cx, "t_uspill", None)]))
                cx.t_uspill = em.dma("pool", u_dram[ts(grp, 512), col * 128:(col + 1) * 128]
                                     .rearrange("(hs p) t -> p hs t", p=128),
                                     uT[:].rearrange("p (hs t) -> p hs t", t=128), deps=utok)
    t_sq = em.op("act", lambda a: a.activation(rstd[:], msacc[:], AF.Sqrt,
                                               scale=1.0 / FFN_H, bias=cx.bias_eps[:]),
                 deps=[em.token("dve"), cx.t_bias])
    t_rsd = em.op("dve", lambda v: v.reciprocal(rstd[:], rstd[:]), deps=[t_sq])

    cx.free(wfc_bf, w1qT, rrelu, u32, ubf, u2d, part, uT)
    em.barrier()

    t_acc = []
    for i in range(B * NQC):
        t_acc.append(em.dma("sync", acc[i][:], acc_dram[ts(i), :]))

    w2_bf = cx.sb([128, 4 * 1024], BF16, "w2_bf")
    w2qT = cx.sb([128, 8 * E], BF16, "w2qT")
    hn_rd = cx.sb([128, 8 * 128], BF16, "hn_rd")
    tmp_o = cx.sb([128, 1024], F32, "tmp_o")
    ps_w = cx.pp["w"]
    HG = 1024
    for grp in range(FFN_H // HG):
        ttok = []
        for sl in range(4):
            tl = em.dma("pool", w2_bf[:].rearrange("p (dc h) -> p dc h", h=HG),
                        AP["L0_ffn_W_proj"][ts(sl, 512), grp * HG:(grp + 1) * HG]
                        .rearrange("(dc p) h -> p dc h", p=128),
                        deps=[getattr(cx, "pdep_w2", None)])
            qt = quantize_tile(cx, w2_bf[:], gscr, 4 * HG, [tl])
            for dc in range(4):
                d_abs = sl * 4 + dc
                for hs in range(HG // 128):
                    ttok.append(transpose128(
                        cx, w2qT[:, hs * E + d_abs * 128: hs * E + (d_abs + 1) * 128],
                        w2_bf[:, dc * HG + hs * 128: dc * HG + (hs + 1) * 128], [qt]))
            cx.pdep_w2 = ttok[-1]
        for b in range(B):
            for tc in range(NQC):
                col = b * NQC + tc
                t_rd = em.dma("sync", hn_rd[:].rearrange("p (hs t) -> p hs t", t=128),
                              u_dram[grp * HG:(grp + 1) * HG, ts(col)]
                              .rearrange("(hs p) t -> p hs t", p=128),
                              deps=[getattr(cx, "pdep_hn", None)])
                for half in range(2):
                    mmlast = None
                    for hs in range(HG // 128):
                        for pc in range(2):
                            mmlast = em.op("pe", lambda t, hs=hs, pc=pc, half=half: t.matmul(
                                ps_w[:, ts(pc, 512)], hn_rd[:, ts(hs)],
                                w2qT[:, hs * E + half * 1024 + pc * 512:
                                     hs * E + half * 1024 + (pc + 1) * 512],
                                start=(hs == 0), stop=(hs == HG // 128 - 1)),
                                deps=(ttok + [t_rd, cx.pdep(ps_w), t_rsd] + t_acc)
                                if (hs == 0 and pc == 0) else ())
                    cx.pdep_hn = mmlast
                    t_sc = em.op("dve", lambda v, col=col: v.tensor_scalar(
                        tmp_o[:], ps_w[:], rstd[:, col:col + 1], None, ALU.mult),
                        deps=[mmlast])
                    cx.puse(ps_w, t_sc)
                    em.op("pool", lambda g, col=col, half=half: g.tensor_tensor(
                        acc[col][:, half * 1024:(half + 1) * 1024],
                        acc[col][:, half * 1024:(half + 1) * 1024],
                        tmp_o[:], ALU.add), deps=[t_sc])
    cx.free(w2_bf, w2qT, hn_rd, tmp_o, gscr, rstd, msacc,
            cx.qz_qa, cx.qz_ratio, cx.qz_meanb)
    em.barrier()


def make_psum_pool(cx):
    cx.pp = {
        "a": cx.ps([128, 512], F32, "pp_a"),
        "c": cx.ps([128, 512], F32, "pp_c"),
        "row": cx.ps([128, 512], F32, "pp_row"),
        "o": cx.ps([128, 512], F32, "pp_o"),
        "tr0": cx.ps([128, 1024], BF16, "pp_tr0"),
        "tr1": cx.ps([128, 1024], BF16, "pp_tr1"),
        "w": cx.ps([128, 1024], F32, "pp_w"),
    }
    cx.tr_rot = 0


def build_launch_A():
    nc = bass.Bass("TRN2", target_bir_lowering=False)
    AP = declare_l0_inputs(nc)
    out_x1 = nc.dram_tensor("x1", [TOK, E], F32, kind="ExternalOutput").ap()
    kind = "ExternalOutput" if DEBUG else "Internal"
    u_dram = nc.dram_tensor("u_scratch", [FFN_H, TOK], BF16, kind=kind).ap()
    acc_dram = nc.dram_tensor("acc_scratch", [TOK, E], F32, kind=kind).ap()
    if DEBUG:
        nc._dbg_xn = nc.dram_tensor("d_xn", [E, TOK], BF16, kind="ExternalOutput").ap()
        nc._dbg_xnw = nc.dram_tensor("d_xnw", [256, WROWS], BF16, kind="ExternalOutput").ap()
        nc._dbg_k = nc.dram_tensor("d_k", [128, WROWS], BF16, kind="ExternalOutput").ap()
        nc._dbg_q = nc.dram_tensor("d_q", [128, OWN], BF16, kind="ExternalOutput").ap()
        nc._dbg_at = nc.dram_tensor("d_at", [128, OWN], BF16, kind="ExternalOutput").ap()
    em = Em(nc)
    cx = Ctx(nc, em)
    emit_consts(cx, AP)
    make_psum_pool(cx)
    xn_own = [cx.sb([128, TOK], BF16, f"xn_own{e}") for e in range(EC)]
    emit_l0(cx, AP, xn_own, u_dram, acc_dram)
    acc = [cx.sb([128, E], F32, f"accf{t}") for t in range(B * NQC)]
    emit_l0_ffn(cx, AP, xn_own, u_dram, acc_dram, acc)
    xtile = cx.sb([128, E], F32, "xres")
    st_toks = []
    for b in range(B):
        for tc in range(NQC):
            i = b * NQC + tc
            tl = em.dma("sync", xtile[:], AP["xw"][b, WIN + tc * 128: WIN + (tc + 1) * 128, :],
                        deps=[st_toks[-1] if st_toks else None])
            ta = em.op("dve", lambda v, i=i: v.tensor_tensor(
                acc[i][:], xtile[:], acc[i][:], ALU.add), deps=[tl])
            st_toks.append(em.dma("sync", out_x1[ts(i), :], acc[i][:], deps=[ta]))
    em.ops["sync"].append(("w", "sync", em.cnt["sync"]))
    em.replay()
    return nc


def declare_B_inputs(nc):
    ap = {}
    def inp(n, sh, dt=F32):
        ap[n] = nc.dram_tensor(n, sh, dt, kind="ExternalInput").ap()
    inp("x1in", [TOK, E])
    inp("c_ident", [128, 128], BF16)
    inp("c_prot", [128, 128])
    inp("L1_attn_Wq", [E, E]); inp("L1_attn_Wk", [E, E]); inp("L1_attn_Wv", [E, E])
    inp("L1_ffn_gate_kernel", [E, 8])
    return ap


def emit_l1pre(cx, AP, outs):
    em = cx.em
    NT = TOK // 128

    xn1f = [cx.sb([128, TOK], BF16, f"xn1f{e}") for e in range(EC)]
    lof = [cx.sb([128, TOK], BF16, f"lof{e}") for e in range(EC)]
    xt = cx.sb([128, E], F32, "b_x")
    xn32 = cx.sb([128, E], F32, "b_xn32")
    scr = cx.sb([128, E], F32, "b_scr")
    hibf = cx.sb([128, E], BF16, "b_hi")
    lobf = cx.sb([128, E], BF16, "b_lo")
    stats = cx.sb([128, 8], F32, "b_stat")
    g_hi = cx.sb([128, EC * 8], BF16, "g_hi")
    g_lo = cx.sb([128, EC * 8], BF16, "g_lo")
    g_32 = cx.sb([128, EC * 8], F32, "g_32")
    glog = cx.sb([128, 8], F32, "b_glog")
    gsc = cx.sb([128, 8], F32, "b_gsc")
    gse = cx.sb([128, 8], F32, "b_gse")

    t_g = em.dma("sync", g_32[:].rearrange("p (ec g) -> p ec g", g=8),
                 AP["L1_ffn_gate_kernel"].rearrange("(ec p) g -> p ec g", p=128))
    tg1 = em.op("dve", lambda v: v.tensor_copy(g_hi[:], g_32[:]), deps=[t_g])
    tg2 = em.op("dve", lambda v: v.tensor_tensor(g_32[:], g_32[:], g_hi[:], ALU.subtract),
                deps=[tg1])
    tg3 = em.op("dve", lambda v: v.tensor_copy(g_lo[:], g_32[:]), deps=[tg2])

    ps_g = cx.pp["a"]
    prev = None
    for t in range(NT):
        tl = em.dma("sync", xt[:], AP["x1in"][ts(t), :], deps=[prev])
        ln = layernorm_tile(cx, xt, xn32, scr, stats, [tl])
        c1 = em.op("dve", lambda v: v.tensor_copy(hibf[:], xn32[:]), deps=[ln])
        c2 = em.op("dve", lambda v: v.tensor_tensor(scr[:], xn32[:], hibf[:], ALU.subtract),
                   deps=[c1])
        c3 = em.op("dve", lambda v: v.tensor_copy(lobf[:], scr[:]), deps=[c2])
        last = None
        for e in range(EC):
            transpose128(cx, xn1f[e][:, ts(t)], hibf[:, ts(e)], [c1])
            last = transpose128(cx, lof[e][:, ts(t)], lobf[:, ts(e)], [c3])
        prev = last
        mml = None
        combos = [(e, src_t, gw) for e in range(EC)
                  for (src_t, gw) in ((xn1f, g_hi), (xn1f, g_lo), (lof, g_hi))]
        for idx, (e, src_t, gw) in enumerate(combos):
            mml = em.op("pe", lambda tt_, e=e, t=t, src_t=src_t, gw=gw, idx=idx: tt_.matmul(
                ps_g[:, 0:8], src_t[e][:, ts(t)], gw[:, e * 8:(e + 1) * 8],
                start=(idx == 0), stop=(idx == len(combos) - 1)),
                deps=[last, tg3, cx.pdep(ps_g)] if idx == 0 else ())
        tge = em.op("dve", lambda v: v.tensor_copy(glog[:], ps_g[:, 0:8]), deps=[mml])
        cx.puse(ps_g, tge)
        m1 = stats[:, 0:1]; m2 = stats[:, 1:2]; w1 = stats[:, 2:3]
        w2 = stats[:, 3:4]; dcol = stats[:, 4:5]
        g1 = em.op("dve", lambda v: v.tensor_reduce(m1, glog[:], AX.X, ALU.max), deps=[tge])
        g2 = em.op("dve", lambda v: v.tensor_scalar(gsc[:], glog[:], m1, None, ALU.is_ge),
                   deps=[g1])
        g3 = em.op("dve", lambda v: v.tensor_scalar(gse[:], gsc[:], -1e30, None, ALU.mult),
                   deps=[g2])
        g4 = em.op("dve", lambda v: v.tensor_tensor(gse[:], glog[:], gse[:], ALU.add),
                   deps=[g3])
        g5 = em.op("dve", lambda v: v.tensor_reduce(m2, gse[:], AX.X, ALU.max), deps=[g4])
        g6 = em.op("dve", lambda v: v.tensor_tensor(dcol, m1, m2, ALU.subtract), deps=[g5])
        g7 = em.op("act", lambda a: a.activation(w1, dcol, AF.Sigmoid), deps=[g6])
        g8 = em.op("dve", lambda v: v.tensor_scalar(w2, w1, -1.0, 1.0, ALU.mult, ALU.add),
                   deps=[g7])
        g9 = em.op("dve", lambda v: v.tensor_scalar(gsc[:], gsc[:], w1, None, ALU.mult),
                   deps=[g8])
        gA = em.op("dve", lambda v: v.tensor_scalar(gse[:], gse[:], m2, None, ALU.is_ge),
                   deps=[g9])
        gB = em.op("dve", lambda v: v.tensor_scalar(gse[:], gse[:], w2, None, ALU.mult),
                   deps=[gA])
        gC = em.op("dve", lambda v: v.tensor_tensor(gse[:], gse[:], gsc[:], ALU.add),
                   deps=[gB])
        em.dma("pool", outs["gsel"][ts(t), :], gse[:], deps=[gC])
    cx.free(g_hi, g_lo, g_32, glog, gsc, gse, xt, xn32, scr, hibf, lobf, stats)
    cx.free(*lof)

    wrow = cx.sb([128, E], BF16, "b_wrow")
    wTblk = cx.sb([128, EC * 128], BF16, "b_wT")
    vstage = cx.sb([128, TOK], BF16, "b_vstage")
    v1t = [cx.sb([128, E], BF16, f"b_v1t{t}") for t in range(TOK // 128)]
    ps_w = cx.pp["w"]
    t_prev_w = None
    for wname, oname, scale in (("L1_attn_Wq", "q1", 0.125),
                                ("L1_attn_Wk", "k1", 1.0),
                                ("L1_attn_Wv", None, 1.0)):
        for ic in range(EC):
            tl = em.dma("pool", wrow[:], AP[wname][ts(ic), :],
                        deps=[t_prev_w, em.token("pe")])
            ttok = []
            for e in range(EC):
                ttok.append(transpose128(cx, wTblk[:, ts(e)], wrow[:, ts(e)], [tl]))
            t_prev_w = ttok[-1]
            mml = None
            for e in range(EC):
                for pc in range(2):
                    mml = em.op("pe", lambda t_, e=e, pc=pc: t_.matmul(
                        ps_w[:, ts(pc, 512)], wTblk[:, ts(e)],
                        xn1f[e][:, ts(pc, 512)], start=(e == 0), stop=(e == EC - 1)),
                        deps=(ttok + [cx.pdep(ps_w)]) if (e == 0 and pc == 0) else ())
            tev = em.op("act", lambda a, scale=scale: a.activation(
                vstage[:], ps_w[:], AF.Copy, scale=scale),
                deps=[mml, getattr(cx, "t_spill_prev", None)])
            cx.puse(ps_w, tev)
            if oname is not None:
                cx.t_spill_prev = em.dma("pool", outs[oname][ts(ic), :], vstage[:], deps=[tev])
                t_prev_w = tev
            else:
                vt = []
                for t in range(TOK // 128):
                    vt.append(transpose128(cx, v1t[t][:, ts(ic)], vstage[:, ts(t)], [tev]))
                t_prev_w = vt[-1]
    for t in range(TOK // 128):
        em.dma("pool", outs["v1"][ts(t), :], v1t[t][:], deps=[em.token("dve")])
    for e in range(EC):
        em.dma("pool", outs["xn1f"][ts(e), :], xn1f[e][:], deps=[em.token("dve")])
    em.barrier()


def build_launch_B():
    nc = bass.Bass("TRN2", target_bir_lowering=False)
    AP = declare_B_inputs(nc)
    outs = {
        "gsel": nc.dram_tensor("gsel", [TOK, 8], F32, kind="ExternalOutput").ap(),
        "q1": nc.dram_tensor("q1", [E, TOK], BF16, kind="ExternalOutput").ap(),
        "k1": nc.dram_tensor("k1", [E, TOK], BF16, kind="ExternalOutput").ap(),
        "v1": nc.dram_tensor("v1", [TOK, E], BF16, kind="ExternalOutput").ap(),
        "xn1f": nc.dram_tensor("xn1f", [E, TOK], BF16, kind="ExternalOutput").ap(),
    }
    em = Em(nc)
    cx = Ctx(nc, em)
    emit_consts(cx, AP)
    make_psum_pool(cx)
    emit_l1pre(cx, AP, outs)
    em.ops["sync"].append(("w", "pool", em.cnt["pool"]))
    em.ops["sync"].append(("w", "sync", em.cnt["sync"]))
    em.replay()
    return nc


def declare_C_inputs(nc):
    ap = {}
    def inp(n, sh, dt=F32):
        ap[n] = nc.dram_tensor(n, sh, dt, kind="ExternalInput").ap()
    inp("x1in", [TOK, E])
    inp("c_ident", [128, 128], BF16)
    inp("c_prot", [128, 128])
    inp("xn1f", [E, TOK], BF16)
    inp("gselT", [8, TOK])
    inp("q1", [E, TOK], BF16)
    inp("k_all", [E, B * S], BF16)
    inp("v_all", [B * S, E], BF16)
    inp("L1_attn_Wo", [E, E])
    inp("lam_q1", [64, 1]); inp("lam_k1", [64, 1])
    inp("lam_q2", [64, 1]); inp("lam_k2", [64, 1])
    inp("subln", [128, 1])
    inp("L1_ffn_wi_0", [MOE_E, E, E]); inp("L1_ffn_wi_1", [MOE_E, E, E])
    inp("L1_ffn_wo", [MOE_E, E, E])
    return ap


def emit_lambda(cx, AP):
    em = cx.em
    cx.neglam_col = cx.sb([128, 1], F32, "neglam")
    cx.slw_col = cx.sb([128, 1], F32, "slwcol")
    lv = cx.sb([64, 4], F32, "lamv")
    l11 = cx.sb([1, 4], F32, "lam_sc")
    subc = cx.sb([128, 1], F32, "subc")
    tloads = [em.dma("sync", lv[:, i:i + 1], AP[n])
              for i, n in enumerate(("lam_q1", "lam_k1", "lam_q2", "lam_k2"))]
    t_su = em.dma("sync", subc[:], AP["subln"])
    pr = cx.pp["row"]
    t1 = em.op("pe", lambda t: t.matmul(pr[0:1, 0:1], lv[:, 0:1], lv[:, 1:2],
                                        start=True, stop=True),
               deps=tloads + [cx.pdep(pr)])
    e1 = em.op("act", lambda a: a.activation(l11[:, 0:1], pr[0:1, 0:1], AF.Exp), deps=[t1])
    cx.puse(pr, e1)
    t2 = em.op("pe", lambda t: t.matmul(pr[0:1, 0:1], lv[:, 2:3], lv[:, 3:4],
                                        start=True, stop=True),
               deps=[e1, cx.pdep(pr)])
    e2 = em.op("act", lambda a: a.activation(l11[:, 1:2], pr[0:1, 0:1], AF.Exp), deps=[t2])
    cx.puse(pr, e2)
    d1 = em.op("dve", lambda v: v.tensor_tensor(l11[:, 2:3], l11[:, 1:2], l11[:, 0:1],
                                                ALU.subtract), deps=[e1, e2])
    d2 = em.op("dve", lambda v: v.tensor_scalar(l11[:, 2:3], l11[:, 2:3],
                                                LAMBDA_INIT, None, ALU.subtract),
               deps=[d1])
    t3 = em.op("pe", lambda t: t.matmul(cx.pp["c"][:, 0:1], cx.ones_rf[:], l11[:, 2:3],
                                        start=True, stop=True),
               deps=[d2, cx.t_ones, cx.pdep(cx.pp["c"])])
    d3 = em.op("dve", lambda v: v.tensor_copy(cx.neglam_col[:], cx.pp["c"][:, 0:1]),
               deps=[t3])
    cx.puse(cx.pp["c"], d3)
    cx.t_lam = d3
    cx.t_slw = em.op("dve", lambda v: v.tensor_scalar(
        cx.slw_col[:], subc[:], 1.0 - LAMBDA_INIT, None, ALU.mult), deps=[t_su])
    cx.free(lv, l11, subc)


MOE_PARANOID = False


def emit_moe(cx, AP, x2acc, xn1f_src, t_xn, grow_fn=None, t_w0=None):
    em = cx.em
    NT = TOK // 128
    grow = cx.sb([1, TOK], F32, "c_grow")
    gbc = cx.sb([128, TOK], F32, "c_gbc")
    wiblk = [cx.sb([128, EC * 128], BF16, f"c_wi{i}") for i in range(2)]
    silu_sb = cx.sb([128, 512], BF16, "c_silu")
    hmul = cx.sb([128, TOK], BF16, "c_hmul")
    h_fm = [cx.sb([128, TOK], BF16, f"c_hfm{m}") for m in range(EC)]
    woblk = [cx.sb([128, 512], BF16, f"c_wo{m}") for m in range(EC)]
    pA, pC = cx.pp["a"], cx.pp["c"]
    pO, pR = cx.pp["o"], cx.pp["row"]
    ps_w = cx.pp["w"]
    t_wi_prev = [t_w0, t_w0]
    t_wo_prev = [t_w0] * EC
    for ex in range(MOE_E):
        if grow_fn is None:
            tg = em.dma("sync", grow[:], AP["gselT"][ex:ex + 1, :],
                        deps=[getattr(cx, "t_gbc", None)])
        else:
            tg = grow_fn(ex, grow, [getattr(cx, "t_gbc", None)])
        btoks = []
        for pc in range(2):
            tb = em.op("pe", lambda t, pc=pc: t.matmul(
                pA[:], cx.ones_rf[:], grow[:, ts(pc, 512)],
                start=True, stop=True), deps=[tg, cx.t_ones, cx.pdep(pA)])
            tc = em.op("dve", lambda v, pc=pc: v.tensor_copy(
                gbc[:, ts(pc, 512)], pA[:]), deps=[tb])
            cx.puse(pA, tc)
            btoks.append(tc)
        cx.t_gbc = btoks[-1]
        for m in range(EC):
            t0 = em.dma("pool", wiblk[0][:].rearrange("p (dc m) -> p dc m", m=128),
                        AP["L1_ffn_wi_0"][ex, :, ts(m)].rearrange("(dc p) m -> p dc m", p=128),
                        deps=[t_wi_prev[0]])
            t1 = em.dma("pool", wiblk[1][:].rearrange("p (dc m) -> p dc m", m=128),
                        AP["L1_ffn_wi_1"][ex, :, ts(m)].rearrange("(dc p) m -> p dc m", p=128),
                        deps=[t_wi_prev[1]])
            mml0 = [None, None]
            mml1 = [None, None]
            for pc in range(2):
                for e in range(EC):
                    mml0[pc] = em.op("pe", lambda t, e=e, pc=pc: t.matmul(
                        [pA, pC][pc][:], wiblk[0][:, ts(e)],
                        xn1f_src[e][:, ts(pc, 512)], start=(e == 0), stop=(e == EC - 1)),
                        deps=([t0, t1, t_xn, cx.pdep([pA, pC][pc])] + btoks) if e == 0 else ())
                for e in range(EC):
                    mml1[pc] = em.op("pe", lambda t, e=e, pc=pc: t.matmul(
                        [pO, pR][pc][:], wiblk[1][:, ts(e)],
                        xn1f_src[e][:, ts(pc, 512)], start=(e == 0), stop=(e == EC - 1)),
                        deps=[t1, cx.pdep([pO, pR][pc])] if e == 0 else ())
            t_wi_prev = [mml0[1], mml1[1]]
            if MOE_PARANOID:
                em.barrier()
            s2 = None
            for pc in range(2):
                s1 = em.op("act", lambda a, pc=pc: a.activation(
                    silu_sb[:], [pA, pC][pc][:], AF.Silu), deps=[mml0[pc], s2])
                cx.puse([pA, pC][pc], s1)
                s2 = em.op("dve", lambda v, pc=pc: v.tensor_tensor(
                    hmul[:, ts(pc, 512)], silu_sb[:], [pO, pR][pc][:], ALU.mult),
                    deps=[s1, mml1[pc]])
                cx.puse([pO, pR][pc], s2)
                em.op("dve", lambda v, m=m, pc=pc: v.tensor_tensor(
                    h_fm[m][:, ts(pc, 512)], hmul[:, ts(pc, 512)], gbc[:, ts(pc, 512)],
                    ALU.mult), deps=[s2, getattr(cx, "t_moe_out_prev", None)] + btoks)
        if MOE_PARANOID:
            em.barrier()
        t_h = em.token("dve")
        for quar in range(4):
            wo_toks = []
            for m in range(EC):
                wo_toks.append(em.dma("pool", woblk[m][:],
                                      AP["L1_ffn_wo"][ex, ts(m), quar * 512:(quar + 1) * 512],
                                      deps=[t_wo_prev[m]]))
            for tc in range(NT):
                mml = None
                for m in range(EC):
                    mml = em.op("pe", lambda t, m=m, tc=tc: t.matmul(
                        ps_w[:, 0:512], h_fm[m][:, ts(tc)],
                        woblk[m][:], start=(m == 0), stop=(m == EC - 1)),
                        deps=(wo_toks + [t_h, cx.pdep(ps_w)]) if m == 0 else ())
                tadd = em.op("dve", lambda v, tc=tc, quar=quar: v.tensor_tensor(
                    x2acc[tc][:, quar * 512:(quar + 1) * 512],
                    x2acc[tc][:, quar * 512:(quar + 1) * 512],
                    ps_w[:, 0:512], ALU.add), deps=[mml])
                cx.puse(ps_w, tadd)
            for m in range(EC):
                t_wo_prev[m] = em.token("pe")
        cx.t_moe_out_prev = em.token("pe")
        if MOE_PARANOID:
            em.barrier()
        if hasattr(cx, "dbg_hfm") and ex == MOE_E - 1:
            for m in range(EC):
                em.dma("pool", cx.dbg_hfm[ts(m), :], h_fm[m][:], deps=[em.token("dve")])
            em.dma("pool", cx.dbg_gbc, gbc[:], deps=[em.token("dve")])
            em.barrier()
    cx.free(grow, gbc, wiblk[0], wiblk[1], silu_sb, hmul, *(h_fm + woblk))
    em.barrier()


def emit_l1attn(cx, x2acc, q1_src, k_src, v_src, t_qkv):
    em = cx.em
    attnf = [cx.sb([128, TOK], BF16, f"c_attn{h}") for h in range(L1_H)]
    khb = cx.sb([128, S], BF16, "c_khb")
    vhb = cx.sb([128, S], BF16, "c_vhb")
    ksq = cx.sb([128, S], BF16, "c_ksq")
    srow = cx.sb([1, 512], F32, "c_srow")
    bnd = cx.sb([1, 8], F32, "c_bnd")
    bndc = cx.sb([128, 1], F32, "c_bndc")
    pT = cx.sb([128, 512], BF16, "c_pT")
    rl = cx.sb([1, 512], F32, "c_rl")
    of = cx.sb([128, 512], F32, "c_of")
    o1n = cx.sb([128, 512], F32, "c_o1n")
    o2n = cx.sb([128, 512], F32, "c_o2n")
    apre = cx.sb([128, 512], F32, "c_apre")
    asq = cx.sb([128, 512], BF16, "c_asq")
    pA, pC, pO, pR = cx.pp["a"], cx.pp["c"], cx.pp["o"], cx.pp["row"]
    NKC1 = S // 128
    t_kv_prev = None
    for h in range(L1_H):
        for b in range(B):
            tk = k_src(h, b, khb, [t_kv_prev])
            tv = v_src(h, b, vhb, [t_kv_prev])
            tq2 = em.op("act", lambda a, h=h, b=b: a.activation(
                ksq[:, 0:512], q1_src[h][:, b * OWN:(b + 1) * OWN], AF.Square),
                deps=[t_qkv, t_kv_prev])
            qb_toks = []
            for half in range(2):
                hp = slice(half * 64, (half + 1) * 64)
                tm1 = em.op("pe", lambda t, hp=hp: t.matmul(
                    pR[0:1, 0:512], cx.ones_cb[hp, :], ksq[hp, 0:512],
                    start=True, stop=True), deps=[tq2, cx.t_ones, cx.pdep(pR)])
                tr1 = em.op("dve", lambda v, half=half: v.tensor_reduce(
                    bnd[:, half:half + 1], pR[0:1, 0:512], AX.X, ALU.max), deps=[tm1])
                cx.puse(pR, tr1)
                qb_toks.append(tr1)
            tk2 = em.op("act", lambda a: a.activation(ksq[:], khb[:], AF.Square),
                        deps=[tk] + qb_toks)
            kb_toks = []
            for half in range(2):
                hp = slice(half * 64, (half + 1) * 64)
                mmk = None
                for pc in range(8):
                    mmk = em.op("pe", lambda t, hp=hp, pc=pc: t.matmul(
                        pR[0:1, 0:512], cx.ones_cb[hp, :], ksq[hp, ts(pc, 512)],
                        start=(pc == 0), stop=(pc == 7)),
                        deps=[tk2, cx.t_ones, cx.pdep(pR)] if pc == 0 else ())
                trk = em.op("dve", lambda v, half=half: v.tensor_reduce(
                    bnd[:, 2 + half:3 + half], pR[0:1, 0:512], AX.X, ALU.max), deps=[mmk])
                cx.puse(pR, trk)
                kb_toks.append(trk)
            sq1 = em.op("act", lambda a: a.activation(bnd[:, 4:6], bnd[:, 0:2], AF.Sqrt),
                        deps=qb_toks + kb_toks)
            sq2 = em.op("act", lambda a: a.activation(bnd[:, 6:8], bnd[:, 2:4], AF.Sqrt),
                        deps=[sq1])
            bb1 = em.op("dve", lambda v: v.tensor_tensor(bnd[:, 4:6], bnd[:, 4:6],
                                                         bnd[:, 6:8], ALU.mult), deps=[sq2])
            bb2 = em.op("dve", lambda v: v.tensor_scalar(bnd[:, 4:6], bnd[:, 4:6],
                                                         -1.02, 70.0, ALU.mult, ALU.add),
                        deps=[bb1])
            if DEBUG and hasattr(cx, "dbg_bnd"):
                em.dma("pool", cx.dbg_bnd[h * B + b: h * B + b + 1, :], bnd[:], deps=[bb2])
            ton = None
            for half in range(2):
                hp = slice(half * 64, (half + 1) * 64)
                tb = em.op("pe", lambda t, half=half: t.matmul(
                    pC[:, 0:1], cx.ones_rf[:], bnd[0:1, 4 + half:5 + half],
                    start=True, stop=True), deps=[bb2, cx.t_ones, cx.pdep(pC)])
                tbc = em.op("dve", lambda v: v.tensor_copy(bndc[:], pC[:, 0:1]), deps=[tb])
                cx.puse(pC, tbc)
                tl = tvm = None
                texp_prev = None
                for kc in range(NKC1):
                    tmm = em.op("pe", lambda t, kc=kc, hp=hp, h=h, b=b: t.matmul(
                        pA[:], khb[hp, ts(kc)], q1_src[h][hp, b * OWN:(b + 1) * OWN],
                        start=True, stop=True), deps=[tk, t_qkv, cx.pdep(pA)])
                    tex = em.op("act", lambda a: a.activation(
                        pT[:], pA[:], AF.Exp, bias=bndc[:]), deps=[tmm, tbc, texp_prev])
                    cx.puse(pA, tex)
                    tl = em.op("pe", lambda t, kc=kc: t.matmul(
                        pR[0:1, 0:512], cx.ones_cb[:], pT[:],
                        start=(kc == 0), stop=(kc == NKC1 - 1)),
                        deps=[tex] + ([cx.pdep(pR)] if kc == 0 else []))
                    tvm = em.op("pe", lambda t, kc=kc: t.matmul(
                        pO[:], vhb[:, ts(kc)], pT[:],
                        start=(kc == 0), stop=(kc == NKC1 - 1)),
                        deps=[tv] + ([cx.pdep(pO)] if kc == 0 else []))
                    texp_prev = tvm
                trc = em.op("dve", lambda v: v.reciprocal(rl[:], pR[0:1, 0:512]),
                            deps=[tl, tvm])
                cx.puse(pR, trc)
                if DEBUG and hasattr(cx, "dbg_rl"):
                    em.dma("pool", cx.dbg_rl[(h * B + b) * 2 + half: (h * B + b) * 2 + half + 1, :],
                           rl[:], deps=[trc])
                tb2 = em.op("pe", lambda t: t.matmul(pC[:], cx.ones_rf[:], rl[:],
                                                     start=True, stop=True),
                            deps=[trc, cx.pdep(pC)])
                tof = em.op("dve", lambda v: v.tensor_copy(of[:], pO[:]), deps=[tvm, tb2])
                cx.puse(pO, tof)
                dst = o1n if half == 0 else o2n
                ton = em.op("dve", lambda v, dst=dst: v.tensor_tensor(
                    dst[:], of[:], pC[:], ALU.mult), deps=[tof])
                cx.puse(pC, ton)
            a1 = em.op("dve", lambda v: v.tensor_scalar(
                apre[:], o2n[:], cx.neglam_col[:], None, ALU.mult),
                deps=[ton, cx.t_lam])
            a2 = em.op("dve", lambda v: v.tensor_tensor(apre[:], apre[:], o1n[:], ALU.add),
                       deps=[a1])
            a3 = em.op("act", lambda a: a.activation(asq[:], apre[:], AF.Square), deps=[a2])
            a4 = em.op("pe", lambda t: t.matmul(pR[0:1, 0:512], cx.ones_cb[:], asq[:],
                                                start=True, stop=True),
                       deps=[a3, cx.pdep(pR)])
            a5 = em.op("act", lambda a: a.activation(srow[:], pR[0:1, 0:512], AF.Sqrt,
                                                     scale=1.0 / 128.0, bias=cx.bias_ln[0:1, :]),
                       deps=[a4, cx.t_bias])
            cx.puse(pR, a5)
            a6 = em.op("dve", lambda v: v.reciprocal(srow[:], srow[:]), deps=[a5])
            a7 = em.op("pe", lambda t: t.matmul(pC[:], cx.ones_rf[:], srow[:],
                                                start=True, stop=True),
                       deps=[a6, cx.pdep(pC)])
            a8 = em.op("dve", lambda v: v.tensor_tensor(apre[:], apre[:], pC[:], ALU.mult),
                       deps=[a7])
            cx.puse(pC, a8)
            a9 = em.op("act", lambda a, h=h, b=b: a.activation(
                attnf[h][:, b * OWN:(b + 1) * OWN], apre[:], AF.Copy, scale=cx.slw_col[:]),
                deps=[a8, cx.t_slw])
            t_kv_prev = a9
    cx.free(khb, vhb, ksq, srow, bnd, bndc, pT, rl, of, o1n, o2n, apre, asq)
    return attnf


def emit_l1wo(cx, AP, x2acc, attnf):
    em = cx.em
    NT = TOK // 128
    woL = cx.sb([128, 512], BF16, "c_woL")
    woT = cx.sb([128, 4 * E], BF16, "c_woT")
    ps_w = cx.pp["w"]
    t_attn = em.token("act")
    t_prev = None
    for g in range(4):
        ttok = []
        for jr in range(EC):
            tl = em.dma("pool", woL[:], AP["L1_attn_Wo"][ts(jr), ts(g, 512)],
                        deps=[t_prev])
            for isub in range(4):
                ttok.append(transpose128(
                    cx, woT[:, isub * E + jr * 128: isub * E + (jr + 1) * 128],
                    woL[:, ts(isub)], [tl]))
            t_prev = ttok[-1]
        for tc in range(NT):
            for half in range(2):
                mml = None
                for isub in range(4):
                    ic = g * 4 + isub
                    for pc in range(2):
                        mml = em.op("pe", lambda t, ic=ic, isub=isub, tc=tc, pc=pc, half=half: t.matmul(
                            ps_w[:, ts(pc, 512)], attnf[ic][:, ts(tc)],
                            woT[:, isub * E + half * 1024 + pc * 512:
                                isub * E + half * 1024 + (pc + 1) * 512],
                            start=(isub == 0), stop=(isub == 3)),
                            deps=(ttok + [t_attn, cx.pdep(ps_w)]) if (isub == 0 and pc == 0) else ())
                tadd = em.op("dve", lambda v, tc=tc, half=half: v.tensor_tensor(
                    x2acc[tc][:, half * 1024:(half + 1) * 1024],
                    x2acc[tc][:, half * 1024:(half + 1) * 1024],
                    ps_w[:], ALU.add), deps=[mml])
                cx.puse(ps_w, tadd)
    cx.free(woL, woT)


def build_launch_C():
    nc = bass.Bass("TRN2", target_bir_lowering=False)
    AP = declare_C_inputs(nc)
    out = nc.dram_tensor("x2", [TOK, E], F32, kind="ExternalOutput").ap()
    em = Em(nc)
    cx = Ctx(nc, em)
    emit_consts(cx, AP)
    make_psum_pool(cx)
    NT = TOK // 128
    x2acc = [cx.sb([128, E], F32, f"x2acc{t}") for t in range(NT)]
    for t in range(NT):
        em.dma("sync", x2acc[t][:], AP["x1in"][ts(t), :])
    emit_lambda(cx, AP)
    xn1f = [cx.sb([128, TOK], BF16, f"c_xn1f{e}") for e in range(EC)]
    for e in range(EC):
        em.dma("sync", xn1f[e][:], AP["xn1f"][ts(e), :])
    t_xn = em.token("sync")
    emit_moe(cx, AP, x2acc, xn1f, t_xn)
    if DEBUG:
        dm = nc.dram_tensor("d_moe", [TOK, E], F32, kind="ExternalOutput").ap()
        for t in range(NT):
            em.dma("pool", dm[ts(t), :], x2acc[t][:], deps=[em.token("dve")])
        em.barrier()
    cx.free(*xn1f)
    q1f = [cx.sb([128, TOK], BF16, f"c_q1f{h}") for h in range(L1_H)]
    for h in range(L1_H):
        em.dma("sync", q1f[h][:], AP["q1"][ts(h), :])
    t_q = em.token("sync")

    def k_src(h, b, dst, deps):
        return cx.em.dma("sync", dst[:], AP["k_all"][ts(h), b * S:(b + 1) * S],
                         deps=deps)

    def v_src(h, b, dst, deps):
        return cx.em.dma("sync", dst[:].rearrange("p (kc d) -> p kc d", d=128),
                         AP["v_all"][b * S:(b + 1) * S, ts(h)]
                         .rearrange("(kc p) d -> p kc d", p=128), deps=deps)

    if DEBUG:
        cx.dbg_rl = nc.dram_tensor("d_rl", [64, 512], F32, kind="ExternalOutput").ap()
        cx.dbg_bnd = nc.dram_tensor("d_bnd", [64, 8], F32, kind="ExternalOutput").ap()
    attnf = emit_l1attn(cx, x2acc, q1f, k_src, v_src, t_q)
    if DEBUG:
        da = nc.dram_tensor("d_attnf", [E, TOK], BF16, kind="ExternalOutput").ap()
        for h in range(L1_H):
            em.dma("pool", da[ts(h), :], attnf[h][:], deps=[em.token("act")])
        em.barrier()
    emit_l1wo(cx, AP, x2acc, attnf)
    em.barrier()
    for t in range(NT):
        em.dma("sync", out[ts(t), :], x2acc[t][:], deps=[em.token("dve")])
    em.ops["sync"].append(("w", "sync", em.cnt["sync"]))
    em.replay()
    return nc



def declare_fused_inputs(nc):
    ap = declare_l0_inputs(nc)
    def inp(n, sh, dt=F32):
        ap[n] = nc.dram_tensor(n, sh, dt, kind="ExternalInput").ap()
    inp("L1_attn_Wq", [E, E]); inp("L1_attn_Wk", [E, E]); inp("L1_attn_Wv", [E, E])
    inp("L1_ffn_gate_kernel", [E, 8])
    inp("L1_attn_Wo", [E, E])
    inp("lam_q1", [64, 1]); inp("lam_k1", [64, 1])
    inp("lam_q2", [64, 1]); inp("lam_k2", [64, 1])
    inp("subln", [128, 1])
    inp("L1_ffn_wi_0", [MOE_E, E, E]); inp("L1_ffn_wi_1", [MOE_E, E, E])
    inp("L1_ffn_wo", [MOE_E, E, E])
    return ap


def emit_l1pre_fused(cx, AP, x1tiles, xn1f, gselT_sb, qin, kin, vin):
    em = cx.em
    NT = TOK // 128

    lof = [cx.sb([128, TOK], BF16, f"f_lof{e}") for e in range(EC)]
    xn32 = cx.sb([128, E], F32, "f_xn32")
    scr = cx.sb([128, E], F32, "f_scr")
    hibf = cx.sb([128, E], BF16, "f_hi")
    lobf = cx.sb([128, E], BF16, "f_lo")
    stats = cx.sb([128, 8], F32, "f_stat")
    g_hi = cx.sb([128, EC * 8], BF16, "f_ghi")
    g_lo = cx.sb([128, EC * 8], BF16, "f_glo")
    g_32 = cx.sb([128, EC * 8], F32, "f_g32")
    glog = cx.sb([128, 8], F32, "f_glog")
    gsc = cx.sb([128, 8], F32, "f_gsc")
    gse = cx.sb([128, 8], F32, "f_gse")
    ident32 = cx.sb([128, 128], F32, "f_id32")

    t_id32 = em.op("dve", lambda v: v.tensor_copy(ident32[:], cx.ident[:]),
                   deps=[cx.t_ident])
    t_g = em.dma("sync", g_32[:].rearrange("p (ec g) -> p ec g", g=8),
                 AP["L1_ffn_gate_kernel"].rearrange("(ec p) g -> p ec g", p=128))
    tg1 = em.op("dve", lambda v: v.tensor_copy(g_hi[:], g_32[:]), deps=[t_g])
    tg2 = em.op("dve", lambda v: v.tensor_tensor(g_32[:], g_32[:], g_hi[:], ALU.subtract),
                deps=[tg1])
    tg3 = em.op("dve", lambda v: v.tensor_copy(g_lo[:], g_32[:]), deps=[tg2])

    ps_g = cx.pp["a"]
    for t in range(NT):
        ln = layernorm_tile(cx, x1tiles[t], xn32, scr, stats, [cx.t_x1])
        c1 = em.op("dve", lambda v: v.tensor_copy(hibf[:], xn32[:]), deps=[ln])
        c2 = em.op("dve", lambda v: v.tensor_tensor(scr[:], xn32[:], hibf[:], ALU.subtract),
                   deps=[c1])
        c3 = em.op("dve", lambda v: v.tensor_copy(lobf[:], scr[:]), deps=[c2])
        last = None
        for e in range(EC):
            transpose128(cx, xn1f[e][:, ts(t)], hibf[:, ts(e)], [c1])
            last = transpose128(cx, lof[e][:, ts(t)], lobf[:, ts(e)], [c3])
        mml = None
        combos = [(e, src_t, gw) for e in range(EC)
                  for (src_t, gw) in ((xn1f, g_hi), (xn1f, g_lo), (lof, g_hi))]
        for idx, (e, src_t, gw) in enumerate(combos):
            mml = em.op("pe", lambda tt_, e=e, t=t, src_t=src_t, gw=gw, idx=idx: tt_.matmul(
                ps_g[:, 0:8], src_t[e][:, ts(t)], gw[:, e * 8:(e + 1) * 8],
                start=(idx == 0), stop=(idx == len(combos) - 1)),
                deps=[last, tg3, cx.pdep(ps_g)] if idx == 0 else ())
        tge = em.op("dve", lambda v: v.tensor_copy(glog[:], ps_g[:, 0:8]), deps=[mml])
        cx.puse(ps_g, tge)
        m1 = stats[:, 0:1]; m2 = stats[:, 1:2]; w1 = stats[:, 2:3]
        w2 = stats[:, 3:4]; dcol = stats[:, 4:5]
        g1 = em.op("dve", lambda v: v.tensor_reduce(m1, glog[:], AX.X, ALU.max), deps=[tge])
        g2 = em.op("dve", lambda v: v.tensor_scalar(gsc[:], glog[:], m1, None, ALU.is_ge),
                   deps=[g1])
        g3 = em.op("dve", lambda v: v.tensor_scalar(gse[:], gsc[:], -1e30, None, ALU.mult),
                   deps=[g2])
        g4 = em.op("dve", lambda v: v.tensor_tensor(gse[:], glog[:], gse[:], ALU.add),
                   deps=[g3])
        g5 = em.op("dve", lambda v: v.tensor_reduce(m2, gse[:], AX.X, ALU.max), deps=[g4])
        g6 = em.op("dve", lambda v: v.tensor_tensor(dcol, m1, m2, ALU.subtract), deps=[g5])
        g7 = em.op("act", lambda a: a.activation(w1, dcol, AF.Sigmoid), deps=[g6])
        g8 = em.op("dve", lambda v: v.tensor_scalar(w2, w1, -1.0, 1.0, ALU.mult, ALU.add),
                   deps=[g7])
        g9 = em.op("dve", lambda v: v.tensor_scalar(gsc[:], gsc[:], w1, None, ALU.mult),
                   deps=[g8])
        gA = em.op("dve", lambda v: v.tensor_scalar(gse[:], gse[:], m2, None, ALU.is_ge),
                   deps=[g9])
        gB = em.op("dve", lambda v: v.tensor_scalar(gse[:], gse[:], w2, None, ALU.mult),
                   deps=[gA])
        gC = em.op("dve", lambda v: v.tensor_tensor(gse[:], gse[:], gsc[:], ALU.add),
                   deps=[gB])
        tt = em.op("pe", lambda t_: t_.transpose(cx.pp["c"][0:8, 0:128], gse[:], ident32[:]),
                   deps=[gC, t_id32, cx.pdep(cx.pp["c"])])
        te = em.op("dve", lambda v, t=t: v.tensor_copy(gselT_sb[:, ts(t)],
                                                       cx.pp["c"][0:8, 0:128]), deps=[tt])
        cx.puse(cx.pp["c"], te)
    cx.free(*(lof + [xn32, scr, hibf, lobf, stats, g_hi, g_lo, g_32,
                     glog, gsc, gse, ident32]))

    wrow = cx.sb([128, E], BF16, "f_wrow")
    wTblk = cx.sb([128, EC * 128], BF16, "f_wT")
    vstage = cx.sb([128, TOK], BF16, "f_vstage")
    v1t = [cx.sb([128, E], BF16, f"f_v1t{t}") for t in range(NT)]
    ps_w = cx.pp["w"]
    t_prev_w = None
    kv_toks = []
    for wname, kind, scale in (("L1_attn_Wq", "q", 0.125),
                               ("L1_attn_Wk", "k", 1.0),
                               ("L1_attn_Wv", "v", 1.0)):
        dst_dram = {"q": qin, "k": kin, "v": None}[kind]
        for ic in range(EC):
            tl = em.dma("pool", wrow[:], AP[wname][ts(ic), :],
                        deps=[t_prev_w, em.token("pe")])
            ttok = []
            for e in range(EC):
                ttok.append(transpose128(cx, wTblk[:, ts(e)], wrow[:, ts(e)], [tl]))
            t_prev_w = ttok[-1]
            mml = None
            for e in range(EC):
                for pc in range(2):
                    mml = em.op("pe", lambda t_, e=e, pc=pc: t_.matmul(
                        ps_w[:, ts(pc, 512)], wTblk[:, ts(e)],
                        xn1f[e][:, ts(pc, 512)], start=(e == 0), stop=(e == EC - 1)),
                        deps=(ttok + [cx.pdep(ps_w)]) if (e == 0 and pc == 0) else ())
            tev = em.op("act", lambda a, scale=scale: a.activation(
                vstage[:], ps_w[:], AF.Copy, scale=scale),
                deps=[mml, getattr(cx, "t_spill_prev", None)])
            cx.puse(ps_w, tev)
            if kind in ("q", "k"):
                tok = em.dma("pool", dst_dram[ts(ic), :], vstage[:], deps=[tev])
                cx.t_spill_prev = tok
                if kind == "k":
                    kv_toks.append(tok)
                t_prev_w = tok
            else:
                vt = []
                for t in range(NT):
                    vt.append(transpose128(cx, v1t[t][:, ts(ic)], vstage[:, ts(t)], [tev]))
                t_prev_w = vt[-1]
    for t in range(NT):
        kv_toks.append(em.dma("pool", vin[ts(t), :], v1t[t][:], deps=[em.token("dve")]))
    cx.free(wrow, wTblk, vstage, *v1t)
    return kv_toks


def build_fused():
    nc = bass.Bass("TRN2", target_bir_lowering=False)
    AP = declare_fused_inputs(nc)
    out = nc.dram_tensor("x2", [TOK, E], F32, kind="ExternalOutput").ap()
    u_dram = nc.dram_tensor("u_scratch", [FFN_H, TOK], BF16, kind="Internal").ap()
    acc_dram = nc.dram_tensor("acc_scratch", [TOK, E], F32, kind="Internal").ap()
    qin = nc.dram_tensor("qin", [E, TOK], BF16, kind="Internal").ap()
    kin = nc.dram_tensor("kin", [E, TOK], BF16, kind="Internal").ap()
    vin = nc.dram_tensor("vin", [TOK, E], BF16, kind="Internal").ap()
    kg = nc.dram_tensor("kg", [NCORES * E, TOK], BF16, kind="Internal").ap()
    vg = nc.dram_tensor("vg", [NCORES * TOK, E], BF16, kind="Internal").ap()
    gselT_dram = nc.dram_tensor("gselT_d", [8, TOK], F32, kind="Internal").ap()
    em = Em(nc)
    cx = Ctx(nc, em)
    emit_consts(cx, AP)
    make_psum_pool(cx)
    emit_lambda(cx, AP)
    NT = TOK // 128

    xn_own = [cx.sb([128, TOK], BF16, f"xn_own{e}") for e in range(EC)]
    emit_l0(cx, AP, xn_own, u_dram, acc_dram)
    acc = [cx.sb([128, E], F32, f"accf{t}") for t in range(B * NQC)]
    emit_l0_ffn(cx, AP, xn_own, u_dram, acc_dram, acc)
    xtile = cx.sb([128, E], F32, "f_xres")
    tprev = None
    for b in range(B):
        for tc in range(NQC):
            i = b * NQC + tc
            tl = em.dma("sync", xtile[:], AP["xw"][b, WIN + tc * 128: WIN + (tc + 1) * 128, :],
                        deps=[tprev])
            tprev = em.op("dve", lambda v, i=i: v.tensor_tensor(
                acc[i][:], xtile[:], acc[i][:], ALU.add), deps=[tl])
    cx.t_x1 = tprev
    if DEBUG:
        dx1 = nc.dram_tensor("d_x1", [TOK, E], F32, kind="ExternalOutput").ap()
        for t in range(NT):
            em.dma("pool", dx1[ts(t), :], acc[t][:], deps=[cx.t_x1])
        em.barrier()
    cx.free(xtile)

    gselT_sb = cx.sb([8, TOK], F32, "f_gselT")
    xn1f = [cx.sb([128, TOK], BF16, f"f_xn1f{e}") for e in range(EC)]
    kv_toks = emit_l1pre_fused(cx, AP, acc, xn1f, gselT_sb, qin, kin, vin)
    t_gd = em.dma("pool", gselT_dram, gselT_sb[:], deps=[em.token("dve")])
    if DEBUG:
        dxn = nc.dram_tensor("d_xn1f", [E, TOK], BF16, kind="ExternalOutput").ap()
        for e in range(EC):
            em.dma("pool", dxn[ts(e), :], xn1f[e][:], deps=[em.token("dve")])
        dgs = nc.dram_tensor("d_gselT", [8, TOK], F32, kind="ExternalOutput").ap()
        em.dma("pool", dgs, gselT_sb[:], deps=[em.token("dve")])
        em.barrier()

    t_ck = em.op("pool", lambda g: g.collective_compute(
        "AllGather", ALU.bypass, replica_groups=[list(range(NCORES))],
        ins=[kin.opt()], outs=[kg.opt()]), deps=kv_toks, inc=1)
    t_cv = em.op("pool", lambda g: g.collective_compute(
        "AllGather", ALU.bypass, replica_groups=[list(range(NCORES))],
        ins=[vin.opt()], outs=[vg.opt()]), deps=[t_ck], inc=1)

    if DEBUG:
        cx.dbg_hfm = nc.dram_tensor("d_hfm", [E, TOK], BF16, kind="ExternalOutput").ap()
        cx.dbg_gbc = nc.dram_tensor("d_gbc", [128, TOK], F32, kind="ExternalOutput").ap()

    def grow_fn(ex, grow, deps):
        return em.dma("sync", grow[:], gselT_dram[ex:ex + 1, :],
                      deps=list(deps) + [t_gd])
    emit_moe(cx, AP, acc, xn1f, em.token("dve"), grow_fn=grow_fn, t_w0=t_cv)
    if DEBUG:
        dm = nc.dram_tensor("d_moe", [TOK, E], F32, kind="ExternalOutput").ap()
        for t in range(NT):
            em.dma("pool", dm[ts(t), :], acc[t][:], deps=[em.token("dve")])
        dxn2 = nc.dram_tensor("d_xn1f_post", [E, TOK], BF16, kind="ExternalOutput").ap()
        for e in range(EC):
            em.dma("pool", dxn2[ts(e), :], xn1f[e][:], deps=[em.token("dve")])
        dgd = nc.dram_tensor("d_gselTd", [8, TOK], F32, kind="ExternalOutput").ap()
        em.dma("pool", dgd, gselT_dram, deps=[em.token("dve")])
        em.barrier()
    cx.free(*xn1f)

    t_gate = em.token("pe")
    q1f = [cx.sb([128, TOK], BF16, f"f_q1f{h}") for h in range(L1_H)]
    for h in range(L1_H):
        em.dma("sync", q1f[h][:], qin[ts(h), :], deps=[em.token("pool")])
    t_q = em.token("sync")

    def k_src(h, b, dst, deps):
        tok = None
        for r in range(NCORES):
            tok = em.dma("sync", dst[:, ts(r, 512)],
                         kg[r * E + h * 128: r * E + (h + 1) * 128,
                            b * OWN:(b + 1) * OWN],
                         deps=list(deps) + [t_ck, t_gate])
        return tok

    def v_src(h, b, dst, deps):
        tok = None
        for r in range(NCORES):
            tok = em.dma("sync",
                         dst[:, ts(r, 512)].rearrange("p (kc d) -> p kc d", d=128),
                         vg[r * TOK + b * OWN: r * TOK + (b + 1) * OWN, ts(h)]
                         .rearrange("(kc p) d -> p kc d", p=128),
                         deps=list(deps) + [t_cv, t_gate])
        return tok

    attnf = emit_l1attn(cx, acc, q1f, k_src, v_src, t_q)
    if DEBUG:
        da = nc.dram_tensor("d_attnf", [E, TOK], BF16, kind="ExternalOutput").ap()
        for h in range(L1_H):
            em.dma("pool", da[ts(h), :], attnf[h][:], deps=[em.token("act")])
        em.barrier()
    emit_l1wo(cx, AP, acc, attnf)
    em.barrier()
    for t in range(NT):
        em.dma("sync", out[ts(t), :], acc[t][:], deps=[em.token("dve")])
    em.ops["sync"].append(("w", "sync", em.cnt["sync"]))
    em.replay()
    return nc



FUSED = True


def kernel(**inputs):
    I = {k: np.asarray(v) for k, v in inputs.items()}
    if not FUSED:
        return _kernel_3launch(I)
    nc = build_fused()
    in_maps = []
    for c in range(NCORES):
        m = prep_core_inputs(c, I)
        m.update({
            "L1_attn_Wq": I["L1_attn_Wq"], "L1_attn_Wk": I["L1_attn_Wk"],
            "L1_attn_Wv": I["L1_attn_Wv"], "L1_attn_Wo": I["L1_attn_Wo"],
            "L1_ffn_gate_kernel": I["L1_ffn_gate_kernel"],
            "lam_q1": I["L1_attn_lambda_q1"].reshape(64, 1),
            "lam_k1": I["L1_attn_lambda_k1"].reshape(64, 1),
            "lam_q2": I["L1_attn_lambda_q2"].reshape(64, 1),
            "lam_k2": I["L1_attn_lambda_k2"].reshape(64, 1),
            "subln": I["L1_attn_subln_weight"].reshape(128, 1),
            "L1_ffn_wi_0": I["L1_ffn_wi_0"], "L1_ffn_wi_1": I["L1_ffn_wi_1"],
            "L1_ffn_wo": I["L1_ffn_wo"],
        })
        in_maps.append(m)
    res = run_bass_kernel_spmd(nc, in_maps, core_ids=list(range(NCORES)))
    out = np.zeros((B, S, E), np.float32)
    for c in range(NCORES):
        r = np.asarray(res.results[c]["x2"], np.float32)
        for b in range(B):
            out[b, c * OWN:(c + 1) * OWN, :] = r[b * OWN:(b + 1) * OWN, :]
    return out


def _kernel_3launch(I):
    ident = np.eye(128, dtype=np.float32).astype(BFNP)
    prot = np.eye(128, dtype=np.float32)[:, list(range(64, 128)) + list(range(64))].copy()

    ncA = build_launch_A()
    in_maps = [prep_core_inputs(c, I) for c in range(NCORES)]
    resA = run_bass_kernel_spmd(ncA, in_maps, core_ids=list(range(NCORES)))
    x1c = [np.asarray(resA.results[c]["x1"]) for c in range(NCORES)]

    ncB = build_launch_B()
    in_mapsB = [{
        "x1in": x1c[c], "c_ident": ident, "c_prot": prot,
        "L1_attn_Wq": I["L1_attn_Wq"], "L1_attn_Wk": I["L1_attn_Wk"],
        "L1_attn_Wv": I["L1_attn_Wv"],
        "L1_ffn_gate_kernel": I["L1_ffn_gate_kernel"],
    } for c in range(NCORES)]
    resB = run_bass_kernel_spmd(ncB, in_mapsB, core_ids=list(range(NCORES)))

    k_all = np.zeros((E, B * S), dtype=BFNP)
    v_all = np.zeros((B * S, E), dtype=BFNP)
    for c in range(NCORES):
        k1 = np.asarray(resB.results[c]["k1"])
        v1 = np.asarray(resB.results[c]["v1"])
        if k1.dtype.itemsize == 2 and k1.dtype != BFNP:
            k1 = k1.view(BFNP); v1 = v1.view(BFNP)
        for b in range(B):
            k_all[:, b * S + c * OWN:(b * S + (c + 1) * OWN)] = k1[:, b * OWN:(b + 1) * OWN]
            v_all[b * S + c * OWN:(b * S + (c + 1) * OWN), :] = v1[b * OWN:(b + 1) * OWN, :]

    ncC = build_launch_C()
    in_mapsC = []
    for c in range(NCORES):
        q1 = np.asarray(resB.results[c]["q1"])
        xn1f = np.asarray(resB.results[c]["xn1f"])
        if q1.dtype.itemsize == 2 and q1.dtype != BFNP:
            q1 = q1.view(BFNP); xn1f = xn1f.view(BFNP)
        gsel = np.asarray(resB.results[c]["gsel"], np.float32)
        in_mapsC.append({
            "x1in": x1c[c], "c_ident": ident, "c_prot": prot,
            "xn1f": xn1f, "gselT": np.ascontiguousarray(gsel.T),
            "q1": q1, "k_all": k_all, "v_all": v_all,
            "L1_attn_Wo": I["L1_attn_Wo"],
            "lam_q1": I["L1_attn_lambda_q1"].reshape(64, 1),
            "lam_k1": I["L1_attn_lambda_k1"].reshape(64, 1),
            "lam_q2": I["L1_attn_lambda_q2"].reshape(64, 1),
            "lam_k2": I["L1_attn_lambda_k2"].reshape(64, 1),
            "subln": I["L1_attn_subln_weight"].reshape(128, 1),
            "L1_ffn_wi_0": I["L1_ffn_wi_0"], "L1_ffn_wi_1": I["L1_ffn_wi_1"],
            "L1_ffn_wo": I["L1_ffn_wo"],
        })
    resC = run_bass_kernel_spmd(ncC, in_mapsC, core_ids=list(range(NCORES)))

    out = np.zeros((B, S, E), np.float32)
    for c in range(NCORES):
        r = np.asarray(resC.results[c]["x2"], np.float32)
        for b in range(B):
            out[b, c * OWN:(c + 1) * OWN, :] = r[b * OWN:(b + 1) * OWN, :]
    return out


def prep_core_inputs(core, I):
    q0 = core * OWN
    x = I["L0_attn_hidden_states"]
    xw = np.zeros((B, WROWS, E), np.float32)
    lo, hi = q0 - WIN, q0 + OWN + WIN
    slo, shi = max(lo, 0), min(hi, S)
    xw[:, slo - lo:shi - lo, :] = x[:, slo:shi, :]
    pos = np.clip(np.arange(lo, hi), 0, S - 1)
    cosw, sinw = _rope_tables_np(pos)
    return {
        "xw": np.ascontiguousarray(xw),
        "c_ident": np.eye(128, dtype=np.float32).astype(BFNP),
        "c_prot": np.eye(128, dtype=np.float32)[:, list(range(64, 128)) + list(range(64))].copy(),
        "c_cosw": cosw, "c_sinw": sinw,
        "c_l0mask": l0_mask_values(core),
        "qscale": np.ascontiguousarray(I["L0_attn_query_scale"].reshape(128, 1)),
        "kscale": np.ascontiguousarray(I["L0_attn_key_scale"].reshape(128, 1)),
        "L0_attn_wq": I["L0_attn_wq"], "L0_attn_wk": I["L0_attn_wk"],
        "L0_attn_wv": I["L0_attn_wv"], "L0_attn_wo": I["L0_attn_wo"],
        "L0_ffn_W_fc": I["L0_ffn_W_fc"], "L0_ffn_W_proj": I["L0_ffn_W_proj"],
    }


# version 05
# speedup vs baseline: 1.0119x; 1.0119x over previous
import contextlib
import math

import numpy as np
import ml_dtypes
import concourse.bass as bass
import concourse.mybir as mybir
from concourse.bass_utils import run_bass_kernel_spmd

F32 = mybir.dt.float32
BF16 = mybir.dt.bfloat16
AF = mybir.ActivationFunctionType
ALU = mybir.AluOpType
AX = mybir.AxisListType
BFNP = ml_dtypes.bfloat16

NCORES = 8
B = 2
S = 4096
E = 2048
OWN = 512
TOK = B * OWN
WIN = 1024
WROWS = OWN + 2 * WIN
NT_W = WROWS // 128
NQC = OWN // 128
NKC = WROWS // 128
EC = E // 128
L0_KV = 4
L0_H = 16
CAP = 50.0
L1_H = 16
LAMBDA_INIT = 0.8 - 0.6 * float(np.exp(-0.3 * 12))
FFN_H = 8192
HC = FFN_H // 128
MOE_E = 8

ENGINES = ("pe", "dve", "act", "pool", "sync")
CCSEMS = ("cc0", "cc1")
DEBUG = False


class Em:
    def __init__(self, nc):
        self.nc = nc
        self.ops = {e: [] for e in ENGINES}
        self.cnt = {e: 0 for e in ENGINES + CCSEMS}
        self.wm = {e: {s: 0 for s in ENGINES} for e in ENGINES}
        self.pend = {e: {} for e in ENGINES}

    def op(self, eng, fn, deps=(), inc=1, sem=None, emit_inc=None):
        for ps, val in self.pend[eng].items():
            if self.wm[eng].get(ps, 0) < val:
                self.ops[eng].append(("w", ps, val))
                self.wm[eng][ps] = val
        self.pend[eng] = {}
        for dep in deps:
            if dep is None:
                continue
            ds, val = dep
            if ds == eng:
                if val > self.cnt[eng]:
                    continue
            if self.wm[eng].get(ds, 0) >= val or val <= 0:
                continue
            self.wm[eng][ds] = val
            self.ops[eng].append(("w", ds, val))
        key = sem if sem is not None else eng
        self.cnt[key] += inc
        self.ops[eng].append(("i", fn, inc if emit_inc is None else emit_inc,
                              sem))
        return (key, self.cnt[key])

    def dma(self, eng, out, in_, deps=(), **kw):
        return self.op(eng, lambda q: q.dma_start(out=out, in_=in_, **kw),
                       deps=deps, inc=16)

    def barrier(self, engines=ENGINES, on=ENGINES):
        snap = {e: self.cnt[e] for e in on}
        for e in engines:
            for s, v in snap.items():
                if v > 0 and s != e and self.pend[e].get(s, 0) < v:
                    self.pend[e][s] = v
        return snap

    def token(self, eng):
        return (eng, self.cnt[eng])

    def replay(self):
        nc = self.nc
        with contextlib.ExitStack() as st:
            sems = {e: st.enter_context(nc.semaphore("sem_" + e))
                    for e in ENGINES + CCSEMS}
            block = st.enter_context(nc.Block())
            eng_obj = {"pe": nc.tensor, "dve": nc.vector, "act": nc.scalar,
                       "pool": nc.gpsimd, "sync": nc.sync}
            deco = {"pe": block.tensor, "dve": block.vector, "act": block.scalar,
                    "pool": block.gpsimd, "sync": block.sync}

            def mk(ename):
                ops = self.ops[ename]
                my = eng_obj[ename]
                sem = sems[ename]

                def body(_q):
                    for rec in ops:
                        if rec[0] == "w":
                            my.wait_ge(sems[rec[1]], rec[2])
                        else:
                            inst = rec[1](my)
                            if rec[2]:
                                tgt = sems[rec[3]] if rec[3] else sem
                                inst.then_inc(tgt, rec[2])
                return body

            for e in ENGINES:
                deco[e](mk(e))


def ts(i, size=128):
    return slice(i * size, (i + 1) * size)


class Ctx:
    def __init__(self, nc, em):
        self.nc = nc
        self.em = em
        self._n = 0
        self.pt = {}
        self._handles = {}
        self._order = {}

    def sb(self, shape, dtype=F32, name=None):
        self._n += 1
        h = self.nc.sbuf_tensor(f"{name or 't'}_{self._n}", list(shape), dtype)
        t = h.__enter__()
        self._handles[id(t)] = h
        self._order[id(t)] = self._n
        return t

    def ps(self, shape, dtype=F32, name=None):
        self._n += 1
        h = self.nc.psum_tensor(f"{name or 'p'}_{self._n}", list(shape), dtype)
        t = h.__enter__()
        self._handles[id(t)] = h
        return t

    def free(self, *tensors):
        self.em.barrier()
        for t in sorted(tensors, key=lambda x: -self._order[id(x)]):
            self._handles.pop(id(t)).__exit__(None, None, None)

    def pdep(self, pt):
        return self.pt.get(id(pt))

    def puse(self, pt, tok):
        self.pt[id(pt)] = tok


def emit_consts(cx, AP):
    em = cx.em
    cx.ident = cx.sb([128, 128], BF16, "ident")
    cx.ones_cb = cx.sb([128, 1], BF16, "ones_cb")
    cx.ones_cf = cx.sb([128, 1], F32, "ones_cf")
    cx.ones_rf = cx.sb([1, 128], F32, "ones_rf")
    cx.t_ident = em.dma("sync", cx.ident[:], AP["c_ident"])
    m1 = em.op("dve", lambda v: v.memset(cx.ones_cb[:], 1.0), inc=0)
    m2 = em.op("dve", lambda v: v.memset(cx.ones_cf[:], 1.0), inc=0)
    cx.t_ones = em.op("dve", lambda v: v.memset(cx.ones_rf[:], 1.0), inc=1)
    cx.prot = cx.sb([128, 128], F32, "prot")
    cx.t_prot = em.dma("sync", cx.prot[:], AP["c_prot"])
    cx.bias_ln = cx.sb([128, 1], F32, "bias_ln")
    cx.bias_eps = cx.sb([128, 1], F32, "bias_eps")
    cx.bias_ncap = cx.sb([128, 1], F32, "bias_ncap")
    em.op("dve", lambda v: v.memset(cx.bias_ln[:], 1e-5), inc=0)
    em.op("dve", lambda v: v.memset(cx.bias_eps[:], 1.1920929e-07), inc=0)
    cx.t_bias = em.op("dve", lambda v: v.memset(cx.bias_ncap[:], -CAP), inc=1)


def layernorm_tile(cx, xt, out_bf, scr, stats, deps):
    em = cx.em
    s1 = stats[:, 0:1]; s2 = stats[:, 1:2]; mcol = stats[:, 2:3]
    vcol = stats[:, 3:4]; sdcol = stats[:, 4:5]; rcol = stats[:, 5:6]
    deps = list(deps)
    t1 = em.op("dve", lambda v: v.tensor_reduce(s1, xt[:], AX.X, ALU.add), deps=deps)
    t2 = em.op("act", lambda a: a.activation(scr[:], xt[:], AF.Square), deps=deps)
    t3 = em.op("dve", lambda v: v.tensor_reduce(s2, scr[:], AX.X, ALU.add), deps=[t2, t1])
    t4 = em.op("dve", lambda v: v.tensor_scalar(mcol, s1, 1.0 / E, None, ALU.mult), deps=[t3])
    t5 = em.op("dve", lambda v: v.tensor_scalar(vcol, s2, 1.0 / E, None, ALU.mult), deps=[t4])
    t6 = em.op("dve", lambda v: v.tensor_tensor(sdcol, mcol, mcol, ALU.mult), deps=[t5])
    t7 = em.op("dve", lambda v: v.tensor_tensor(vcol, vcol, sdcol, ALU.subtract), deps=[t6])
    t8 = em.op("act", lambda a: a.activation(sdcol, vcol, AF.Sqrt, bias=cx.bias_ln[:]),
               deps=[t7, cx.t_bias])
    t9 = em.op("dve", lambda v: v.reciprocal(rcol, sdcol), deps=[t8])
    tA = em.op("dve", lambda v: v.tensor_scalar(out_bf[:], xt[:], mcol, rcol,
                                                ALU.subtract, ALU.mult), deps=[t9])
    return tA




def _rope_tables_np(positions):
    half = 64
    inv = (1.0 / 10000.0 ** (np.arange(0, half, dtype=np.float32) / half))
    ang = np.outer(positions.astype(np.float32), inv)
    ang = np.concatenate([ang, ang], -1)
    cos = np.cos(ang).astype(np.float32)
    sin = np.sin(ang).astype(np.float32)
    sinA = sin.copy()
    sinA[:, :half] *= -1.0
    return np.ascontiguousarray(cos.T), np.ascontiguousarray(sinA.T)


def l0_mask_pairs():
    pairs = []
    for kc in range(NKC):
        qlo, qhi = max(0, kc - 16), min(NQC - 1, kc)
        for qc in range(qlo, qhi + 1):
            if kc - qc in (0, 16) or kc < WIN // 128 or kc >= (WIN + OWN) // 128:
                pairs.append((kc, qc))
    return pairs


_L0_PAIRS = l0_mask_pairs()
_L0_PAIR_IDX = {p: i for i, p in enumerate(_L0_PAIRS)}


def l0_mask_values(core):
    q0 = core * OWN
    out = np.zeros((len(_L0_PAIRS), 128, 128), np.float32)
    for i, (kc, qc) in enumerate(_L0_PAIRS):
        kpos = q0 - WIN + kc * 128 + np.arange(128)
        qpos = q0 + qc * 128 + np.arange(128)
        valid = (np.abs(qpos[None, :] - kpos[:, None]) <= WIN) & \
                (kpos[:, None] >= 0) & (kpos[:, None] < S)
        out[i] = valid.astype(np.float32)
    return np.ascontiguousarray(out.astype(BFNP))


def declare_l0_inputs(nc):
    ap = {}
    def inp(n, sh, dt=F32):
        ap[n] = nc.dram_tensor(n, sh, dt, kind="ExternalInput").ap()
    inp("xw", [B, WROWS, E])
    inp("c_ident", [128, 128], BF16)
    inp("c_prot", [128, 128])
    inp("c_cosw", [128, WROWS]); inp("c_sinw", [128, WROWS])
    inp("c_l0mask", [len(_L0_PAIRS), 128, 128], BF16)
    inp("qscale", [128, 1]); inp("kscale", [128, 1])
    inp("L0_attn_wq", [E, E]); inp("L0_attn_wk", [E, 512])
    inp("L0_attn_wv", [E, 512]); inp("L0_attn_wo", [E, E])
    inp("L0_ffn_W_fc", [FFN_H, E]); inp("L0_ffn_W_proj", [E, FFN_H])
    return ap




def transpose128(cx, dst_ap, src_ap, deps):
    em = cx.em
    bank = cx.tr_rot % 2
    cx.tr_rot += 1
    pb = cx.pp["tr0" if bank == 0 else "tr1"]
    tt = em.op("pe", lambda t: t.transpose(pb[:, 0:128], src_ap, cx.ident[:]),
               deps=list(deps) + [cx.t_ident, cx.pdep(pb)])
    te = em.op("dve", lambda v: v.tensor_copy(dst_ap, pb[:, 0:128]), deps=[tt])
    cx.puse(pb, te)
    return te


def rope_norm(cx, raw, dst_bf, width, scale_col, rc, rs, sc1, sc2, sqs, rows,
              deps):
    em = cx.em
    rowp = cx.pp["row"]
    bc = cx.pp["c"]
    rotp = cx.pp["o"]
    deps = list(deps)
    a1 = em.op("dve", lambda v: v.tensor_tensor(sc1[:, 0:width], raw[:, 0:width],
                                                rc[:, 0:width], ALU.mult), deps=deps)
    a2 = em.op("pe", lambda t: t.matmul(rotp[:, 0:width], cx.prot[:], raw[:, 0:width],
                                        start=True, stop=True),
               deps=deps + [cx.t_prot, cx.pdep(rotp)])
    a3 = em.op("dve", lambda v: v.tensor_tensor(sc2[:, 0:width], rotp[:, 0:width],
                                                rs[:, 0:width], ALU.mult), deps=[a2])
    cx.puse(rotp, a3)
    a4 = em.op("dve", lambda v: v.tensor_tensor(sc1[:, 0:width], sc1[:, 0:width],
                                                sc2[:, 0:width], ALU.add), deps=[a1, a3])
    a5 = em.op("act", lambda a: a.activation(sqs[:, 0:width], sc1[:, 0:width], AF.Square),
               deps=[a4])
    np_ = (width + 511) // 512
    sq_toks = []
    for p in range(np_):
        w0, w1 = p * 512, min(width, (p + 1) * 512)
        tm = em.op("pe", lambda t, w0=w0, w1=w1: t.matmul(
            rowp[0:1, 0:w1 - w0], cx.ones_cb[:], sqs[:, w0:w1], start=True, stop=True),
            deps=[a5, cx.t_ones, cx.pdep(rowp)])
        tq = em.op("act", lambda a, w0=w0, w1=w1: a.activation(
            rows[:, w0:w1], rowp[0:1, 0:w1 - w0], AF.Sqrt), deps=[tm])
        cx.puse(rowp, tq)
        sq_toks.append(tq)
    a7 = em.op("dve", lambda v: v.tensor_scalar(rows[:, 0:width], rows[:, 0:width],
                                                1e-6, None, ALU.add), deps=sq_toks)
    a8 = em.op("dve", lambda v: v.reciprocal(rows[:, 0:width], rows[:, 0:width]), deps=[a7])
    out_toks = []
    for p in range(np_):
        w0, w1 = p * 512, min(width, (p + 1) * 512)
        tb = em.op("pe", lambda t, w0=w0, w1=w1: t.matmul(
            bc[:, 0:w1 - w0], cx.ones_rf[:], rows[:, w0:w1], start=True, stop=True),
            deps=[a8, cx.t_ones, cx.pdep(bc)])
        tm = em.op("dve", lambda v, w0=w0, w1=w1: v.tensor_tensor(
            sc2[:, w0:w1], sc1[:, w0:w1], bc[:, 0:w1 - w0], ALU.mult), deps=[tb])
        cx.puse(bc, tm)
        tc = em.op("act", lambda a, w0=w0, w1=w1: a.activation(
            dst_bf[:, w0:w1], sc2[:, w0:w1], AF.Copy, scale=scale_col), deps=[tm])
        out_toks.append(tc)
    return out_toks


L0_D_CONST = 128


def xn_own_cols(xn_own, e, b):
    return xn_own[e][:, b * OWN:(b + 1) * OWN]


def emit_l0(cx, AP, xn_own, u_dram, acc_dram):
    em = cx.em

    rope_cos = cx.sb([128, WROWS], BF16, "ropec")
    rope_sin = cx.sb([128, WROWS], BF16, "ropes")
    qscale = cx.sb([128, 1], F32, "qscale")
    kscale = cx.sb([128, 1], F32, "kscale")
    t_rc = em.dma("pool", rope_cos[:], AP["c_cosw"])
    t_rs = em.dma("pool", rope_sin[:], AP["c_sinw"])
    t_qs = em.dma("sync", qscale[:], AP["qscale"])
    t_ks = em.dma("sync", kscale[:], AP["kscale"])

    pA = pB = cx.pp["a"]

    sc1 = cx.sb([128, 512], F32, "sc1")
    sc2 = cx.sb([128, 512], F32, "sc2")
    sqs = cx.sb([128, 512], BF16, "sqs")
    rows = cx.sb([1, 512], F32, "rows")
    kraw = cx.sb([128, 512], F32, "kraw")
    vpc = cx.sb([128, 512], BF16, "vpc")
    wblk = [cx.sb([128, EC * 128], BF16, f"wblk{i}") for i in range(2)]
    npieces = WROWS // 512

    for b in range(B):
        K = [cx.sb([128, WROWS], BF16, f"K_{h}") for h in range(L0_KV)]
        Vt = [cx.sb([128, NKC * 128], BF16, f"V_{h}") for h in range(L0_KV)]
        xnw = [cx.sb([128, WROWS], BF16, f"xnw_{e}") for e in range(EC)]
        xt = [cx.sb([128, E], F32, "lnx0")]
        scr = cx.sb([128, E], F32, "lnscr")
        xbf = cx.sb([128, E], BF16, "lnbf")
        stats = cx.sb([128, 8], F32, "lnstat")
        tile_done = []
        prev_use = [None]
        for t in range(NT_W):
            buf = xt[0]
            tl = em.dma("sync", buf[:], AP["xw"][b, ts(t), :], deps=[prev_use[0]])
            ln = layernorm_tile(cx, buf, xbf, scr, stats, [tl])
            last = None
            for e in range(EC):
                last = transpose128(cx, xnw[e][:, ts(t)], xbf[:, ts(e)], [ln])
            tile_done.append(last)
            prev_use[0] = last
        if DEBUG and b == 0 and hasattr(cx.nc, "_dbg_xnw"):
            em.dma("pool", cx.nc._dbg_xnw[0:128, :], xnw[4][:], deps=tile_done)
            em.dma("pool", cx.nc._dbg_xnw[128:256, :], xnw[5][:], deps=tile_done)
            em.barrier()
        for e in range(EC):
            em.op("dve", lambda v, e=e, b=b: v.tensor_copy(
                xn_own[e][:, b * OWN:(b + 1) * OWN], xnw[e][:, WIN:WIN + OWN]),
                deps=[tile_done[-1]] if e == 0 else ())
        t_xnown = em.token("dve")
        cx.free(xt[0], scr, xbf, stats)

        wtoks = [None, None]
        kv_done = {}
        q_done = {}
        for kh in range(L0_KV):
            wb = wblk[kh % 2]
            t_wk = em.dma("pool", wb[:].rearrange("p (ec i) -> p ec i", i=128),
                          AP["L0_attn_wk"][:, ts(kh)].rearrange("(ec p) i -> p ec i", p=128),
                          deps=[wtoks[kh % 2]])
            for p in range(npieces):
                pm = pA
                mmlast = None
                for e in range(EC):
                    mmlast = em.op("pe", lambda t, e=e, p=p, wb=wb, pm=pm: t.matmul(
                        pm[:], wb[:, ts(e)],
                        xnw[e][:, ts(p, 512)], start=(e == 0), stop=(e == EC - 1)),
                        deps=([t_wk, cx.pdep(pm)] + tile_done) if e == 0 else ())
                wtoks[kh % 2] = mmlast
                tev = em.op("dve", lambda v, pm=pm: v.tensor_copy(kraw[:], pm[:]),
                            deps=[mmlast])
                cx.puse(pm, tev)
                kv_done.setdefault(kh, []).extend(rope_norm(
                    cx, kraw, K[kh][:, ts(p, 512)], 512, kscale[:],
                    rope_cos[:, ts(p, 512)], rope_sin[:, ts(p, 512)],
                    sc1, sc2, sqs, rows, [tev, t_rc, t_rs, t_ks]))
            t_wv = em.dma("pool", wb[:].rearrange("p (ec i) -> p ec i", i=128),
                          AP["L0_attn_wv"][:, ts(kh)].rearrange("(ec p) i -> p ec i", p=128),
                          deps=[wtoks[kh % 2], em.token("pe")])
            for p in range(npieces):
                pm = pA
                mmlast = None
                for e in range(EC):
                    mmlast = em.op("pe", lambda t, e=e, p=p, wb=wb, pm=pm: t.matmul(
                        pm[:], wb[:, ts(e)],
                        xnw[e][:, ts(p, 512)], start=(e == 0), stop=(e == EC - 1)),
                        deps=[t_wv, cx.pdep(pm)] if e == 0 else ())
                wtoks[kh % 2] = mmlast
                tev = em.op("dve", lambda v, pm=pm: v.tensor_copy(vpc[:], pm[:]),
                            deps=[mmlast])
                cx.puse(pm, tev)
                for j in range(4):
                    transpose128(cx, Vt[kh][:, ts(p * 4 + j)], vpc[:, ts(j)], [tev])
        t_kv = em.token("dve")
        if DEBUG and b == 0 and hasattr(cx.nc, "_dbg_k"):
            em.dma("pool", cx.nc._dbg_k, K[0][:], deps=[t_kv])
            for e in range(EC):
                em.dma("pool", cx.nc._dbg_xn[ts(e), :], xn_own[e][:], deps=[t_xnown])
            em.barrier()
        cx.free(*xnw)

        Qf = [cx.sb([128, OWN], BF16, f"Qf_{h}") for h in range(L0_H)]
        qraw = cx.sb([128, OWN], F32, "qraw")
        for h in range(L0_H):
            wb = wblk[h % 2]
            t_wq = em.dma("pool", wb[:].rearrange("p (ec i) -> p ec i", i=128),
                          AP["L0_attn_wq"][:, ts(h)].rearrange("(ec p) i -> p ec i", p=128),
                          deps=[wtoks[h % 2], em.token("pe")])
            mmlast = None
            for e in range(EC):
                mmlast = em.op("pe", lambda t, e=e, b=b, wb=wb: t.matmul(
                    pA[:], wb[:, ts(e)], xn_own_cols(xn_own, e, b),
                    start=(e == 0), stop=(e == EC - 1)),
                    deps=[t_wq, cx.pdep(pA), t_xnown] if e == 0 else ())
            wtoks[h % 2] = mmlast
            tev = em.op("dve", lambda v: v.tensor_copy(qraw[:], pA[:]), deps=[mmlast])
            cx.puse(pA, tev)
            q_done[h] = rope_norm(cx, qraw, Qf[h], OWN, qscale[:],
                      rope_cos[:, WIN:WIN + OWN], rope_sin[:, WIN:WIN + OWN],
                      sc1, sc2, sqs, rows, [tev, t_qs])

        if DEBUG and b == 0 and hasattr(cx.nc, "_dbg_q"):
            em.dma("pool", cx.nc._dbg_q, Qf[0][:], deps=[em.token("act"), em.token("dve")])
            em.barrier()
        masks = cx.sb([128, len(_L0_PAIRS) * 128], BF16, "l0mask")
        t_mk = em.dma("sync", masks[:].rearrange("p (n k) -> p n k", k=128),
                      AP["c_l0mask"].rearrange("n p k -> p n k"))
        attn = [cx.sb([128, OWN], BF16, f"attn_{h}") for h in range(L0_H)]
        tS = cx.sb([128, 512], F32, "tS")
        pT = cx.sb([128, 512], BF16, "pTl0")
        rl = cx.sb([1, 512], F32, "rl")
        of = cx.sb([128, 512], F32, "of")
        ps_l = cx.pp["row"]
        ps_o = cx.pp["o"]
        ps_bc = cx.pp["c"]
        for h in range(L0_H):
            kh = h // (L0_H // L0_KV)
            t_pT_prev = None
            tl = tv = None
            for kc in range(NKC):
                qlo, qhi = max(0, kc - 16), min(NQC - 1, kc)
                r0, r1 = qlo * 128, (qhi + 1) * 128
                tmm = em.op("pe", lambda t, kc=kc, kh=kh, h=h, r0=r0, r1=r1: t.matmul(
                    pA[:, 0:r1 - r0], K[kh][:, ts(kc)], Qf[h][:, r0:r1],
                    start=True, stop=True),
                    deps=[cx.pdep(pA)] + (kv_done[kh] + q_done[h] if kc == 0 else []))
                tt = em.op("act", lambda a, r0=r0, r1=r1: a.activation(
                    tS[:, 0:r1 - r0], pA[:, 0:r1 - r0], AF.Tanh,
                    scale=1.0 / (CAP * math.sqrt(L0_D_CONST))), deps=[tmm])
                cx.puse(pA, tt)
                tex = em.op("act", lambda a, r0=r0, r1=r1: a.activation(
                    pT[:, 0:r1 - r0], tS[:, 0:r1 - r0], AF.Exp, scale=CAP,
                    bias=cx.bias_ncap[:]), deps=[tt, t_pT_prev, cx.t_bias])
                mtoks = [tex]
                for qc in range(qlo, qhi + 1):
                    pi = _L0_PAIR_IDX.get((kc, qc))
                    if pi is not None:
                        mtoks.append(em.op("pool", lambda g, pi=pi, qc=qc, r0=r0: g.tensor_tensor(
                            pT[:, qc * 128 - r0: qc * 128 - r0 + 128],
                            pT[:, qc * 128 - r0: qc * 128 - r0 + 128],
                            masks[:, ts(pi)], ALU.mult), deps=[tex, t_mk]))
                tl = em.op("pe", lambda t, kc=kc, r0=r0, r1=r1: t.matmul(
                    ps_l[0:1, r0:r1], cx.ones_cb[:], pT[:, 0:r1 - r0],
                    start=(kc == 0), stop=(kc == NKC - 1)),
                    deps=mtoks + ([cx.pdep(ps_l)] if kc == 0 else []))
                tv = em.op("pe", lambda t, kc=kc, kh=kh, r0=r0, r1=r1: t.matmul(
                    ps_o[:, r0:r1], Vt[kh][:, ts(kc)], pT[:, 0:r1 - r0],
                    start=(kc == 0), stop=(kc == NKC - 1)),
                    deps=([t_kv, cx.pdep(ps_o)] if kc == 0 else []))
                t_pT_prev = tv
            trc = em.op("dve", lambda v: v.reciprocal(rl[:], ps_l[0:1, 0:512]), deps=[tl, tv])
            cx.puse(ps_l, trc)
            tb = em.op("pe", lambda t: t.matmul(ps_bc[:], cx.ones_rf[:], rl[:],
                                                start=True, stop=True),
                       deps=[trc, cx.pdep(ps_bc)])
            tof = em.op("dve", lambda v: v.tensor_copy(of[:], ps_o[:, 0:512]), deps=[tv, tb])
            cx.puse(ps_o, tof)
            tat = em.op("dve", lambda v, h=h: v.tensor_tensor(
                attn[h][:], of[:], ps_bc[:], ALU.mult), deps=[tof])
            cx.puse(ps_bc, tat)

        cx.free(tS, pT, rl, of)

        t_attn_all = em.token("dve")
        if DEBUG and b == 0 and hasattr(cx.nc, "_dbg_at"):
            em.dma("pool", cx.nc._dbg_at, attn[0][:], deps=[t_attn_all])
            em.barrier()
        wo_grp = cx.sb([128, 4 * E], BF16, "wo_grp")
        acc = [cx.sb([128, E], F32, f"accw{t}") for t in range(NQC)]
        ps_w = cx.pp["w"]
        tev = None
        for g in range(4):
            t_wo = em.dma("pool", wo_grp[:].rearrange("p (hh e) -> p hh e", e=E),
                          AP["L0_attn_wo"][ts(g, 512), :].rearrange("(hh p) e -> p hh e", p=128),
                          deps=[getattr(cx, "pdep_wo", None)])
            for tc in range(NQC):
                for half in range(2):
                    mmlast = None
                    for hh in range(4):
                        h = g * 4 + hh
                        for pc in range(2):
                            mmlast = em.op("pe", lambda t, h=h, hh=hh, tc=tc, pc=pc, half=half: t.matmul(
                                ps_w[:, ts(pc, 512)], attn[h][:, ts(tc)],
                                wo_grp[:, hh * E + half * 1024 + pc * 512:
                                       hh * E + half * 1024 + (pc + 1) * 512],
                                start=(hh == 0), stop=(hh == 3)),
                                deps=[t_wo, cx.pdep(ps_w), t_attn_all]
                                if (hh == 0 and pc == 0) else ())
                    if g == 0:
                        tev = em.op("dve", lambda v, tc=tc, half=half: v.tensor_copy(
                            acc[tc][:, half * 1024:(half + 1) * 1024], ps_w[:]),
                            deps=[mmlast])
                    else:
                        tev = em.op("dve", lambda v, tc=tc, half=half: v.tensor_tensor(
                            acc[tc][:, half * 1024:(half + 1) * 1024],
                            acc[tc][:, half * 1024:(half + 1) * 1024],
                            ps_w[:], ALU.add), deps=[mmlast])
                    cx.puse(ps_w, tev)
            cx.pdep_wo = tev
        sp_toks = []
        for tc in range(NQC):
            sp_toks.append(em.dma("pool", acc_dram[(b * NQC + tc) * 128:
                                                   (b * NQC + tc + 1) * 128, :],
                                  acc[tc][:], deps=[tev]))
        em.barrier()
        cx.free(*(acc + [wo_grp] + attn + [masks, qraw] + Qf + Vt + K))

    cx.free(sc1, sc2, sqs, rows, kraw, vpc, wblk[0], wblk[1],
            rope_cos, rope_sin, qscale, kscale)
    em.barrier()


def quantize_tile(cx, wt_bf, gscr, width, deps):
    em = cx.em
    ng = width // 64
    s_mean = gscr[:, 0:ng]
    s_meanb = cx.qz_meanb[:, 0:ng]
    s_recip = gscr[:, ng:2 * ng]
    deps = list(deps)
    t1 = em.op("dve", lambda v: v.tensor_reduce(
        s_mean, wt_bf[:, 0:width].rearrange("p (g k) -> p g k", k=64),
        AX.X, ALU.add, apply_absolute_value=True), deps=deps)
    t2 = em.op("act", lambda a: a.activation(s_meanb, s_mean, AF.Copy, scale=1.0 / 64.0),
               deps=[t1])
    t3 = em.op("dve", lambda v: v.tensor_scalar(s_meanb, s_meanb, 1e-8, None, ALU.max),
               deps=[t2])
    t4 = em.op("dve", lambda v: v.reciprocal(s_recip, s_meanb), deps=[t3])
    rb = cx.qz_ratio[:, 0:width]
    qa = cx.qz_qa[:, 0:width]
    t5 = em.op("dve", lambda v: v.tensor_tensor(
        rb.rearrange("p (g k) -> p g k", k=64),
        wt_bf[:, 0:width].rearrange("p (g k) -> p g k", k=64),
        s_recip.unsqueeze(2).broadcast_to((128, ng, 64)), ALU.mult),
        deps=[t4])
    t6 = em.op("pool", lambda g: g.tensor_scalar(qa, rb, 0.5, None, ALU.is_gt), deps=[t5])
    t7 = em.op("dve", lambda v: v.tensor_scalar(rb, rb, -0.5, None, ALU.is_lt), deps=[t6])
    t8 = em.op("pool", lambda g: g.tensor_tensor(qa, qa, rb, ALU.subtract), deps=[t7])
    t9 = em.op("dve", lambda v: v.tensor_tensor(
        wt_bf[:, 0:width].rearrange("p (g k) -> p g k", k=64),
        qa.rearrange("p (g k) -> p g k", k=64),
        s_meanb.unsqueeze(2).broadcast_to((128, ng, 64)), ALU.mult),
        deps=[t8])
    return t9


def emit_l0_ffn(cx, AP, xn_own, u_dram, acc_dram, acc):
    em = cx.em
    cx.qz_meanb = cx.sb([128, 512], BF16, "qz_meanb")
    cx.qz_ratio = cx.sb([128, 8192], BF16, "qz_ratio")
    cx.qz_qa = cx.sb([128, 8192], BF16, "qz_qa")
    gscr = cx.sb([128, 1024], F32, "qz_gscr")

    pA = pB = cx.pp["a"]
    msacc = cx.sb([128, 8], F32, "msacc")
    rstd = cx.sb([128, 8], F32, "rstd")
    t_ms0 = em.op("dve", lambda v: v.memset(msacc[:], 0.0), inc=1)

    wfc_bf = cx.sb([128, 4 * E], BF16, "wfc_bf")
    w1qT = cx.sb([128, EC * 512], BF16, "w1qT")
    rrelu = cx.sb([128, 512], F32, "rrelu")
    u32 = cx.sb([128, 512], F32, "u32")
    ubf = cx.sb([128, 512], BF16, "ubf")
    u2d = cx.sb([128, 512], BF16, "u2dump")
    part = cx.sb([128, 1], F32, "part")
    uT = cx.sb([128, 512], BF16, "uT")

    for grp in range(16):
        qz = []
        for j in range(4):
            tl = em.dma("pool", wfc_bf[:, j * E:(j + 1) * E],
                        AP["L0_ffn_W_fc"][ts(grp * 4 + j), :],
                        deps=[getattr(cx, "pdep_wfc", None)])
            qz.append(quantize_tile(cx, wfc_bf[:, j * E:(j + 1) * E], gscr, E, [tl]))
        ttok = []
        for j in range(4):
            for e in range(EC):
                ttok.append(transpose128(cx, w1qT[:, e * 512 + j * 128: e * 512 + (j + 1) * 128],
                                         wfc_bf[:, j * E + e * 128: j * E + (e + 1) * 128],
                                         [qz[j]]))
        cx.pdep_wfc = ttok[-1]
        for b in range(B):
            for tc in range(NQC):
                pm = pA
                mmlast = None
                for e in range(EC):
                    mmlast = em.op("pe", lambda t, e=e, b=b, tc=tc, pm=pm: t.matmul(
                        pm[:], xn_own[e][:, b * OWN + tc * 128: b * OWN + (tc + 1) * 128],
                        w1qT[:, ts(e, 512)], start=(e == 0), stop=(e == EC - 1)),
                        deps=(ttok + [cx.pdep(pm)]) if e == 0 else ())
                t_r = em.op("act", lambda a, pm=pm: a.activation(rrelu[:], pm[:], AF.Relu),
                            deps=[mmlast])
                cx.puse(pm, t_r)
                t_u = em.op("act", lambda a: a.activation(u32[:], rrelu[:], AF.Square),
                            deps=[t_r])
                t_ub = em.op("dve", lambda v: v.tensor_copy(ubf[:], u32[:]), deps=[t_u])
                t_u2 = em.op("act", lambda a: a.activation(
                    u2d[:], u32[:], AF.Square, accum_out=part[:]), deps=[t_u, t_ub])
                col = b * NQC + tc
                em.op("dve", lambda v, col=col: v.tensor_tensor(
                    msacc[:, col:col + 1], msacc[:, col:col + 1], part[:], ALU.add),
                    deps=[t_u2, t_ms0])
                utok = []
                for j in range(4):
                    utok.append(transpose128(cx, uT[:, ts(j)], ubf[:, ts(j)],
                                             [t_ub, getattr(cx, "t_uspill", None)]))
                cx.t_uspill = em.dma("pool", u_dram[ts(grp, 512), col * 128:(col + 1) * 128]
                                     .rearrange("(hs p) t -> p hs t", p=128),
                                     uT[:].rearrange("p (hs t) -> p hs t", t=128), deps=utok)
    t_sq = em.op("act", lambda a: a.activation(rstd[:], msacc[:], AF.Sqrt,
                                               scale=1.0 / FFN_H, bias=cx.bias_eps[:]),
                 deps=[em.token("dve"), cx.t_bias])
    t_rsd = em.op("dve", lambda v: v.reciprocal(rstd[:], rstd[:]), deps=[t_sq])

    cx.free(wfc_bf, w1qT, rrelu, u32, ubf, u2d, part, uT)
    em.barrier()

    t_acc = []
    for i in range(B * NQC):
        t_acc.append(em.dma("sync", acc[i][:], acc_dram[ts(i), :]))

    w2_bf = cx.sb([128, 4 * 1024], BF16, "w2_bf")
    w2qT = cx.sb([128, 8 * E], BF16, "w2qT")
    hn_rd = cx.sb([128, 8 * 128], BF16, "hn_rd")
    tmp_o = cx.sb([128, 1024], F32, "tmp_o")
    ps_w = cx.pp["w"]
    HG = 1024
    for grp in range(FFN_H // HG):
        ttok = []
        for sl in range(4):
            tl = em.dma("pool", w2_bf[:].rearrange("p (dc h) -> p dc h", h=HG),
                        AP["L0_ffn_W_proj"][ts(sl, 512), grp * HG:(grp + 1) * HG]
                        .rearrange("(dc p) h -> p dc h", p=128),
                        deps=[getattr(cx, "pdep_w2", None)])
            qt = quantize_tile(cx, w2_bf[:], gscr, 4 * HG, [tl])
            for dc in range(4):
                d_abs = sl * 4 + dc
                for hs in range(HG // 128):
                    ttok.append(transpose128(
                        cx, w2qT[:, hs * E + d_abs * 128: hs * E + (d_abs + 1) * 128],
                        w2_bf[:, dc * HG + hs * 128: dc * HG + (hs + 1) * 128], [qt]))
            cx.pdep_w2 = ttok[-1]
        for b in range(B):
            for tc in range(NQC):
                col = b * NQC + tc
                t_rd = em.dma("sync", hn_rd[:].rearrange("p (hs t) -> p hs t", t=128),
                              u_dram[grp * HG:(grp + 1) * HG, ts(col)]
                              .rearrange("(hs p) t -> p hs t", p=128),
                              deps=[getattr(cx, "pdep_hn", None)])
                for half in range(2):
                    mmlast = None
                    for hs in range(HG // 128):
                        for pc in range(2):
                            mmlast = em.op("pe", lambda t, hs=hs, pc=pc, half=half: t.matmul(
                                ps_w[:, ts(pc, 512)], hn_rd[:, ts(hs)],
                                w2qT[:, hs * E + half * 1024 + pc * 512:
                                     hs * E + half * 1024 + (pc + 1) * 512],
                                start=(hs == 0), stop=(hs == HG // 128 - 1)),
                                deps=(ttok + [t_rd, cx.pdep(ps_w), t_rsd] + t_acc)
                                if (hs == 0 and pc == 0) else ())
                    cx.pdep_hn = mmlast
                    t_sc = em.op("dve", lambda v, col=col: v.tensor_scalar(
                        tmp_o[:], ps_w[:], rstd[:, col:col + 1], None, ALU.mult),
                        deps=[mmlast])
                    cx.puse(ps_w, t_sc)
                    em.op("pool", lambda g, col=col, half=half: g.tensor_tensor(
                        acc[col][:, half * 1024:(half + 1) * 1024],
                        acc[col][:, half * 1024:(half + 1) * 1024],
                        tmp_o[:], ALU.add), deps=[t_sc])
    cx.free(w2_bf, w2qT, hn_rd, tmp_o, gscr, rstd, msacc,
            cx.qz_qa, cx.qz_ratio, cx.qz_meanb)
    em.barrier()


def make_psum_pool(cx):
    cx.pp = {
        "a": cx.ps([128, 512], F32, "pp_a"),
        "c": cx.ps([128, 512], F32, "pp_c"),
        "row": cx.ps([128, 512], F32, "pp_row"),
        "o": cx.ps([128, 512], F32, "pp_o"),
        "tr0": cx.ps([128, 1024], BF16, "pp_tr0"),
        "tr1": cx.ps([128, 1024], BF16, "pp_tr1"),
        "w": cx.ps([128, 1024], F32, "pp_w"),
    }
    cx.tr_rot = 0


def build_launch_A():
    nc = bass.Bass("TRN2", target_bir_lowering=False)
    AP = declare_l0_inputs(nc)
    out_x1 = nc.dram_tensor("x1", [TOK, E], F32, kind="ExternalOutput").ap()
    kind = "ExternalOutput" if DEBUG else "Internal"
    u_dram = nc.dram_tensor("u_scratch", [FFN_H, TOK], BF16, kind=kind).ap()
    acc_dram = nc.dram_tensor("acc_scratch", [TOK, E], F32, kind=kind).ap()
    if DEBUG:
        nc._dbg_xn = nc.dram_tensor("d_xn", [E, TOK], BF16, kind="ExternalOutput").ap()
        nc._dbg_xnw = nc.dram_tensor("d_xnw", [256, WROWS], BF16, kind="ExternalOutput").ap()
        nc._dbg_k = nc.dram_tensor("d_k", [128, WROWS], BF16, kind="ExternalOutput").ap()
        nc._dbg_q = nc.dram_tensor("d_q", [128, OWN], BF16, kind="ExternalOutput").ap()
        nc._dbg_at = nc.dram_tensor("d_at", [128, OWN], BF16, kind="ExternalOutput").ap()
    em = Em(nc)
    cx = Ctx(nc, em)
    emit_consts(cx, AP)
    make_psum_pool(cx)
    xn_own = [cx.sb([128, TOK], BF16, f"xn_own{e}") for e in range(EC)]
    emit_l0(cx, AP, xn_own, u_dram, acc_dram)
    acc = [cx.sb([128, E], F32, f"accf{t}") for t in range(B * NQC)]
    emit_l0_ffn(cx, AP, xn_own, u_dram, acc_dram, acc)
    xtile = cx.sb([128, E], F32, "xres")
    st_toks = []
    for b in range(B):
        for tc in range(NQC):
            i = b * NQC + tc
            tl = em.dma("sync", xtile[:], AP["xw"][b, WIN + tc * 128: WIN + (tc + 1) * 128, :],
                        deps=[st_toks[-1] if st_toks else None])
            ta = em.op("dve", lambda v, i=i: v.tensor_tensor(
                acc[i][:], xtile[:], acc[i][:], ALU.add), deps=[tl])
            st_toks.append(em.dma("sync", out_x1[ts(i), :], acc[i][:], deps=[ta]))
    em.ops["sync"].append(("w", "sync", em.cnt["sync"]))
    em.replay()
    return nc


def declare_B_inputs(nc):
    ap = {}
    def inp(n, sh, dt=F32):
        ap[n] = nc.dram_tensor(n, sh, dt, kind="ExternalInput").ap()
    inp("x1in", [TOK, E])
    inp("c_ident", [128, 128], BF16)
    inp("c_prot", [128, 128])
    inp("L1_attn_Wq", [E, E]); inp("L1_attn_Wk", [E, E]); inp("L1_attn_Wv", [E, E])
    inp("L1_ffn_gate_kernel", [E, 8])
    return ap


def emit_l1pre(cx, AP, outs):
    em = cx.em
    NT = TOK // 128

    xn1f = [cx.sb([128, TOK], BF16, f"xn1f{e}") for e in range(EC)]
    lof = [cx.sb([128, TOK], BF16, f"lof{e}") for e in range(EC)]
    xt = cx.sb([128, E], F32, "b_x")
    xn32 = cx.sb([128, E], F32, "b_xn32")
    scr = cx.sb([128, E], F32, "b_scr")
    hibf = cx.sb([128, E], BF16, "b_hi")
    lobf = cx.sb([128, E], BF16, "b_lo")
    stats = cx.sb([128, 8], F32, "b_stat")
    g_hi = cx.sb([128, EC * 8], BF16, "g_hi")
    g_lo = cx.sb([128, EC * 8], BF16, "g_lo")
    g_32 = cx.sb([128, EC * 8], F32, "g_32")
    glog = cx.sb([128, 8], F32, "b_glog")
    gsc = cx.sb([128, 8], F32, "b_gsc")
    gse = cx.sb([128, 8], F32, "b_gse")

    t_g = em.dma("sync", g_32[:].rearrange("p (ec g) -> p ec g", g=8),
                 AP["L1_ffn_gate_kernel"].rearrange("(ec p) g -> p ec g", p=128))
    tg1 = em.op("dve", lambda v: v.tensor_copy(g_hi[:], g_32[:]), deps=[t_g])
    tg2 = em.op("dve", lambda v: v.tensor_tensor(g_32[:], g_32[:], g_hi[:], ALU.subtract),
                deps=[tg1])
    tg3 = em.op("dve", lambda v: v.tensor_copy(g_lo[:], g_32[:]), deps=[tg2])

    ps_g = cx.pp["a"]
    prev = None
    for t in range(NT):
        tl = em.dma("sync", xt[:], AP["x1in"][ts(t), :], deps=[prev])
        ln = layernorm_tile(cx, xt, xn32, scr, stats, [tl])
        c1 = em.op("dve", lambda v: v.tensor_copy(hibf[:], xn32[:]), deps=[ln])
        c2 = em.op("dve", lambda v: v.tensor_tensor(scr[:], xn32[:], hibf[:], ALU.subtract),
                   deps=[c1])
        c3 = em.op("dve", lambda v: v.tensor_copy(lobf[:], scr[:]), deps=[c2])
        last = None
        for e in range(EC):
            transpose128(cx, xn1f[e][:, ts(t)], hibf[:, ts(e)], [c1])
            last = transpose128(cx, lof[e][:, ts(t)], lobf[:, ts(e)], [c3])
        prev = last
        mml = None
        combos = [(e, src_t, gw) for e in range(EC)
                  for (src_t, gw) in ((xn1f, g_hi), (xn1f, g_lo), (lof, g_hi))]
        for idx, (e, src_t, gw) in enumerate(combos):
            mml = em.op("pe", lambda tt_, e=e, t=t, src_t=src_t, gw=gw, idx=idx: tt_.matmul(
                ps_g[:, 0:8], src_t[e][:, ts(t)], gw[:, e * 8:(e + 1) * 8],
                start=(idx == 0), stop=(idx == len(combos) - 1)),
                deps=[last, tg3, cx.pdep(ps_g)] if idx == 0 else ())
        tge = em.op("dve", lambda v: v.tensor_copy(glog[:], ps_g[:, 0:8]), deps=[mml])
        cx.puse(ps_g, tge)
        m1 = stats[:, 0:1]; m2 = stats[:, 1:2]; w1 = stats[:, 2:3]
        w2 = stats[:, 3:4]; dcol = stats[:, 4:5]
        g1 = em.op("dve", lambda v: v.tensor_reduce(m1, glog[:], AX.X, ALU.max), deps=[tge])
        g2 = em.op("dve", lambda v: v.tensor_scalar(gsc[:], glog[:], m1, None, ALU.is_ge),
                   deps=[g1])
        g3 = em.op("dve", lambda v: v.tensor_scalar(gse[:], gsc[:], -1e30, None, ALU.mult),
                   deps=[g2])
        g4 = em.op("dve", lambda v: v.tensor_tensor(gse[:], glog[:], gse[:], ALU.add),
                   deps=[g3])
        g5 = em.op("dve", lambda v: v.tensor_reduce(m2, gse[:], AX.X, ALU.max), deps=[g4])
        g6 = em.op("dve", lambda v: v.tensor_tensor(dcol, m1, m2, ALU.subtract), deps=[g5])
        g7 = em.op("act", lambda a: a.activation(w1, dcol, AF.Sigmoid), deps=[g6])
        g8 = em.op("dve", lambda v: v.tensor_scalar(w2, w1, -1.0, 1.0, ALU.mult, ALU.add),
                   deps=[g7])
        g9 = em.op("dve", lambda v: v.tensor_scalar(gsc[:], gsc[:], w1, None, ALU.mult),
                   deps=[g8])
        gA = em.op("dve", lambda v: v.tensor_scalar(gse[:], gse[:], m2, None, ALU.is_ge),
                   deps=[g9])
        gB = em.op("dve", lambda v: v.tensor_scalar(gse[:], gse[:], w2, None, ALU.mult),
                   deps=[gA])
        gC = em.op("dve", lambda v: v.tensor_tensor(gse[:], gse[:], gsc[:], ALU.add),
                   deps=[gB])
        em.dma("pool", outs["gsel"][ts(t), :], gse[:], deps=[gC])
    cx.free(g_hi, g_lo, g_32, glog, gsc, gse, xt, xn32, scr, hibf, lobf, stats)
    cx.free(*lof)

    wrow = cx.sb([128, E], BF16, "b_wrow")
    wTblk = cx.sb([128, EC * 128], BF16, "b_wT")
    vstage = cx.sb([128, TOK], BF16, "b_vstage")
    v1t = [cx.sb([128, E], BF16, f"b_v1t{t}") for t in range(TOK // 128)]
    ps_w = cx.pp["w"]
    t_prev_w = None
    for wname, oname, scale in (("L1_attn_Wq", "q1", 0.125),
                                ("L1_attn_Wk", "k1", 1.0),
                                ("L1_attn_Wv", None, 1.0)):
        for ic in range(EC):
            tl = em.dma("pool", wrow[:], AP[wname][ts(ic), :],
                        deps=[t_prev_w, em.token("pe")])
            ttok = []
            for e in range(EC):
                ttok.append(transpose128(cx, wTblk[:, ts(e)], wrow[:, ts(e)], [tl]))
            t_prev_w = ttok[-1]
            mml = None
            for e in range(EC):
                for pc in range(2):
                    mml = em.op("pe", lambda t_, e=e, pc=pc: t_.matmul(
                        ps_w[:, ts(pc, 512)], wTblk[:, ts(e)],
                        xn1f[e][:, ts(pc, 512)], start=(e == 0), stop=(e == EC - 1)),
                        deps=(ttok + [cx.pdep(ps_w)]) if (e == 0 and pc == 0) else ())
            tev = em.op("act", lambda a, scale=scale: a.activation(
                vstage[:], ps_w[:], AF.Copy, scale=scale),
                deps=[mml, getattr(cx, "t_spill_prev", None)])
            cx.puse(ps_w, tev)
            if oname is not None:
                cx.t_spill_prev = em.dma("pool", outs[oname][ts(ic), :], vstage[:], deps=[tev])
                t_prev_w = tev
            else:
                vt = []
                for t in range(TOK // 128):
                    vt.append(transpose128(cx, v1t[t][:, ts(ic)], vstage[:, ts(t)], [tev]))
                t_prev_w = vt[-1]
    for t in range(TOK // 128):
        em.dma("pool", outs["v1"][ts(t), :], v1t[t][:], deps=[em.token("dve")])
    for e in range(EC):
        em.dma("pool", outs["xn1f"][ts(e), :], xn1f[e][:], deps=[em.token("dve")])
    em.barrier()


def build_launch_B():
    nc = bass.Bass("TRN2", target_bir_lowering=False)
    AP = declare_B_inputs(nc)
    outs = {
        "gsel": nc.dram_tensor("gsel", [TOK, 8], F32, kind="ExternalOutput").ap(),
        "q1": nc.dram_tensor("q1", [E, TOK], BF16, kind="ExternalOutput").ap(),
        "k1": nc.dram_tensor("k1", [E, TOK], BF16, kind="ExternalOutput").ap(),
        "v1": nc.dram_tensor("v1", [TOK, E], BF16, kind="ExternalOutput").ap(),
        "xn1f": nc.dram_tensor("xn1f", [E, TOK], BF16, kind="ExternalOutput").ap(),
    }
    em = Em(nc)
    cx = Ctx(nc, em)
    emit_consts(cx, AP)
    make_psum_pool(cx)
    emit_l1pre(cx, AP, outs)
    em.ops["sync"].append(("w", "pool", em.cnt["pool"]))
    em.ops["sync"].append(("w", "sync", em.cnt["sync"]))
    em.replay()
    return nc


def declare_C_inputs(nc):
    ap = {}
    def inp(n, sh, dt=F32):
        ap[n] = nc.dram_tensor(n, sh, dt, kind="ExternalInput").ap()
    inp("x1in", [TOK, E])
    inp("c_ident", [128, 128], BF16)
    inp("c_prot", [128, 128])
    inp("xn1f", [E, TOK], BF16)
    inp("gselT", [8, TOK])
    inp("q1", [E, TOK], BF16)
    inp("k_all", [E, B * S], BF16)
    inp("v_all", [B * S, E], BF16)
    inp("L1_attn_Wo", [E, E])
    inp("lam_q1", [64, 1]); inp("lam_k1", [64, 1])
    inp("lam_q2", [64, 1]); inp("lam_k2", [64, 1])
    inp("subln", [128, 1])
    inp("L1_ffn_wi_0", [MOE_E, E, E]); inp("L1_ffn_wi_1", [MOE_E, E, E])
    inp("L1_ffn_wo", [MOE_E, E, E])
    return ap


def emit_lambda(cx, AP):
    em = cx.em
    cx.neglam_col = cx.sb([128, 1], F32, "neglam")
    cx.slw_col = cx.sb([128, 1], F32, "slwcol")
    lv = cx.sb([64, 4], F32, "lamv")
    l11 = cx.sb([1, 4], F32, "lam_sc")
    subc = cx.sb([128, 1], F32, "subc")
    tloads = [em.dma("sync", lv[:, i:i + 1], AP[n])
              for i, n in enumerate(("lam_q1", "lam_k1", "lam_q2", "lam_k2"))]
    t_su = em.dma("sync", subc[:], AP["subln"])
    pr = cx.pp["row"]
    t1 = em.op("pe", lambda t: t.matmul(pr[0:1, 0:1], lv[:, 0:1], lv[:, 1:2],
                                        start=True, stop=True),
               deps=tloads + [cx.pdep(pr)])
    e1 = em.op("act", lambda a: a.activation(l11[:, 0:1], pr[0:1, 0:1], AF.Exp), deps=[t1])
    cx.puse(pr, e1)
    t2 = em.op("pe", lambda t: t.matmul(pr[0:1, 0:1], lv[:, 2:3], lv[:, 3:4],
                                        start=True, stop=True),
               deps=[e1, cx.pdep(pr)])
    e2 = em.op("act", lambda a: a.activation(l11[:, 1:2], pr[0:1, 0:1], AF.Exp), deps=[t2])
    cx.puse(pr, e2)
    d1 = em.op("dve", lambda v: v.tensor_tensor(l11[:, 2:3], l11[:, 1:2], l11[:, 0:1],
                                                ALU.subtract), deps=[e1, e2])
    d2 = em.op("dve", lambda v: v.tensor_scalar(l11[:, 2:3], l11[:, 2:3],
                                                LAMBDA_INIT, None, ALU.subtract),
               deps=[d1])
    t3 = em.op("pe", lambda t: t.matmul(cx.pp["c"][:, 0:1], cx.ones_rf[:], l11[:, 2:3],
                                        start=True, stop=True),
               deps=[d2, cx.t_ones, cx.pdep(cx.pp["c"])])
    d3 = em.op("dve", lambda v: v.tensor_copy(cx.neglam_col[:], cx.pp["c"][:, 0:1]),
               deps=[t3])
    cx.puse(cx.pp["c"], d3)
    cx.t_lam = d3
    cx.t_slw = em.op("dve", lambda v: v.tensor_scalar(
        cx.slw_col[:], subc[:], 1.0 - LAMBDA_INIT, None, ALU.mult), deps=[t_su])
    cx.free(lv, l11, subc)


MOE_PARANOID = False


def emit_moe(cx, AP, x2acc, xn1f_src, t_xn, grow_fn=None, t_w0=None):
    em = cx.em
    NT = TOK // 128
    grow = cx.sb([1, TOK], F32, "c_grow")
    gbc = cx.sb([128, TOK], F32, "c_gbc")
    wiblk = [cx.sb([128, EC * 128], BF16, f"c_wi{i}") for i in range(2)]
    silu_sb = cx.sb([128, 512], BF16, "c_silu")
    hmul = cx.sb([128, TOK], BF16, "c_hmul")
    h_fm = [cx.sb([128, TOK], BF16, f"c_hfm{m}") for m in range(EC)]
    woblk = [cx.sb([128, 512], BF16, f"c_wo{m}") for m in range(EC)]
    pA, pC = cx.pp["a"], cx.pp["c"]
    pO, pR = cx.pp["o"], cx.pp["row"]
    ps_w = cx.pp["w"]
    t_wi_prev = [t_w0, t_w0]
    t_wo_prev = [t_w0] * EC
    for ex in range(MOE_E):
        if grow_fn is None:
            tg = em.dma("sync", grow[:], AP["gselT"][ex:ex + 1, :],
                        deps=[getattr(cx, "t_gbc", None)])
        else:
            tg = grow_fn(ex, grow, [getattr(cx, "t_gbc", None)])
        btoks = []
        for pc in range(2):
            tb = em.op("pe", lambda t, pc=pc: t.matmul(
                pA[:], cx.ones_rf[:], grow[:, ts(pc, 512)],
                start=True, stop=True), deps=[tg, cx.t_ones, cx.pdep(pA)])
            tc = em.op("dve", lambda v, pc=pc: v.tensor_copy(
                gbc[:, ts(pc, 512)], pA[:]), deps=[tb])
            cx.puse(pA, tc)
            btoks.append(tc)
        cx.t_gbc = btoks[-1]
        for m in range(EC):
            t0 = em.dma("pool", wiblk[0][:].rearrange("p (dc m) -> p dc m", m=128),
                        AP["L1_ffn_wi_0"][ex, :, ts(m)].rearrange("(dc p) m -> p dc m", p=128),
                        deps=[t_wi_prev[0]])
            t1 = em.dma("pool", wiblk[1][:].rearrange("p (dc m) -> p dc m", m=128),
                        AP["L1_ffn_wi_1"][ex, :, ts(m)].rearrange("(dc p) m -> p dc m", p=128),
                        deps=[t_wi_prev[1]])
            mml0 = [None, None]
            mml1 = [None, None]
            for pc in range(2):
                for e in range(EC):
                    mml0[pc] = em.op("pe", lambda t, e=e, pc=pc: t.matmul(
                        [pA, pC][pc][:], wiblk[0][:, ts(e)],
                        xn1f_src[e][:, ts(pc, 512)], start=(e == 0), stop=(e == EC - 1)),
                        deps=([t0, t1, t_xn, cx.pdep([pA, pC][pc])] + btoks) if e == 0 else ())
                for e in range(EC):
                    mml1[pc] = em.op("pe", lambda t, e=e, pc=pc: t.matmul(
                        [pO, pR][pc][:], wiblk[1][:, ts(e)],
                        xn1f_src[e][:, ts(pc, 512)], start=(e == 0), stop=(e == EC - 1)),
                        deps=[t1, cx.pdep([pO, pR][pc])] if e == 0 else ())
            t_wi_prev = [mml0[1], mml1[1]]
            if MOE_PARANOID:
                em.barrier()
            s2 = None
            for pc in range(2):
                s1 = em.op("act", lambda a, pc=pc: a.activation(
                    silu_sb[:], [pA, pC][pc][:], AF.Silu), deps=[mml0[pc], s2])
                cx.puse([pA, pC][pc], s1)
                s2 = em.op("dve", lambda v, pc=pc: v.tensor_tensor(
                    hmul[:, ts(pc, 512)], silu_sb[:], [pO, pR][pc][:], ALU.mult),
                    deps=[s1, mml1[pc]])
                cx.puse([pO, pR][pc], s2)
                em.op("dve", lambda v, m=m, pc=pc: v.tensor_tensor(
                    h_fm[m][:, ts(pc, 512)], hmul[:, ts(pc, 512)], gbc[:, ts(pc, 512)],
                    ALU.mult), deps=[s2, getattr(cx, "t_moe_out_prev", None)] + btoks)
        if MOE_PARANOID:
            em.barrier()
        t_h = em.token("dve")
        half_last = [cx.pdep(ps_w), cx.pdep(ps_w)]
        for quar in range(4):
            wo_toks = []
            for m in range(EC):
                wo_toks.append(em.dma("pool", woblk[m][:],
                                      AP["L1_ffn_wo"][ex, ts(m), quar * 512:(quar + 1) * 512],
                                      deps=[t_wo_prev[m]]))
            for tc in range(NT):
                hw = (tc % 2) * 512
                mml = None
                for m in range(EC):
                    mml = em.op("pe", lambda t, m=m, tc=tc, hw=hw: t.matmul(
                        ps_w[:, hw:hw + 512], h_fm[m][:, ts(tc)],
                        woblk[m][:], start=(m == 0), stop=(m == EC - 1)),
                        deps=(wo_toks + [t_h, half_last[tc % 2]]) if m == 0 else ())
                tadd = em.op("dve", lambda v, tc=tc, quar=quar, hw=hw: v.tensor_tensor(
                    x2acc[tc][:, quar * 512:(quar + 1) * 512],
                    x2acc[tc][:, quar * 512:(quar + 1) * 512],
                    ps_w[:, hw:hw + 512], ALU.add), deps=[mml])
                half_last[tc % 2] = tadd
            for m in range(EC):
                t_wo_prev[m] = em.token("pe")
        cx.puse(ps_w, max(half_last, key=lambda t: t[1]))
        cx.t_moe_out_prev = em.token("pe")
        if MOE_PARANOID:
            em.barrier()
        if hasattr(cx, "dbg_hfm") and ex == MOE_E - 1:
            for m in range(EC):
                em.dma("pool", cx.dbg_hfm[ts(m), :], h_fm[m][:], deps=[em.token("dve")])
            em.dma("pool", cx.dbg_gbc, gbc[:], deps=[em.token("dve")])
            em.barrier()
    cx.free(grow, gbc, wiblk[0], wiblk[1], silu_sb, hmul, *(h_fm + woblk))
    em.barrier()


def emit_l1attn(cx, x2acc, q1_src, k_src, v_src, t_qkv):
    em = cx.em
    attnf = [cx.sb([128, TOK], BF16, f"c_attn{h}") for h in range(L1_H)]
    khb = cx.sb([128, S], BF16, "c_khb")
    vhb = cx.sb([128, S], BF16, "c_vhb")
    ksq = cx.sb([128, S], BF16, "c_ksq")
    srow = cx.sb([1, 512], F32, "c_srow")
    bnd = cx.sb([1, 8], F32, "c_bnd")
    bndc = cx.sb([128, 1], F32, "c_bndc")
    pT = cx.sb([128, 512], BF16, "c_pT")
    rl = cx.sb([1, 512], F32, "c_rl")
    of = cx.sb([128, 512], F32, "c_of")
    o1n = cx.sb([128, 512], F32, "c_o1n")
    o2n = cx.sb([128, 512], F32, "c_o2n")
    apre = cx.sb([128, 512], F32, "c_apre")
    asq = cx.sb([128, 512], BF16, "c_asq")
    pA, pC, pO, pR = cx.pp["a"], cx.pp["c"], cx.pp["o"], cx.pp["row"]
    NKC1 = S // 128
    t_kv_prev = None
    for h in range(L1_H):
        for b in range(B):
            tk = k_src(h, b, khb, [t_kv_prev])
            tv = v_src(h, b, vhb, [t_kv_prev])
            tq2 = em.op("act", lambda a, h=h, b=b: a.activation(
                ksq[:, 0:512], q1_src[h][:, b * OWN:(b + 1) * OWN], AF.Square),
                deps=[t_qkv, t_kv_prev])
            qb_toks = []
            for half in range(2):
                hp = slice(half * 64, (half + 1) * 64)
                tm1 = em.op("pe", lambda t, hp=hp: t.matmul(
                    pR[0:1, 0:512], cx.ones_cb[hp, :], ksq[hp, 0:512],
                    start=True, stop=True), deps=[tq2, cx.t_ones, cx.pdep(pR)])
                tr1 = em.op("dve", lambda v, half=half: v.tensor_reduce(
                    bnd[:, half:half + 1], pR[0:1, 0:512], AX.X, ALU.max), deps=[tm1])
                cx.puse(pR, tr1)
                qb_toks.append(tr1)
            tk2 = em.op("act", lambda a: a.activation(ksq[:], khb[:], AF.Square),
                        deps=[tk] + qb_toks)
            kb_toks = []
            for half in range(2):
                hp = slice(half * 64, (half + 1) * 64)
                mmk = None
                for pc in range(8):
                    mmk = em.op("pe", lambda t, hp=hp, pc=pc: t.matmul(
                        pR[0:1, 0:512], cx.ones_cb[hp, :], ksq[hp, ts(pc, 512)],
                        start=(pc == 0), stop=(pc == 7)),
                        deps=[tk2, cx.t_ones, cx.pdep(pR)] if pc == 0 else ())
                trk = em.op("dve", lambda v, half=half: v.tensor_reduce(
                    bnd[:, 2 + half:3 + half], pR[0:1, 0:512], AX.X, ALU.max), deps=[mmk])
                cx.puse(pR, trk)
                kb_toks.append(trk)
            sq1 = em.op("act", lambda a: a.activation(bnd[:, 4:6], bnd[:, 0:2], AF.Sqrt),
                        deps=qb_toks + kb_toks)
            sq2 = em.op("act", lambda a: a.activation(bnd[:, 6:8], bnd[:, 2:4], AF.Sqrt),
                        deps=[sq1])
            bb1 = em.op("dve", lambda v: v.tensor_tensor(bnd[:, 4:6], bnd[:, 4:6],
                                                         bnd[:, 6:8], ALU.mult), deps=[sq2])
            bb2 = em.op("dve", lambda v: v.tensor_scalar(bnd[:, 4:6], bnd[:, 4:6],
                                                         -1.02, 70.0, ALU.mult, ALU.add),
                        deps=[bb1])
            if DEBUG and hasattr(cx, "dbg_bnd"):
                em.dma("pool", cx.dbg_bnd[h * B + b: h * B + b + 1, :], bnd[:], deps=[bb2])
            ton = None
            for half in range(2):
                hp = slice(half * 64, (half + 1) * 64)
                tb = em.op("pe", lambda t, half=half: t.matmul(
                    pC[:, 0:1], cx.ones_rf[:], bnd[0:1, 4 + half:5 + half],
                    start=True, stop=True), deps=[bb2, cx.t_ones, cx.pdep(pC)])
                tbc = em.op("dve", lambda v: v.tensor_copy(bndc[:], pC[:, 0:1]), deps=[tb])
                cx.puse(pC, tbc)
                tl = tvm = None
                texp_prev = None
                for kc in range(NKC1):
                    tmm = em.op("pe", lambda t, kc=kc, hp=hp, h=h, b=b: t.matmul(
                        pA[:], khb[hp, ts(kc)], q1_src[h][hp, b * OWN:(b + 1) * OWN],
                        start=True, stop=True), deps=[tk, t_qkv, cx.pdep(pA)])
                    tex = em.op("act", lambda a: a.activation(
                        pT[:], pA[:], AF.Exp, bias=bndc[:]), deps=[tmm, tbc, texp_prev])
                    cx.puse(pA, tex)
                    tl = em.op("pe", lambda t, kc=kc: t.matmul(
                        pR[0:1, 0:512], cx.ones_cb[:], pT[:],
                        start=(kc == 0), stop=(kc == NKC1 - 1)),
                        deps=[tex] + ([cx.pdep(pR)] if kc == 0 else []))
                    tvm = em.op("pe", lambda t, kc=kc: t.matmul(
                        pO[:], vhb[:, ts(kc)], pT[:],
                        start=(kc == 0), stop=(kc == NKC1 - 1)),
                        deps=[tv] + ([cx.pdep(pO)] if kc == 0 else []))
                    texp_prev = tvm
                trc = em.op("dve", lambda v: v.reciprocal(rl[:], pR[0:1, 0:512]),
                            deps=[tl, tvm])
                cx.puse(pR, trc)
                if DEBUG and hasattr(cx, "dbg_rl"):
                    em.dma("pool", cx.dbg_rl[(h * B + b) * 2 + half: (h * B + b) * 2 + half + 1, :],
                           rl[:], deps=[trc])
                tb2 = em.op("pe", lambda t: t.matmul(pC[:], cx.ones_rf[:], rl[:],
                                                     start=True, stop=True),
                            deps=[trc, cx.pdep(pC)])
                tof = em.op("dve", lambda v: v.tensor_copy(of[:], pO[:]), deps=[tvm, tb2])
                cx.puse(pO, tof)
                dst = o1n if half == 0 else o2n
                ton = em.op("dve", lambda v, dst=dst: v.tensor_tensor(
                    dst[:], of[:], pC[:], ALU.mult), deps=[tof])
                cx.puse(pC, ton)
            a1 = em.op("dve", lambda v: v.tensor_scalar(
                apre[:], o2n[:], cx.neglam_col[:], None, ALU.mult),
                deps=[ton, cx.t_lam])
            a2 = em.op("dve", lambda v: v.tensor_tensor(apre[:], apre[:], o1n[:], ALU.add),
                       deps=[a1])
            a3 = em.op("act", lambda a: a.activation(asq[:], apre[:], AF.Square), deps=[a2])
            a4 = em.op("pe", lambda t: t.matmul(pR[0:1, 0:512], cx.ones_cb[:], asq[:],
                                                start=True, stop=True),
                       deps=[a3, cx.pdep(pR)])
            a5 = em.op("act", lambda a: a.activation(srow[:], pR[0:1, 0:512], AF.Sqrt,
                                                     scale=1.0 / 128.0, bias=cx.bias_ln[0:1, :]),
                       deps=[a4, cx.t_bias])
            cx.puse(pR, a5)
            a6 = em.op("dve", lambda v: v.reciprocal(srow[:], srow[:]), deps=[a5])
            a7 = em.op("pe", lambda t: t.matmul(pC[:], cx.ones_rf[:], srow[:],
                                                start=True, stop=True),
                       deps=[a6, cx.pdep(pC)])
            a8 = em.op("dve", lambda v: v.tensor_tensor(apre[:], apre[:], pC[:], ALU.mult),
                       deps=[a7])
            cx.puse(pC, a8)
            a9 = em.op("act", lambda a, h=h, b=b: a.activation(
                attnf[h][:, b * OWN:(b + 1) * OWN], apre[:], AF.Copy, scale=cx.slw_col[:]),
                deps=[a8, cx.t_slw])
            t_kv_prev = a9
    cx.free(khb, vhb, ksq, srow, bnd, bndc, pT, rl, of, o1n, o2n, apre, asq)
    return attnf


def emit_l1wo(cx, AP, x2acc, attnf):
    em = cx.em
    NT = TOK // 128
    woL = cx.sb([128, 512], BF16, "c_woL")
    woT = cx.sb([128, 4 * E], BF16, "c_woT")
    ps_w = cx.pp["w"]
    t_attn = em.token("act")
    t_prev = None
    for g in range(4):
        ttok = []
        for jr in range(EC):
            tl = em.dma("pool", woL[:], AP["L1_attn_Wo"][ts(jr), ts(g, 512)],
                        deps=[t_prev])
            for isub in range(4):
                ttok.append(transpose128(
                    cx, woT[:, isub * E + jr * 128: isub * E + (jr + 1) * 128],
                    woL[:, ts(isub)], [tl]))
            t_prev = ttok[-1]
        for tc in range(NT):
            for half in range(2):
                mml = None
                for isub in range(4):
                    ic = g * 4 + isub
                    for pc in range(2):
                        mml = em.op("pe", lambda t, ic=ic, isub=isub, tc=tc, pc=pc, half=half: t.matmul(
                            ps_w[:, ts(pc, 512)], attnf[ic][:, ts(tc)],
                            woT[:, isub * E + half * 1024 + pc * 512:
                                isub * E + half * 1024 + (pc + 1) * 512],
                            start=(isub == 0), stop=(isub == 3)),
                            deps=(ttok + [t_attn, cx.pdep(ps_w)]) if (isub == 0 and pc == 0) else ())
                tadd = em.op("dve", lambda v, tc=tc, half=half: v.tensor_tensor(
                    x2acc[tc][:, half * 1024:(half + 1) * 1024],
                    x2acc[tc][:, half * 1024:(half + 1) * 1024],
                    ps_w[:], ALU.add), deps=[mml])
                cx.puse(ps_w, tadd)
    cx.free(woL, woT)


def build_launch_C():
    nc = bass.Bass("TRN2", target_bir_lowering=False)
    AP = declare_C_inputs(nc)
    out = nc.dram_tensor("x2", [TOK, E], F32, kind="ExternalOutput").ap()
    em = Em(nc)
    cx = Ctx(nc, em)
    emit_consts(cx, AP)
    make_psum_pool(cx)
    NT = TOK // 128
    x2acc = [cx.sb([128, E], F32, f"x2acc{t}") for t in range(NT)]
    for t in range(NT):
        em.dma("sync", x2acc[t][:], AP["x1in"][ts(t), :])
    emit_lambda(cx, AP)
    xn1f = [cx.sb([128, TOK], BF16, f"c_xn1f{e}") for e in range(EC)]
    for e in range(EC):
        em.dma("sync", xn1f[e][:], AP["xn1f"][ts(e), :])
    t_xn = em.token("sync")
    emit_moe(cx, AP, x2acc, xn1f, t_xn)
    if DEBUG:
        dm = nc.dram_tensor("d_moe", [TOK, E], F32, kind="ExternalOutput").ap()
        for t in range(NT):
            em.dma("pool", dm[ts(t), :], x2acc[t][:], deps=[em.token("dve")])
        em.barrier()
    cx.free(*xn1f)
    q1f = [cx.sb([128, TOK], BF16, f"c_q1f{h}") for h in range(L1_H)]
    for h in range(L1_H):
        em.dma("sync", q1f[h][:], AP["q1"][ts(h), :])
    t_q = em.token("sync")

    def k_src(h, b, dst, deps):
        return cx.em.dma("sync", dst[:], AP["k_all"][ts(h), b * S:(b + 1) * S],
                         deps=deps)

    def v_src(h, b, dst, deps):
        return cx.em.dma("sync", dst[:].rearrange("p (kc d) -> p kc d", d=128),
                         AP["v_all"][b * S:(b + 1) * S, ts(h)]
                         .rearrange("(kc p) d -> p kc d", p=128), deps=deps)

    if DEBUG:
        cx.dbg_rl = nc.dram_tensor("d_rl", [64, 512], F32, kind="ExternalOutput").ap()
        cx.dbg_bnd = nc.dram_tensor("d_bnd", [64, 8], F32, kind="ExternalOutput").ap()
    attnf = emit_l1attn(cx, x2acc, q1f, k_src, v_src, t_q)
    if DEBUG:
        da = nc.dram_tensor("d_attnf", [E, TOK], BF16, kind="ExternalOutput").ap()
        for h in range(L1_H):
            em.dma("pool", da[ts(h), :], attnf[h][:], deps=[em.token("act")])
        em.barrier()
    emit_l1wo(cx, AP, x2acc, attnf)
    em.barrier()
    for t in range(NT):
        em.dma("sync", out[ts(t), :], x2acc[t][:], deps=[em.token("dve")])
    em.ops["sync"].append(("w", "sync", em.cnt["sync"]))
    em.replay()
    return nc



def declare_fused_inputs(nc):
    ap = declare_l0_inputs(nc)
    def inp(n, sh, dt=F32):
        ap[n] = nc.dram_tensor(n, sh, dt, kind="ExternalInput").ap()
    inp("L1_attn_Wq", [E, E]); inp("L1_attn_Wk", [E, E]); inp("L1_attn_Wv", [E, E])
    inp("L1_ffn_gate_kernel", [E, 8])
    inp("L1_attn_Wo", [E, E])
    inp("lam_q1", [64, 1]); inp("lam_k1", [64, 1])
    inp("lam_q2", [64, 1]); inp("lam_k2", [64, 1])
    inp("subln", [128, 1])
    inp("L1_ffn_wi_0", [MOE_E, E, E]); inp("L1_ffn_wi_1", [MOE_E, E, E])
    inp("L1_ffn_wo", [MOE_E, E, E])
    return ap


def emit_l1pre_fused(cx, AP, x1tiles, xn1f, gselT_sb, qin, kin, vin):
    em = cx.em
    NT = TOK // 128

    lof = [cx.sb([128, TOK], BF16, f"f_lof{e}") for e in range(EC)]
    xn32 = cx.sb([128, E], F32, "f_xn32")
    scr = cx.sb([128, E], F32, "f_scr")
    hibf = cx.sb([128, E], BF16, "f_hi")
    lobf = cx.sb([128, E], BF16, "f_lo")
    stats = cx.sb([128, 8], F32, "f_stat")
    g_hi = cx.sb([128, EC * 8], BF16, "f_ghi")
    g_lo = cx.sb([128, EC * 8], BF16, "f_glo")
    g_32 = cx.sb([128, EC * 8], F32, "f_g32")
    glog = cx.sb([128, 8], F32, "f_glog")
    gsc = cx.sb([128, 8], F32, "f_gsc")
    gse = cx.sb([128, 8], F32, "f_gse")
    ident32 = cx.sb([128, 128], F32, "f_id32")

    t_id32 = em.op("dve", lambda v: v.tensor_copy(ident32[:], cx.ident[:]),
                   deps=[cx.t_ident])
    t_g = em.dma("sync", g_32[:].rearrange("p (ec g) -> p ec g", g=8),
                 AP["L1_ffn_gate_kernel"].rearrange("(ec p) g -> p ec g", p=128))
    tg1 = em.op("dve", lambda v: v.tensor_copy(g_hi[:], g_32[:]), deps=[t_g])
    tg2 = em.op("dve", lambda v: v.tensor_tensor(g_32[:], g_32[:], g_hi[:], ALU.subtract),
                deps=[tg1])
    tg3 = em.op("dve", lambda v: v.tensor_copy(g_lo[:], g_32[:]), deps=[tg2])

    ps_g = cx.pp["a"]
    for t in range(NT):
        ln = layernorm_tile(cx, x1tiles[t], xn32, scr, stats, [cx.t_x1])
        c1 = em.op("dve", lambda v: v.tensor_copy(hibf[:], xn32[:]), deps=[ln])
        c2 = em.op("dve", lambda v: v.tensor_tensor(scr[:], xn32[:], hibf[:], ALU.subtract),
                   deps=[c1])
        c3 = em.op("dve", lambda v: v.tensor_copy(lobf[:], scr[:]), deps=[c2])
        last = None
        for e in range(EC):
            transpose128(cx, xn1f[e][:, ts(t)], hibf[:, ts(e)], [c1])
            last = transpose128(cx, lof[e][:, ts(t)], lobf[:, ts(e)], [c3])
        mml = None
        combos = [(e, src_t, gw) for e in range(EC)
                  for (src_t, gw) in ((xn1f, g_hi), (xn1f, g_lo), (lof, g_hi))]
        for idx, (e, src_t, gw) in enumerate(combos):
            mml = em.op("pe", lambda tt_, e=e, t=t, src_t=src_t, gw=gw, idx=idx: tt_.matmul(
                ps_g[:, 0:8], src_t[e][:, ts(t)], gw[:, e * 8:(e + 1) * 8],
                start=(idx == 0), stop=(idx == len(combos) - 1)),
                deps=[last, tg3, cx.pdep(ps_g)] if idx == 0 else ())
        tge = em.op("dve", lambda v: v.tensor_copy(glog[:], ps_g[:, 0:8]), deps=[mml])
        cx.puse(ps_g, tge)
        m1 = stats[:, 0:1]; m2 = stats[:, 1:2]; w1 = stats[:, 2:3]
        w2 = stats[:, 3:4]; dcol = stats[:, 4:5]
        g1 = em.op("dve", lambda v: v.tensor_reduce(m1, glog[:], AX.X, ALU.max), deps=[tge])
        g2 = em.op("dve", lambda v: v.tensor_scalar(gsc[:], glog[:], m1, None, ALU.is_ge),
                   deps=[g1])
        g3 = em.op("dve", lambda v: v.tensor_scalar(gse[:], gsc[:], -1e30, None, ALU.mult),
                   deps=[g2])
        g4 = em.op("dve", lambda v: v.tensor_tensor(gse[:], glog[:], gse[:], ALU.add),
                   deps=[g3])
        g5 = em.op("dve", lambda v: v.tensor_reduce(m2, gse[:], AX.X, ALU.max), deps=[g4])
        g6 = em.op("dve", lambda v: v.tensor_tensor(dcol, m1, m2, ALU.subtract), deps=[g5])
        g7 = em.op("act", lambda a: a.activation(w1, dcol, AF.Sigmoid), deps=[g6])
        g8 = em.op("dve", lambda v: v.tensor_scalar(w2, w1, -1.0, 1.0, ALU.mult, ALU.add),
                   deps=[g7])
        g9 = em.op("dve", lambda v: v.tensor_scalar(gsc[:], gsc[:], w1, None, ALU.mult),
                   deps=[g8])
        gA = em.op("dve", lambda v: v.tensor_scalar(gse[:], gse[:], m2, None, ALU.is_ge),
                   deps=[g9])
        gB = em.op("dve", lambda v: v.tensor_scalar(gse[:], gse[:], w2, None, ALU.mult),
                   deps=[gA])
        gC = em.op("dve", lambda v: v.tensor_tensor(gse[:], gse[:], gsc[:], ALU.add),
                   deps=[gB])
        tt = em.op("pe", lambda t_: t_.transpose(cx.pp["c"][0:8, 0:128], gse[:], ident32[:]),
                   deps=[gC, t_id32, cx.pdep(cx.pp["c"])])
        te = em.op("dve", lambda v, t=t: v.tensor_copy(gselT_sb[:, ts(t)],
                                                       cx.pp["c"][0:8, 0:128]), deps=[tt])
        cx.puse(cx.pp["c"], te)
    cx.free(*(lof + [xn32, scr, hibf, lobf, stats, g_hi, g_lo, g_32,
                     glog, gsc, gse, ident32]))

    wrow = cx.sb([128, E], BF16, "f_wrow")
    wTblk = cx.sb([128, EC * 128], BF16, "f_wT")
    vstage = cx.sb([128, TOK], BF16, "f_vstage")
    v1t = [cx.sb([128, E], BF16, f"f_v1t{t}") for t in range(NT)]
    ps_w = cx.pp["w"]
    t_prev_w = None
    kv_toks = []
    for wname, kind, scale in (("L1_attn_Wq", "q", 0.125),
                               ("L1_attn_Wk", "k", 1.0),
                               ("L1_attn_Wv", "v", 1.0)):
        dst_dram = {"q": qin, "k": kin, "v": None}[kind]
        for ic in range(EC):
            tl = em.dma("pool", wrow[:], AP[wname][ts(ic), :],
                        deps=[t_prev_w, em.token("pe")])
            ttok = []
            for e in range(EC):
                ttok.append(transpose128(cx, wTblk[:, ts(e)], wrow[:, ts(e)], [tl]))
            t_prev_w = ttok[-1]
            mml = None
            for e in range(EC):
                for pc in range(2):
                    mml = em.op("pe", lambda t_, e=e, pc=pc: t_.matmul(
                        ps_w[:, ts(pc, 512)], wTblk[:, ts(e)],
                        xn1f[e][:, ts(pc, 512)], start=(e == 0), stop=(e == EC - 1)),
                        deps=(ttok + [cx.pdep(ps_w)]) if (e == 0 and pc == 0) else ())
            tev = em.op("act", lambda a, scale=scale: a.activation(
                vstage[:], ps_w[:], AF.Copy, scale=scale),
                deps=[mml, getattr(cx, "t_spill_prev", None)])
            cx.puse(ps_w, tev)
            if kind in ("q", "k"):
                tok = em.dma("pool", dst_dram[ts(ic), :], vstage[:], deps=[tev])
                cx.t_spill_prev = tok
                if kind == "k":
                    kv_toks.append(tok)
                t_prev_w = tok
            else:
                vt = []
                for t in range(NT):
                    vt.append(transpose128(cx, v1t[t][:, ts(ic)], vstage[:, ts(t)], [tev]))
                t_prev_w = vt[-1]
    for t in range(NT):
        kv_toks.append(em.dma("pool", vin[ts(t), :], v1t[t][:], deps=[em.token("dve")]))
    cx.free(wrow, wTblk, vstage, *v1t)
    return kv_toks


def build_fused():
    nc = bass.Bass("TRN2", target_bir_lowering=False)
    AP = declare_fused_inputs(nc)
    out = nc.dram_tensor("x2", [TOK, E], F32, kind="ExternalOutput").ap()
    u_dram = nc.dram_tensor("u_scratch", [FFN_H, TOK], BF16, kind="Internal").ap()
    acc_dram = nc.dram_tensor("acc_scratch", [TOK, E], F32, kind="Internal").ap()
    qin = nc.dram_tensor("qin", [E, TOK], BF16, kind="Internal").ap()
    kin = nc.dram_tensor("kin", [E, TOK], BF16, kind="Internal").ap()
    vin = nc.dram_tensor("vin", [TOK, E], BF16, kind="Internal").ap()
    kg = nc.dram_tensor("kg", [NCORES * E, TOK], BF16, kind="Internal").ap()
    vg = nc.dram_tensor("vg", [NCORES * TOK, E], BF16, kind="Internal").ap()
    gselT_dram = nc.dram_tensor("gselT_d", [8, TOK], F32, kind="Internal").ap()
    em = Em(nc)
    cx = Ctx(nc, em)
    emit_consts(cx, AP)
    make_psum_pool(cx)
    emit_lambda(cx, AP)
    NT = TOK // 128

    xn_own = [cx.sb([128, TOK], BF16, f"xn_own{e}") for e in range(EC)]
    emit_l0(cx, AP, xn_own, u_dram, acc_dram)
    acc = [cx.sb([128, E], F32, f"accf{t}") for t in range(B * NQC)]
    emit_l0_ffn(cx, AP, xn_own, u_dram, acc_dram, acc)
    xtile = cx.sb([128, E], F32, "f_xres")
    tprev = None
    for b in range(B):
        for tc in range(NQC):
            i = b * NQC + tc
            tl = em.dma("sync", xtile[:], AP["xw"][b, WIN + tc * 128: WIN + (tc + 1) * 128, :],
                        deps=[tprev])
            tprev = em.op("dve", lambda v, i=i: v.tensor_tensor(
                acc[i][:], xtile[:], acc[i][:], ALU.add), deps=[tl])
    cx.t_x1 = tprev
    if DEBUG:
        dx1 = nc.dram_tensor("d_x1", [TOK, E], F32, kind="ExternalOutput").ap()
        for t in range(NT):
            em.dma("pool", dx1[ts(t), :], acc[t][:], deps=[cx.t_x1])
        em.barrier()
    cx.free(xtile)

    gselT_sb = cx.sb([8, TOK], F32, "f_gselT")
    xn1f = [cx.sb([128, TOK], BF16, f"f_xn1f{e}") for e in range(EC)]
    kv_toks = emit_l1pre_fused(cx, AP, acc, xn1f, gselT_sb, qin, kin, vin)
    t_gd = em.dma("pool", gselT_dram, gselT_sb[:], deps=[em.token("dve")])
    if DEBUG:
        dxn = nc.dram_tensor("d_xn1f", [E, TOK], BF16, kind="ExternalOutput").ap()
        for e in range(EC):
            em.dma("pool", dxn[ts(e), :], xn1f[e][:], deps=[em.token("dve")])
        dgs = nc.dram_tensor("d_gselT", [8, TOK], F32, kind="ExternalOutput").ap()
        em.dma("pool", dgs, gselT_sb[:], deps=[em.token("dve")])
        em.barrier()

    t_ck = em.op("pool", lambda g: g.collective_compute(
        "AllGather", ALU.bypass, replica_groups=[list(range(NCORES))],
        ins=[kin.opt()], outs=[kg.opt()]), deps=kv_toks, inc=1)
    t_cv = em.op("pool", lambda g: g.collective_compute(
        "AllGather", ALU.bypass, replica_groups=[list(range(NCORES))],
        ins=[vin.opt()], outs=[vg.opt()]), deps=[t_ck], inc=1)

    if DEBUG:
        cx.dbg_hfm = nc.dram_tensor("d_hfm", [E, TOK], BF16, kind="ExternalOutput").ap()
        cx.dbg_gbc = nc.dram_tensor("d_gbc", [128, TOK], F32, kind="ExternalOutput").ap()

    def grow_fn(ex, grow, deps):
        return em.dma("sync", grow[:], gselT_dram[ex:ex + 1, :],
                      deps=list(deps) + [t_gd])
    emit_moe(cx, AP, acc, xn1f, em.token("dve"), grow_fn=grow_fn, t_w0=t_cv)
    if DEBUG:
        dm = nc.dram_tensor("d_moe", [TOK, E], F32, kind="ExternalOutput").ap()
        for t in range(NT):
            em.dma("pool", dm[ts(t), :], acc[t][:], deps=[em.token("dve")])
        dxn2 = nc.dram_tensor("d_xn1f_post", [E, TOK], BF16, kind="ExternalOutput").ap()
        for e in range(EC):
            em.dma("pool", dxn2[ts(e), :], xn1f[e][:], deps=[em.token("dve")])
        dgd = nc.dram_tensor("d_gselTd", [8, TOK], F32, kind="ExternalOutput").ap()
        em.dma("pool", dgd, gselT_dram, deps=[em.token("dve")])
        em.barrier()
    cx.free(*xn1f)

    t_gate = em.token("pe")
    q1f = [cx.sb([128, TOK], BF16, f"f_q1f{h}") for h in range(L1_H)]
    for h in range(L1_H):
        em.dma("sync", q1f[h][:], qin[ts(h), :], deps=[em.token("pool")])
    t_q = em.token("sync")

    def k_src(h, b, dst, deps):
        tok = None
        for r in range(NCORES):
            tok = em.dma("sync", dst[:, ts(r, 512)],
                         kg[r * E + h * 128: r * E + (h + 1) * 128,
                            b * OWN:(b + 1) * OWN],
                         deps=list(deps) + [t_ck, t_gate])
        return tok

    def v_src(h, b, dst, deps):
        tok = None
        for r in range(NCORES):
            tok = em.dma("sync",
                         dst[:, ts(r, 512)].rearrange("p (kc d) -> p kc d", d=128),
                         vg[r * TOK + b * OWN: r * TOK + (b + 1) * OWN, ts(h)]
                         .rearrange("(kc p) d -> p kc d", p=128),
                         deps=list(deps) + [t_cv, t_gate])
        return tok

    attnf = emit_l1attn(cx, acc, q1f, k_src, v_src, t_q)
    if DEBUG:
        da = nc.dram_tensor("d_attnf", [E, TOK], BF16, kind="ExternalOutput").ap()
        for h in range(L1_H):
            em.dma("pool", da[ts(h), :], attnf[h][:], deps=[em.token("act")])
        em.barrier()
    emit_l1wo(cx, AP, acc, attnf)
    em.barrier()
    for t in range(NT):
        em.dma("sync", out[ts(t), :], acc[t][:], deps=[em.token("dve")])
    em.ops["sync"].append(("w", "sync", em.cnt["sync"]))
    em.replay()
    return nc



FUSED = True


def kernel(**inputs):
    I = {k: np.asarray(v) for k, v in inputs.items()}
    if not FUSED:
        return _kernel_3launch(I)
    nc = build_fused()
    in_maps = []
    for c in range(NCORES):
        m = prep_core_inputs(c, I)
        m.update({
            "L1_attn_Wq": I["L1_attn_Wq"], "L1_attn_Wk": I["L1_attn_Wk"],
            "L1_attn_Wv": I["L1_attn_Wv"], "L1_attn_Wo": I["L1_attn_Wo"],
            "L1_ffn_gate_kernel": I["L1_ffn_gate_kernel"],
            "lam_q1": I["L1_attn_lambda_q1"].reshape(64, 1),
            "lam_k1": I["L1_attn_lambda_k1"].reshape(64, 1),
            "lam_q2": I["L1_attn_lambda_q2"].reshape(64, 1),
            "lam_k2": I["L1_attn_lambda_k2"].reshape(64, 1),
            "subln": I["L1_attn_subln_weight"].reshape(128, 1),
            "L1_ffn_wi_0": I["L1_ffn_wi_0"], "L1_ffn_wi_1": I["L1_ffn_wi_1"],
            "L1_ffn_wo": I["L1_ffn_wo"],
        })
        in_maps.append(m)
    res = run_bass_kernel_spmd(nc, in_maps, core_ids=list(range(NCORES)))
    out = np.zeros((B, S, E), np.float32)
    for c in range(NCORES):
        r = np.asarray(res.results[c]["x2"], np.float32)
        for b in range(B):
            out[b, c * OWN:(c + 1) * OWN, :] = r[b * OWN:(b + 1) * OWN, :]
    return out


def _kernel_3launch(I):
    ident = np.eye(128, dtype=np.float32).astype(BFNP)
    prot = np.eye(128, dtype=np.float32)[:, list(range(64, 128)) + list(range(64))].copy()

    ncA = build_launch_A()
    in_maps = [prep_core_inputs(c, I) for c in range(NCORES)]
    resA = run_bass_kernel_spmd(ncA, in_maps, core_ids=list(range(NCORES)))
    x1c = [np.asarray(resA.results[c]["x1"]) for c in range(NCORES)]

    ncB = build_launch_B()
    in_mapsB = [{
        "x1in": x1c[c], "c_ident": ident, "c_prot": prot,
        "L1_attn_Wq": I["L1_attn_Wq"], "L1_attn_Wk": I["L1_attn_Wk"],
        "L1_attn_Wv": I["L1_attn_Wv"],
        "L1_ffn_gate_kernel": I["L1_ffn_gate_kernel"],
    } for c in range(NCORES)]
    resB = run_bass_kernel_spmd(ncB, in_mapsB, core_ids=list(range(NCORES)))

    k_all = np.zeros((E, B * S), dtype=BFNP)
    v_all = np.zeros((B * S, E), dtype=BFNP)
    for c in range(NCORES):
        k1 = np.asarray(resB.results[c]["k1"])
        v1 = np.asarray(resB.results[c]["v1"])
        if k1.dtype.itemsize == 2 and k1.dtype != BFNP:
            k1 = k1.view(BFNP); v1 = v1.view(BFNP)
        for b in range(B):
            k_all[:, b * S + c * OWN:(b * S + (c + 1) * OWN)] = k1[:, b * OWN:(b + 1) * OWN]
            v_all[b * S + c * OWN:(b * S + (c + 1) * OWN), :] = v1[b * OWN:(b + 1) * OWN, :]

    ncC = build_launch_C()
    in_mapsC = []
    for c in range(NCORES):
        q1 = np.asarray(resB.results[c]["q1"])
        xn1f = np.asarray(resB.results[c]["xn1f"])
        if q1.dtype.itemsize == 2 and q1.dtype != BFNP:
            q1 = q1.view(BFNP); xn1f = xn1f.view(BFNP)
        gsel = np.asarray(resB.results[c]["gsel"], np.float32)
        in_mapsC.append({
            "x1in": x1c[c], "c_ident": ident, "c_prot": prot,
            "xn1f": xn1f, "gselT": np.ascontiguousarray(gsel.T),
            "q1": q1, "k_all": k_all, "v_all": v_all,
            "L1_attn_Wo": I["L1_attn_Wo"],
            "lam_q1": I["L1_attn_lambda_q1"].reshape(64, 1),
            "lam_k1": I["L1_attn_lambda_k1"].reshape(64, 1),
            "lam_q2": I["L1_attn_lambda_q2"].reshape(64, 1),
            "lam_k2": I["L1_attn_lambda_k2"].reshape(64, 1),
            "subln": I["L1_attn_subln_weight"].reshape(128, 1),
            "L1_ffn_wi_0": I["L1_ffn_wi_0"], "L1_ffn_wi_1": I["L1_ffn_wi_1"],
            "L1_ffn_wo": I["L1_ffn_wo"],
        })
    resC = run_bass_kernel_spmd(ncC, in_mapsC, core_ids=list(range(NCORES)))

    out = np.zeros((B, S, E), np.float32)
    for c in range(NCORES):
        r = np.asarray(resC.results[c]["x2"], np.float32)
        for b in range(B):
            out[b, c * OWN:(c + 1) * OWN, :] = r[b * OWN:(b + 1) * OWN, :]
    return out


def prep_core_inputs(core, I):
    q0 = core * OWN
    x = I["L0_attn_hidden_states"]
    xw = np.zeros((B, WROWS, E), np.float32)
    lo, hi = q0 - WIN, q0 + OWN + WIN
    slo, shi = max(lo, 0), min(hi, S)
    xw[:, slo - lo:shi - lo, :] = x[:, slo:shi, :]
    pos = np.clip(np.arange(lo, hi), 0, S - 1)
    cosw, sinw = _rope_tables_np(pos)
    return {
        "xw": np.ascontiguousarray(xw),
        "c_ident": np.eye(128, dtype=np.float32).astype(BFNP),
        "c_prot": np.eye(128, dtype=np.float32)[:, list(range(64, 128)) + list(range(64))].copy(),
        "c_cosw": cosw, "c_sinw": sinw,
        "c_l0mask": l0_mask_values(core),
        "qscale": np.ascontiguousarray(I["L0_attn_query_scale"].reshape(128, 1)),
        "kscale": np.ascontiguousarray(I["L0_attn_key_scale"].reshape(128, 1)),
        "L0_attn_wq": I["L0_attn_wq"], "L0_attn_wk": I["L0_attn_wk"],
        "L0_attn_wv": I["L0_attn_wv"], "L0_attn_wo": I["L0_attn_wo"],
        "L0_ffn_W_fc": I["L0_ffn_W_fc"], "L0_ffn_W_proj": I["L0_ffn_W_proj"],
    }


# version 07 (final)
# speedup vs baseline: 1.0318x; 1.0197x over previous
import contextlib
import math

import numpy as np
import ml_dtypes
import concourse.bass as bass
import concourse.mybir as mybir
from concourse.bass_utils import run_bass_kernel_spmd

F32 = mybir.dt.float32
BF16 = mybir.dt.bfloat16
AF = mybir.ActivationFunctionType
ALU = mybir.AluOpType
AX = mybir.AxisListType
BFNP = ml_dtypes.bfloat16

NCORES = 8
B = 2
S = 4096
E = 2048
OWN = 512
TOK = B * OWN
WIN = 1024
WROWS = OWN + 2 * WIN
NT_W = WROWS // 128
NQC = OWN // 128
NKC = WROWS // 128
EC = E // 128
L0_KV = 4
L0_H = 16
CAP = 50.0
L1_H = 16
LAMBDA_INIT = 0.8 - 0.6 * float(np.exp(-0.3 * 12))
FFN_H = 8192
HC = FFN_H // 128
MOE_E = 8

ENGINES = ("pe", "dve", "act", "pool", "sync")
CCSEMS = ("cc0", "cc1")
DEBUG = False


class Em:
    def __init__(self, nc):
        self.nc = nc
        self.ops = {e: [] for e in ENGINES}
        self.cnt = {e: 0 for e in ENGINES + CCSEMS}
        self.wm = {e: {s: 0 for s in ENGINES} for e in ENGINES}
        self.pend = {e: {} for e in ENGINES}

    def op(self, eng, fn, deps=(), inc=1, sem=None, emit_inc=None):
        for ps, val in self.pend[eng].items():
            if self.wm[eng].get(ps, 0) < val:
                self.ops[eng].append(("w", ps, val))
                self.wm[eng][ps] = val
        self.pend[eng] = {}
        for dep in deps:
            if dep is None:
                continue
            ds, val = dep
            if ds == eng:
                if val > self.cnt[eng]:
                    continue
            if self.wm[eng].get(ds, 0) >= val or val <= 0:
                continue
            self.wm[eng][ds] = val
            self.ops[eng].append(("w", ds, val))
        key = sem if sem is not None else eng
        self.cnt[key] += inc
        self.ops[eng].append(("i", fn, inc if emit_inc is None else emit_inc,
                              sem))
        return (key, self.cnt[key])

    def dma(self, eng, out, in_, deps=(), **kw):
        return self.op(eng, lambda q: q.dma_start(out=out, in_=in_, **kw),
                       deps=deps, inc=16)

    def barrier(self, engines=ENGINES, on=ENGINES):
        snap = {e: self.cnt[e] for e in on}
        for e in engines:
            for s, v in snap.items():
                if v > 0 and s != e and self.pend[e].get(s, 0) < v:
                    self.pend[e][s] = v
        return snap

    def token(self, eng):
        return (eng, self.cnt[eng])

    def replay(self):
        nc = self.nc
        with contextlib.ExitStack() as st:
            sems = {e: st.enter_context(nc.semaphore("sem_" + e))
                    for e in ENGINES + CCSEMS}
            block = st.enter_context(nc.Block())
            eng_obj = {"pe": nc.tensor, "dve": nc.vector, "act": nc.scalar,
                       "pool": nc.gpsimd, "sync": nc.sync}
            deco = {"pe": block.tensor, "dve": block.vector, "act": block.scalar,
                    "pool": block.gpsimd, "sync": block.sync}

            def mk(ename):
                ops = self.ops[ename]
                my = eng_obj[ename]
                sem = sems[ename]

                def body(_q):
                    for rec in ops:
                        if rec[0] == "w":
                            my.wait_ge(sems[rec[1]], rec[2])
                        else:
                            inst = rec[1](my)
                            if rec[2]:
                                tgt = sems[rec[3]] if rec[3] else sem
                                inst.then_inc(tgt, rec[2])
                return body

            for e in ENGINES:
                deco[e](mk(e))


def ts(i, size=128):
    return slice(i * size, (i + 1) * size)


class Ctx:
    def __init__(self, nc, em):
        self.nc = nc
        self.em = em
        self._n = 0
        self.pt = {}
        self._handles = {}
        self._order = {}

    def sb(self, shape, dtype=F32, name=None):
        self._n += 1
        h = self.nc.sbuf_tensor(f"{name or 't'}_{self._n}", list(shape), dtype)
        t = h.__enter__()
        self._handles[id(t)] = h
        self._order[id(t)] = self._n
        return t

    def ps(self, shape, dtype=F32, name=None):
        self._n += 1
        h = self.nc.psum_tensor(f"{name or 'p'}_{self._n}", list(shape), dtype)
        t = h.__enter__()
        self._handles[id(t)] = h
        return t

    def free(self, *tensors):
        self.em.barrier()
        for t in sorted(tensors, key=lambda x: -self._order[id(x)]):
            self._handles.pop(id(t)).__exit__(None, None, None)

    def pdep(self, pt):
        return self.pt.get(id(pt))

    def puse(self, pt, tok):
        self.pt[id(pt)] = tok


def emit_consts(cx, AP):
    em = cx.em
    cx.ident = cx.sb([128, 128], BF16, "ident")
    cx.ones_cb = cx.sb([128, 1], BF16, "ones_cb")
    cx.ones_cf = cx.sb([128, 1], F32, "ones_cf")
    cx.ones_rf = cx.sb([1, 128], F32, "ones_rf")
    cx.t_ident = em.dma("sync", cx.ident[:], AP["c_ident"])
    m1 = em.op("dve", lambda v: v.memset(cx.ones_cb[:], 1.0), inc=0)
    m2 = em.op("dve", lambda v: v.memset(cx.ones_cf[:], 1.0), inc=0)
    cx.t_ones = em.op("dve", lambda v: v.memset(cx.ones_rf[:], 1.0), inc=1)
    cx.prot = cx.sb([128, 128], F32, "prot")
    cx.t_prot = em.dma("sync", cx.prot[:], AP["c_prot"])
    cx.bias_ln = cx.sb([128, 1], F32, "bias_ln")
    cx.bias_eps = cx.sb([128, 1], F32, "bias_eps")
    cx.bias_ncap = cx.sb([128, 1], F32, "bias_ncap")
    em.op("dve", lambda v: v.memset(cx.bias_ln[:], 1e-5), inc=0)
    em.op("dve", lambda v: v.memset(cx.bias_eps[:], 1.1920929e-07), inc=0)
    cx.t_bias = em.op("dve", lambda v: v.memset(cx.bias_ncap[:], -CAP), inc=1)


def layernorm_tile(cx, xt, out_bf, scr, stats, deps):
    em = cx.em
    s1 = stats[:, 0:1]; s2 = stats[:, 1:2]; mcol = stats[:, 2:3]
    vcol = stats[:, 3:4]; sdcol = stats[:, 4:5]; rcol = stats[:, 5:6]
    deps = list(deps)
    t1 = em.op("dve", lambda v: v.tensor_reduce(s1, xt[:], AX.X, ALU.add), deps=deps)
    t2 = em.op("act", lambda a: a.activation(scr[:], xt[:], AF.Square), deps=deps)
    t3 = em.op("dve", lambda v: v.tensor_reduce(s2, scr[:], AX.X, ALU.add), deps=[t2, t1])
    t4 = em.op("dve", lambda v: v.tensor_scalar(mcol, s1, 1.0 / E, None, ALU.mult), deps=[t3])
    t5 = em.op("dve", lambda v: v.tensor_scalar(vcol, s2, 1.0 / E, None, ALU.mult), deps=[t4])
    t6 = em.op("dve", lambda v: v.tensor_tensor(sdcol, mcol, mcol, ALU.mult), deps=[t5])
    t7 = em.op("dve", lambda v: v.tensor_tensor(vcol, vcol, sdcol, ALU.subtract), deps=[t6])
    t8 = em.op("act", lambda a: a.activation(sdcol, vcol, AF.Sqrt, bias=cx.bias_ln[:]),
               deps=[t7, cx.t_bias])
    t9 = em.op("dve", lambda v: v.reciprocal(rcol, sdcol), deps=[t8])
    tA = em.op("dve", lambda v: v.tensor_scalar(out_bf[:], xt[:], mcol, rcol,
                                                ALU.subtract, ALU.mult), deps=[t9])
    return tA




def _rope_tables_np(positions):
    half = 64
    inv = (1.0 / 10000.0 ** (np.arange(0, half, dtype=np.float32) / half))
    ang = np.outer(positions.astype(np.float32), inv)
    ang = np.concatenate([ang, ang], -1)
    cos = np.cos(ang).astype(np.float32)
    sin = np.sin(ang).astype(np.float32)
    sinA = sin.copy()
    sinA[:, :half] *= -1.0
    return np.ascontiguousarray(cos.T), np.ascontiguousarray(sinA.T)


def l0_mask_pairs():
    pairs = []
    for kc in range(NKC):
        qlo, qhi = max(0, kc - 16), min(NQC - 1, kc)
        for qc in range(qlo, qhi + 1):
            if kc - qc in (0, 16) or kc < WIN // 128 or kc >= (WIN + OWN) // 128:
                pairs.append((kc, qc))
    return pairs


_L0_PAIRS = l0_mask_pairs()
_L0_PAIR_IDX = {p: i for i, p in enumerate(_L0_PAIRS)}


def l0_mask_values(core):
    q0 = core * OWN
    out = np.zeros((len(_L0_PAIRS), 128, 128), np.float32)
    for i, (kc, qc) in enumerate(_L0_PAIRS):
        kpos = q0 - WIN + kc * 128 + np.arange(128)
        qpos = q0 + qc * 128 + np.arange(128)
        valid = (np.abs(qpos[None, :] - kpos[:, None]) <= WIN) & \
                (kpos[:, None] >= 0) & (kpos[:, None] < S)
        out[i] = valid.astype(np.float32)
    return np.ascontiguousarray(out.astype(BFNP))


def declare_l0_inputs(nc):
    ap = {}
    def inp(n, sh, dt=F32):
        ap[n] = nc.dram_tensor(n, sh, dt, kind="ExternalInput").ap()
    inp("xw", [B, WROWS, E])
    inp("c_ident", [128, 128], BF16)
    inp("c_prot", [128, 128])
    inp("c_cosw", [128, WROWS]); inp("c_sinw", [128, WROWS])
    inp("c_l0mask", [len(_L0_PAIRS), 128, 128], BF16)
    inp("qscale", [128, 1]); inp("kscale", [128, 1])
    inp("L0_attn_wq", [E, E]); inp("L0_attn_wk", [E, 512])
    inp("L0_attn_wv", [E, 512]); inp("L0_attn_wo", [E, E])
    inp("L0_ffn_W_fc", [FFN_H, E]); inp("L0_ffn_W_proj", [E, FFN_H])
    return ap




def transpose128(cx, dst_ap, src_ap, deps):
    em = cx.em
    bank = cx.tr_rot % 2
    cx.tr_rot += 1
    pb = cx.pp["tr0" if bank == 0 else "tr1"]
    tt = em.op("pe", lambda t: t.transpose(pb[:, 0:128], src_ap, cx.ident[:]),
               deps=list(deps) + [cx.t_ident, cx.pdep(pb)])
    te = em.op("dve", lambda v: v.tensor_copy(dst_ap, pb[:, 0:128]), deps=[tt])
    cx.puse(pb, te)
    return te


def rope_norm(cx, raw, dst_bf, width, scale_col, rc, rs, sc1, sc2, sqs, rows,
              deps):
    em = cx.em
    rowp = cx.pp["row"]
    bc = cx.pp["c"]
    rotp = cx.pp["o"]
    deps = list(deps)
    a1 = em.op("dve", lambda v: v.tensor_tensor(sc1[:, 0:width], raw[:, 0:width],
                                                rc[:, 0:width], ALU.mult), deps=deps)
    a2 = em.op("pe", lambda t: t.matmul(rotp[:, 0:width], cx.prot[:], raw[:, 0:width],
                                        start=True, stop=True),
               deps=deps + [cx.t_prot, cx.pdep(rotp)])
    a3 = em.op("dve", lambda v: v.tensor_tensor(sc2[:, 0:width], rotp[:, 0:width],
                                                rs[:, 0:width], ALU.mult), deps=[a2])
    cx.puse(rotp, a3)
    a4 = em.op("dve", lambda v: v.tensor_tensor(sc1[:, 0:width], sc1[:, 0:width],
                                                sc2[:, 0:width], ALU.add), deps=[a1, a3])
    a5 = em.op("act", lambda a: a.activation(sqs[:, 0:width], sc1[:, 0:width], AF.Square),
               deps=[a4])
    np_ = (width + 511) // 512
    sq_toks = []
    for p in range(np_):
        w0, w1 = p * 512, min(width, (p + 1) * 512)
        tm = em.op("pe", lambda t, w0=w0, w1=w1: t.matmul(
            rowp[0:1, 0:w1 - w0], cx.ones_cb[:], sqs[:, w0:w1], start=True, stop=True),
            deps=[a5, cx.t_ones, cx.pdep(rowp)])
        tq = em.op("act", lambda a, w0=w0, w1=w1: a.activation(
            rows[:, w0:w1], rowp[0:1, 0:w1 - w0], AF.Sqrt), deps=[tm])
        cx.puse(rowp, tq)
        sq_toks.append(tq)
    a7 = em.op("dve", lambda v: v.tensor_scalar(rows[:, 0:width], rows[:, 0:width],
                                                1e-6, None, ALU.add), deps=sq_toks)
    a8 = em.op("dve", lambda v: v.reciprocal(rows[:, 0:width], rows[:, 0:width]), deps=[a7])
    out_toks = []
    for p in range(np_):
        w0, w1 = p * 512, min(width, (p + 1) * 512)
        tb = em.op("pe", lambda t, w0=w0, w1=w1: t.matmul(
            bc[:, 0:w1 - w0], cx.ones_rf[:], rows[:, w0:w1], start=True, stop=True),
            deps=[a8, cx.t_ones, cx.pdep(bc)])
        tm = em.op("dve", lambda v, w0=w0, w1=w1: v.tensor_tensor(
            sc2[:, w0:w1], sc1[:, w0:w1], bc[:, 0:w1 - w0], ALU.mult), deps=[tb])
        cx.puse(bc, tm)
        tc = em.op("act", lambda a, w0=w0, w1=w1: a.activation(
            dst_bf[:, w0:w1], sc2[:, w0:w1], AF.Copy, scale=scale_col), deps=[tm])
        out_toks.append(tc)
    return out_toks


L0_D_CONST = 128


def xn_own_cols(xn_own, e, b):
    return xn_own[e][:, b * OWN:(b + 1) * OWN]


def emit_l0(cx, AP, xn_own, u_dram, acc_dram):
    em = cx.em

    rope_cos = cx.sb([128, WROWS], BF16, "ropec")
    rope_sin = cx.sb([128, WROWS], BF16, "ropes")
    qscale = cx.sb([128, 1], F32, "qscale")
    kscale = cx.sb([128, 1], F32, "kscale")
    t_rc = em.dma("pool", rope_cos[:], AP["c_cosw"])
    t_rs = em.dma("pool", rope_sin[:], AP["c_sinw"])
    t_qs = em.dma("sync", qscale[:], AP["qscale"])
    t_ks = em.dma("sync", kscale[:], AP["kscale"])

    pA = pB = cx.pp["a"]

    sc1 = cx.sb([128, 512], F32, "sc1")
    sc2 = cx.sb([128, 512], F32, "sc2")
    sqs = cx.sb([128, 512], BF16, "sqs")
    rows = cx.sb([1, 512], F32, "rows")
    kraw = cx.sb([128, 512], F32, "kraw")
    vpc = cx.sb([128, 512], BF16, "vpc")
    wblk = [cx.sb([128, EC * 128], BF16, f"wblk{i}") for i in range(2)]
    npieces = WROWS // 512

    for b in range(B):
        K = [cx.sb([128, WROWS], BF16, f"K_{h}") for h in range(L0_KV)]
        Vt = [cx.sb([128, NKC * 128], BF16, f"V_{h}") for h in range(L0_KV)]
        xnw = [cx.sb([128, WROWS], BF16, f"xnw_{e}") for e in range(EC)]
        xt = [cx.sb([128, E], F32, "lnx0")]
        scr = cx.sb([128, E], F32, "lnscr")
        xbf = cx.sb([128, E], BF16, "lnbf")
        stats = cx.sb([128, 8], F32, "lnstat")
        tile_done = []
        prev_use = [None]
        for t in range(NT_W):
            buf = xt[0]
            tl = em.dma("sync", buf[:], AP["xw"][b, ts(t), :], deps=[prev_use[0]])
            ln = layernorm_tile(cx, buf, xbf, scr, stats, [tl])
            last = None
            for e in range(EC):
                last = transpose128(cx, xnw[e][:, ts(t)], xbf[:, ts(e)], [ln])
            tile_done.append(last)
            prev_use[0] = last
        if DEBUG and b == 0 and hasattr(cx.nc, "_dbg_xnw"):
            em.dma("pool", cx.nc._dbg_xnw[0:128, :], xnw[4][:], deps=tile_done)
            em.dma("pool", cx.nc._dbg_xnw[128:256, :], xnw[5][:], deps=tile_done)
            em.barrier()
        for e in range(EC):
            em.op("dve", lambda v, e=e, b=b: v.tensor_copy(
                xn_own[e][:, b * OWN:(b + 1) * OWN], xnw[e][:, WIN:WIN + OWN]),
                deps=[tile_done[-1]] if e == 0 else ())
        t_xnown = em.token("dve")
        cx.free(xt[0], scr, xbf, stats)

        wtoks = [None, None]
        kv_done = {}
        q_done = {}
        for kh in range(L0_KV):
            wb = wblk[kh % 2]
            t_wk = em.dma("pool", wb[:].rearrange("p (ec i) -> p ec i", i=128),
                          AP["L0_attn_wk"][:, ts(kh)].rearrange("(ec p) i -> p ec i", p=128),
                          deps=[wtoks[kh % 2]])
            for p in range(npieces):
                pm = pA
                mmlast = None
                for e in range(EC):
                    mmlast = em.op("pe", lambda t, e=e, p=p, wb=wb, pm=pm: t.matmul(
                        pm[:], wb[:, ts(e)],
                        xnw[e][:, ts(p, 512)], start=(e == 0), stop=(e == EC - 1)),
                        deps=([t_wk, cx.pdep(pm)] + tile_done) if e == 0 else ())
                wtoks[kh % 2] = mmlast
                tev = em.op("dve", lambda v, pm=pm: v.tensor_copy(kraw[:], pm[:]),
                            deps=[mmlast])
                cx.puse(pm, tev)
                kv_done.setdefault(kh, []).extend(rope_norm(
                    cx, kraw, K[kh][:, ts(p, 512)], 512, kscale[:],
                    rope_cos[:, ts(p, 512)], rope_sin[:, ts(p, 512)],
                    sc1, sc2, sqs, rows, [tev, t_rc, t_rs, t_ks]))
            t_wv = em.dma("pool", wb[:].rearrange("p (ec i) -> p ec i", i=128),
                          AP["L0_attn_wv"][:, ts(kh)].rearrange("(ec p) i -> p ec i", p=128),
                          deps=[wtoks[kh % 2], em.token("pe")])
            for p in range(npieces):
                pm = pA
                mmlast = None
                for e in range(EC):
                    mmlast = em.op("pe", lambda t, e=e, p=p, wb=wb, pm=pm: t.matmul(
                        pm[:], wb[:, ts(e)],
                        xnw[e][:, ts(p, 512)], start=(e == 0), stop=(e == EC - 1)),
                        deps=[t_wv, cx.pdep(pm)] if e == 0 else ())
                wtoks[kh % 2] = mmlast
                tev = em.op("dve", lambda v, pm=pm: v.tensor_copy(vpc[:], pm[:]),
                            deps=[mmlast])
                cx.puse(pm, tev)
                for j in range(4):
                    transpose128(cx, Vt[kh][:, ts(p * 4 + j)], vpc[:, ts(j)], [tev])
        t_kv = em.token("dve")
        if DEBUG and b == 0 and hasattr(cx.nc, "_dbg_k"):
            em.dma("pool", cx.nc._dbg_k, K[0][:], deps=[t_kv])
            for e in range(EC):
                em.dma("pool", cx.nc._dbg_xn[ts(e), :], xn_own[e][:], deps=[t_xnown])
            em.barrier()
        cx.free(*xnw)

        Qf = [cx.sb([128, OWN], BF16, f"Qf_{h}") for h in range(L0_H)]
        qraw = cx.sb([128, OWN], F32, "qraw")
        for h in range(L0_H):
            wb = wblk[h % 2]
            t_wq = em.dma("pool", wb[:].rearrange("p (ec i) -> p ec i", i=128),
                          AP["L0_attn_wq"][:, ts(h)].rearrange("(ec p) i -> p ec i", p=128),
                          deps=[wtoks[h % 2], em.token("pe")])
            mmlast = None
            for e in range(EC):
                mmlast = em.op("pe", lambda t, e=e, b=b, wb=wb: t.matmul(
                    pA[:], wb[:, ts(e)], xn_own_cols(xn_own, e, b),
                    start=(e == 0), stop=(e == EC - 1)),
                    deps=[t_wq, cx.pdep(pA), t_xnown] if e == 0 else ())
            wtoks[h % 2] = mmlast
            tev = em.op("dve", lambda v: v.tensor_copy(qraw[:], pA[:]), deps=[mmlast])
            cx.puse(pA, tev)
            q_done[h] = rope_norm(cx, qraw, Qf[h], OWN, qscale[:],
                      rope_cos[:, WIN:WIN + OWN], rope_sin[:, WIN:WIN + OWN],
                      sc1, sc2, sqs, rows, [tev, t_qs])

        if DEBUG and b == 0 and hasattr(cx.nc, "_dbg_q"):
            em.dma("pool", cx.nc._dbg_q, Qf[0][:], deps=[em.token("act"), em.token("dve")])
            em.barrier()
        masks = cx.sb([128, len(_L0_PAIRS) * 128], BF16, "l0mask")
        t_mk = em.dma("sync", masks[:].rearrange("p (n k) -> p n k", k=128),
                      AP["c_l0mask"].rearrange("n p k -> p n k"))
        attn = [cx.sb([128, OWN], BF16, f"attn_{h}") for h in range(L0_H)]
        tS = cx.sb([128, 512], F32, "tS")
        pT = cx.sb([128, 512], BF16, "pTl0")
        rl = cx.sb([1, 512], F32, "rl")
        of = cx.sb([128, 512], F32, "of")
        ps_l = cx.pp["row"]
        ps_o = cx.pp["o"]
        ps_bc = cx.pp["c"]
        for h in range(L0_H):
            kh = h // (L0_H // L0_KV)
            t_pT_prev = None
            tl = tv = None
            for kc in range(NKC):
                qlo, qhi = max(0, kc - 16), min(NQC - 1, kc)
                r0, r1 = qlo * 128, (qhi + 1) * 128
                tmm = em.op("pe", lambda t, kc=kc, kh=kh, h=h, r0=r0, r1=r1: t.matmul(
                    pA[:, 0:r1 - r0], K[kh][:, ts(kc)], Qf[h][:, r0:r1],
                    start=True, stop=True),
                    deps=[cx.pdep(pA)] + (kv_done[kh] + q_done[h] if kc == 0 else []))
                tt = em.op("act", lambda a, r0=r0, r1=r1: a.activation(
                    tS[:, 0:r1 - r0], pA[:, 0:r1 - r0], AF.Tanh,
                    scale=1.0 / (CAP * math.sqrt(L0_D_CONST))), deps=[tmm])
                cx.puse(pA, tt)
                tex = em.op("act", lambda a, r0=r0, r1=r1: a.activation(
                    pT[:, 0:r1 - r0], tS[:, 0:r1 - r0], AF.Exp, scale=CAP,
                    bias=cx.bias_ncap[:]), deps=[tt, t_pT_prev, cx.t_bias])
                mtoks = [tex]
                for qc in range(qlo, qhi + 1):
                    pi = _L0_PAIR_IDX.get((kc, qc))
                    if pi is not None:
                        mtoks.append(em.op("pool", lambda g, pi=pi, qc=qc, r0=r0: g.tensor_tensor(
                            pT[:, qc * 128 - r0: qc * 128 - r0 + 128],
                            pT[:, qc * 128 - r0: qc * 128 - r0 + 128],
                            masks[:, ts(pi)], ALU.mult), deps=[tex, t_mk]))
                tl = em.op("pe", lambda t, kc=kc, r0=r0, r1=r1: t.matmul(
                    ps_l[0:1, r0:r1], cx.ones_cb[:], pT[:, 0:r1 - r0],
                    start=(kc == 0), stop=(kc == NKC - 1)),
                    deps=mtoks + ([cx.pdep(ps_l)] if kc == 0 else []))
                tv = em.op("pe", lambda t, kc=kc, kh=kh, r0=r0, r1=r1: t.matmul(
                    ps_o[:, r0:r1], Vt[kh][:, ts(kc)], pT[:, 0:r1 - r0],
                    start=(kc == 0), stop=(kc == NKC - 1)),
                    deps=([t_kv, cx.pdep(ps_o)] if kc == 0 else []))
                t_pT_prev = tv
            trc = em.op("dve", lambda v: v.reciprocal(rl[:], ps_l[0:1, 0:512]), deps=[tl, tv])
            cx.puse(ps_l, trc)
            tb = em.op("pe", lambda t: t.matmul(ps_bc[:], cx.ones_rf[:], rl[:],
                                                start=True, stop=True),
                       deps=[trc, cx.pdep(ps_bc)])
            tof = em.op("dve", lambda v: v.tensor_copy(of[:], ps_o[:, 0:512]), deps=[tv, tb])
            cx.puse(ps_o, tof)
            tat = em.op("dve", lambda v, h=h: v.tensor_tensor(
                attn[h][:], of[:], ps_bc[:], ALU.mult), deps=[tof])
            cx.puse(ps_bc, tat)

        cx.free(tS, pT, rl, of)

        t_attn_all = em.token("dve")
        if DEBUG and b == 0 and hasattr(cx.nc, "_dbg_at"):
            em.dma("pool", cx.nc._dbg_at, attn[0][:], deps=[t_attn_all])
            em.barrier()
        wo_grp = cx.sb([128, 4 * E], BF16, "wo_grp")
        acc = [cx.sb([128, E], F32, f"accw{t}") for t in range(NQC)]
        ps_w = cx.pp["w"]
        tev = None
        for g in range(4):
            t_wo = em.dma("pool", wo_grp[:].rearrange("p (hh e) -> p hh e", e=E),
                          AP["L0_attn_wo"][ts(g, 512), :].rearrange("(hh p) e -> p hh e", p=128),
                          deps=[getattr(cx, "pdep_wo", None)])
            for tc in range(NQC):
                for half in range(2):
                    mmlast = None
                    for hh in range(4):
                        h = g * 4 + hh
                        for pc in range(2):
                            mmlast = em.op("pe", lambda t, h=h, hh=hh, tc=tc, pc=pc, half=half: t.matmul(
                                ps_w[:, ts(pc, 512)], attn[h][:, ts(tc)],
                                wo_grp[:, hh * E + half * 1024 + pc * 512:
                                       hh * E + half * 1024 + (pc + 1) * 512],
                                start=(hh == 0), stop=(hh == 3)),
                                deps=[t_wo, cx.pdep(ps_w), t_attn_all]
                                if (hh == 0 and pc == 0) else ())
                    if g == 0:
                        tev = em.op("dve", lambda v, tc=tc, half=half: v.tensor_copy(
                            acc[tc][:, half * 1024:(half + 1) * 1024], ps_w[:]),
                            deps=[mmlast])
                    else:
                        tev = em.op("dve", lambda v, tc=tc, half=half: v.tensor_tensor(
                            acc[tc][:, half * 1024:(half + 1) * 1024],
                            acc[tc][:, half * 1024:(half + 1) * 1024],
                            ps_w[:], ALU.add), deps=[mmlast])
                    cx.puse(ps_w, tev)
            cx.pdep_wo = tev
        sp_toks = []
        for tc in range(NQC):
            sp_toks.append(em.dma("pool", acc_dram[(b * NQC + tc) * 128:
                                                   (b * NQC + tc + 1) * 128, :],
                                  acc[tc][:], deps=[tev]))
        em.barrier()
        cx.free(*(acc + [wo_grp] + attn + [masks, qraw] + Qf + Vt + K))

    cx.free(sc1, sc2, sqs, rows, kraw, vpc, wblk[0], wblk[1],
            rope_cos, rope_sin, qscale, kscale)
    em.barrier()


def quantize_tile(cx, wt_bf, gscr, width, deps):
    em = cx.em
    ng = width // 64
    s_mean = gscr[:, 0:ng]
    s_meanb = cx.qz_meanb[:, 0:ng]
    s_recip = gscr[:, ng:2 * ng]
    deps = list(deps)
    t1 = em.op("dve", lambda v: v.tensor_reduce(
        s_mean, wt_bf[:, 0:width].rearrange("p (g k) -> p g k", k=64),
        AX.X, ALU.add, apply_absolute_value=True), deps=deps)
    t2 = em.op("act", lambda a: a.activation(s_meanb, s_mean, AF.Copy, scale=1.0 / 64.0),
               deps=[t1])
    t3 = em.op("dve", lambda v: v.tensor_scalar(s_meanb, s_meanb, 1e-8, None, ALU.max),
               deps=[t2])
    t4 = em.op("dve", lambda v: v.reciprocal(s_recip, s_meanb), deps=[t3])
    rb = cx.qz_ratio[:, 0:width]
    qa = cx.qz_qa[:, 0:width]
    t5 = em.op("dve", lambda v: v.tensor_tensor(
        rb.rearrange("p (g k) -> p g k", k=64),
        wt_bf[:, 0:width].rearrange("p (g k) -> p g k", k=64),
        s_recip.unsqueeze(2).broadcast_to((128, ng, 64)), ALU.mult),
        deps=[t4])
    t6 = em.op("pool", lambda g: g.tensor_scalar(qa, rb, 0.5, None, ALU.is_gt), deps=[t5])
    t7 = em.op("dve", lambda v: v.tensor_scalar(rb, rb, -0.5, None, ALU.is_lt), deps=[t6])
    t8 = em.op("pool", lambda g: g.tensor_tensor(qa, qa, rb, ALU.subtract), deps=[t7])
    t9 = em.op("dve", lambda v: v.tensor_tensor(
        wt_bf[:, 0:width].rearrange("p (g k) -> p g k", k=64),
        qa.rearrange("p (g k) -> p g k", k=64),
        s_meanb.unsqueeze(2).broadcast_to((128, ng, 64)), ALU.mult),
        deps=[t8])
    return t9


def emit_l0_ffn(cx, AP, xn_own, u_dram, acc_dram, acc):
    em = cx.em
    cx.qz_meanb = cx.sb([128, 512], BF16, "qz_meanb")
    cx.qz_ratio = cx.sb([128, 8192], BF16, "qz_ratio")
    cx.qz_qa = cx.sb([128, 8192], BF16, "qz_qa")
    gscr = cx.sb([128, 1024], F32, "qz_gscr")

    pA = pB = cx.pp["a"]
    msacc = cx.sb([128, 8], F32, "msacc")
    rstd = cx.sb([128, 8], F32, "rstd")
    t_ms0 = em.op("dve", lambda v: v.memset(msacc[:], 0.0), inc=1)

    wfc_bf = cx.sb([128, 4 * E], BF16, "wfc_bf")
    w1qT = cx.sb([128, EC * 512], BF16, "w1qT")
    rrelu = cx.sb([128, 512], F32, "rrelu")
    u32 = cx.sb([128, 512], F32, "u32")
    ubf = cx.sb([128, 512], BF16, "ubf")
    u2d = cx.sb([128, 512], BF16, "u2dump")
    part = cx.sb([128, 1], F32, "part")
    uT = cx.sb([128, 512], BF16, "uT")

    for grp in range(16):
        qz = []
        for j in range(4):
            tl = em.dma("pool", wfc_bf[:, j * E:(j + 1) * E],
                        AP["L0_ffn_W_fc"][ts(grp * 4 + j), :],
                        deps=[getattr(cx, "pdep_wfc", None)])
            qz.append(quantize_tile(cx, wfc_bf[:, j * E:(j + 1) * E], gscr, E, [tl]))
        ttok = []
        for j in range(4):
            for e in range(EC):
                ttok.append(transpose128(cx, w1qT[:, e * 512 + j * 128: e * 512 + (j + 1) * 128],
                                         wfc_bf[:, j * E + e * 128: j * E + (e + 1) * 128],
                                         [qz[j]]))
        cx.pdep_wfc = ttok[-1]
        for b in range(B):
            for tc in range(NQC):
                pm = pA
                mmlast = None
                for e in range(EC):
                    mmlast = em.op("pe", lambda t, e=e, b=b, tc=tc, pm=pm: t.matmul(
                        pm[:], xn_own[e][:, b * OWN + tc * 128: b * OWN + (tc + 1) * 128],
                        w1qT[:, ts(e, 512)], start=(e == 0), stop=(e == EC - 1)),
                        deps=(ttok + [cx.pdep(pm)]) if e == 0 else ())
                t_r = em.op("act", lambda a, pm=pm: a.activation(rrelu[:], pm[:], AF.Relu),
                            deps=[mmlast])
                cx.puse(pm, t_r)
                t_u = em.op("act", lambda a: a.activation(u32[:], rrelu[:], AF.Square),
                            deps=[t_r])
                t_ub = em.op("dve", lambda v: v.tensor_copy(ubf[:], u32[:]), deps=[t_u])
                t_u2 = em.op("act", lambda a: a.activation(
                    u2d[:], u32[:], AF.Square, accum_out=part[:]), deps=[t_u, t_ub])
                col = b * NQC + tc
                em.op("dve", lambda v, col=col: v.tensor_tensor(
                    msacc[:, col:col + 1], msacc[:, col:col + 1], part[:], ALU.add),
                    deps=[t_u2, t_ms0])
                utok = []
                for j in range(4):
                    utok.append(transpose128(cx, uT[:, ts(j)], ubf[:, ts(j)],
                                             [t_ub, getattr(cx, "t_uspill", None)]))
                cx.t_uspill = em.dma("pool", u_dram[ts(grp, 512), col * 128:(col + 1) * 128]
                                     .rearrange("(hs p) t -> p hs t", p=128),
                                     uT[:].rearrange("p (hs t) -> p hs t", t=128), deps=utok)
    t_sq = em.op("act", lambda a: a.activation(rstd[:], msacc[:], AF.Sqrt,
                                               scale=1.0 / FFN_H, bias=cx.bias_eps[:]),
                 deps=[em.token("dve"), cx.t_bias])
    t_rsd = em.op("dve", lambda v: v.reciprocal(rstd[:], rstd[:]), deps=[t_sq])

    cx.free(wfc_bf, w1qT, rrelu, u32, ubf, u2d, part, uT)
    em.barrier()

    t_acc = []
    for i in range(B * NQC):
        t_acc.append(em.dma("sync", acc[i][:], acc_dram[ts(i), :]))

    w2_bf = cx.sb([128, 4 * 1024], BF16, "w2_bf")
    w2qT = cx.sb([128, 8 * E], BF16, "w2qT")
    hn_rd = cx.sb([128, 8 * 128], BF16, "hn_rd")
    tmp_o = cx.sb([128, 1024], F32, "tmp_o")
    ps_w = cx.pp["w"]
    HG = 1024
    for grp in range(FFN_H // HG):
        ttok = []
        for sl in range(4):
            tl = em.dma("pool", w2_bf[:].rearrange("p (dc h) -> p dc h", h=HG),
                        AP["L0_ffn_W_proj"][ts(sl, 512), grp * HG:(grp + 1) * HG]
                        .rearrange("(dc p) h -> p dc h", p=128),
                        deps=[getattr(cx, "pdep_w2", None)])
            qt = quantize_tile(cx, w2_bf[:], gscr, 4 * HG, [tl])
            for dc in range(4):
                d_abs = sl * 4 + dc
                for hs in range(HG // 128):
                    ttok.append(transpose128(
                        cx, w2qT[:, hs * E + d_abs * 128: hs * E + (d_abs + 1) * 128],
                        w2_bf[:, dc * HG + hs * 128: dc * HG + (hs + 1) * 128], [qt]))
            cx.pdep_w2 = ttok[-1]
        for b in range(B):
            for tc in range(NQC):
                col = b * NQC + tc
                t_rd = em.dma("sync", hn_rd[:].rearrange("p (hs t) -> p hs t", t=128),
                              u_dram[grp * HG:(grp + 1) * HG, ts(col)]
                              .rearrange("(hs p) t -> p hs t", p=128),
                              deps=[getattr(cx, "pdep_hn", None)])
                for half in range(2):
                    mmlast = None
                    for hs in range(HG // 128):
                        for pc in range(2):
                            mmlast = em.op("pe", lambda t, hs=hs, pc=pc, half=half: t.matmul(
                                ps_w[:, ts(pc, 512)], hn_rd[:, ts(hs)],
                                w2qT[:, hs * E + half * 1024 + pc * 512:
                                     hs * E + half * 1024 + (pc + 1) * 512],
                                start=(hs == 0), stop=(hs == HG // 128 - 1)),
                                deps=(ttok + [t_rd, cx.pdep(ps_w), t_rsd] + t_acc)
                                if (hs == 0 and pc == 0) else ())
                    cx.pdep_hn = mmlast
                    t_sc = em.op("dve", lambda v, col=col: v.tensor_scalar(
                        tmp_o[:], ps_w[:], rstd[:, col:col + 1], None, ALU.mult),
                        deps=[mmlast])
                    cx.puse(ps_w, t_sc)
                    em.op("pool", lambda g, col=col, half=half: g.tensor_tensor(
                        acc[col][:, half * 1024:(half + 1) * 1024],
                        acc[col][:, half * 1024:(half + 1) * 1024],
                        tmp_o[:], ALU.add), deps=[t_sc])
    cx.free(w2_bf, w2qT, hn_rd, tmp_o, gscr, rstd, msacc,
            cx.qz_qa, cx.qz_ratio, cx.qz_meanb)
    em.barrier()


def make_psum_pool(cx):
    cx.pp = {
        "a": cx.ps([128, 512], F32, "pp_a"),
        "c": cx.ps([128, 512], F32, "pp_c"),
        "row": cx.ps([128, 512], F32, "pp_row"),
        "o": cx.ps([128, 512], F32, "pp_o"),
        "tr0": cx.ps([128, 1024], BF16, "pp_tr0"),
        "tr1": cx.ps([128, 1024], BF16, "pp_tr1"),
        "w": cx.ps([128, 1024], F32, "pp_w"),
    }
    cx.tr_rot = 0


def build_launch_A():
    nc = bass.Bass("TRN2", target_bir_lowering=False)
    AP = declare_l0_inputs(nc)
    out_x1 = nc.dram_tensor("x1", [TOK, E], F32, kind="ExternalOutput").ap()
    kind = "ExternalOutput" if DEBUG else "Internal"
    u_dram = nc.dram_tensor("u_scratch", [FFN_H, TOK], BF16, kind=kind).ap()
    acc_dram = nc.dram_tensor("acc_scratch", [TOK, E], F32, kind=kind).ap()
    if DEBUG:
        nc._dbg_xn = nc.dram_tensor("d_xn", [E, TOK], BF16, kind="ExternalOutput").ap()
        nc._dbg_xnw = nc.dram_tensor("d_xnw", [256, WROWS], BF16, kind="ExternalOutput").ap()
        nc._dbg_k = nc.dram_tensor("d_k", [128, WROWS], BF16, kind="ExternalOutput").ap()
        nc._dbg_q = nc.dram_tensor("d_q", [128, OWN], BF16, kind="ExternalOutput").ap()
        nc._dbg_at = nc.dram_tensor("d_at", [128, OWN], BF16, kind="ExternalOutput").ap()
    em = Em(nc)
    cx = Ctx(nc, em)
    emit_consts(cx, AP)
    make_psum_pool(cx)
    xn_own = [cx.sb([128, TOK], BF16, f"xn_own{e}") for e in range(EC)]
    emit_l0(cx, AP, xn_own, u_dram, acc_dram)
    acc = [cx.sb([128, E], F32, f"accf{t}") for t in range(B * NQC)]
    emit_l0_ffn(cx, AP, xn_own, u_dram, acc_dram, acc)
    xtile = cx.sb([128, E], F32, "xres")
    st_toks = []
    for b in range(B):
        for tc in range(NQC):
            i = b * NQC + tc
            tl = em.dma("sync", xtile[:], AP["xw"][b, WIN + tc * 128: WIN + (tc + 1) * 128, :],
                        deps=[st_toks[-1] if st_toks else None])
            ta = em.op("dve", lambda v, i=i: v.tensor_tensor(
                acc[i][:], xtile[:], acc[i][:], ALU.add), deps=[tl])
            st_toks.append(em.dma("sync", out_x1[ts(i), :], acc[i][:], deps=[ta]))
    em.ops["sync"].append(("w", "sync", em.cnt["sync"]))
    em.replay()
    return nc


def declare_B_inputs(nc):
    ap = {}
    def inp(n, sh, dt=F32):
        ap[n] = nc.dram_tensor(n, sh, dt, kind="ExternalInput").ap()
    inp("x1in", [TOK, E])
    inp("c_ident", [128, 128], BF16)
    inp("c_prot", [128, 128])
    inp("L1_attn_Wq", [E, E]); inp("L1_attn_Wk", [E, E]); inp("L1_attn_Wv", [E, E])
    inp("L1_ffn_gate_kernel", [E, 8])
    return ap


def emit_l1pre(cx, AP, outs):
    em = cx.em
    NT = TOK // 128

    xn1f = [cx.sb([128, TOK], BF16, f"xn1f{e}") for e in range(EC)]
    lof = [cx.sb([128, TOK], BF16, f"lof{e}") for e in range(EC)]
    xt = cx.sb([128, E], F32, "b_x")
    xn32 = cx.sb([128, E], F32, "b_xn32")
    scr = cx.sb([128, E], F32, "b_scr")
    hibf = cx.sb([128, E], BF16, "b_hi")
    lobf = cx.sb([128, E], BF16, "b_lo")
    stats = cx.sb([128, 8], F32, "b_stat")
    g_hi = cx.sb([128, EC * 8], BF16, "g_hi")
    g_lo = cx.sb([128, EC * 8], BF16, "g_lo")
    g_32 = cx.sb([128, EC * 8], F32, "g_32")
    glog = cx.sb([128, 8], F32, "b_glog")
    gsc = cx.sb([128, 8], F32, "b_gsc")
    gse = cx.sb([128, 8], F32, "b_gse")

    t_g = em.dma("sync", g_32[:].rearrange("p (ec g) -> p ec g", g=8),
                 AP["L1_ffn_gate_kernel"].rearrange("(ec p) g -> p ec g", p=128))
    tg1 = em.op("dve", lambda v: v.tensor_copy(g_hi[:], g_32[:]), deps=[t_g])
    tg2 = em.op("dve", lambda v: v.tensor_tensor(g_32[:], g_32[:], g_hi[:], ALU.subtract),
                deps=[tg1])
    tg3 = em.op("dve", lambda v: v.tensor_copy(g_lo[:], g_32[:]), deps=[tg2])

    ps_g = cx.pp["a"]
    prev = None
    for t in range(NT):
        tl = em.dma("sync", xt[:], AP["x1in"][ts(t), :], deps=[prev])
        ln = layernorm_tile(cx, xt, xn32, scr, stats, [tl])
        c1 = em.op("dve", lambda v: v.tensor_copy(hibf[:], xn32[:]), deps=[ln])
        c2 = em.op("dve", lambda v: v.tensor_tensor(scr[:], xn32[:], hibf[:], ALU.subtract),
                   deps=[c1])
        c3 = em.op("dve", lambda v: v.tensor_copy(lobf[:], scr[:]), deps=[c2])
        last = None
        for e in range(EC):
            transpose128(cx, xn1f[e][:, ts(t)], hibf[:, ts(e)], [c1])
            last = transpose128(cx, lof[e][:, ts(t)], lobf[:, ts(e)], [c3])
        prev = last
        mml = None
        combos = [(e, src_t, gw) for e in range(EC)
                  for (src_t, gw) in ((xn1f, g_hi), (xn1f, g_lo), (lof, g_hi))]
        for idx, (e, src_t, gw) in enumerate(combos):
            mml = em.op("pe", lambda tt_, e=e, t=t, src_t=src_t, gw=gw, idx=idx: tt_.matmul(
                ps_g[:, 0:8], src_t[e][:, ts(t)], gw[:, e * 8:(e + 1) * 8],
                start=(idx == 0), stop=(idx == len(combos) - 1)),
                deps=[last, tg3, cx.pdep(ps_g)] if idx == 0 else ())
        tge = em.op("dve", lambda v: v.tensor_copy(glog[:], ps_g[:, 0:8]), deps=[mml])
        cx.puse(ps_g, tge)
        m1 = stats[:, 0:1]; m2 = stats[:, 1:2]; w1 = stats[:, 2:3]
        w2 = stats[:, 3:4]; dcol = stats[:, 4:5]
        g1 = em.op("dve", lambda v: v.tensor_reduce(m1, glog[:], AX.X, ALU.max), deps=[tge])
        g2 = em.op("dve", lambda v: v.tensor_scalar(gsc[:], glog[:], m1, None, ALU.is_ge),
                   deps=[g1])
        g3 = em.op("dve", lambda v: v.tensor_scalar(gse[:], gsc[:], -1e30, None, ALU.mult),
                   deps=[g2])
        g4 = em.op("dve", lambda v: v.tensor_tensor(gse[:], glog[:], gse[:], ALU.add),
                   deps=[g3])
        g5 = em.op("dve", lambda v: v.tensor_reduce(m2, gse[:], AX.X, ALU.max), deps=[g4])
        g6 = em.op("dve", lambda v: v.tensor_tensor(dcol, m1, m2, ALU.subtract), deps=[g5])
        g7 = em.op("act", lambda a: a.activation(w1, dcol, AF.Sigmoid), deps=[g6])
        g8 = em.op("dve", lambda v: v.tensor_scalar(w2, w1, -1.0, 1.0, ALU.mult, ALU.add),
                   deps=[g7])
        g9 = em.op("dve", lambda v: v.tensor_scalar(gsc[:], gsc[:], w1, None, ALU.mult),
                   deps=[g8])
        gA = em.op("dve", lambda v: v.tensor_scalar(gse[:], gse[:], m2, None, ALU.is_ge),
                   deps=[g9])
        gB = em.op("dve", lambda v: v.tensor_scalar(gse[:], gse[:], w2, None, ALU.mult),
                   deps=[gA])
        gC = em.op("dve", lambda v: v.tensor_tensor(gse[:], gse[:], gsc[:], ALU.add),
                   deps=[gB])
        em.dma("pool", outs["gsel"][ts(t), :], gse[:], deps=[gC])
    cx.free(g_hi, g_lo, g_32, glog, gsc, gse, xt, xn32, scr, hibf, lobf, stats)
    cx.free(*lof)

    wrow = cx.sb([128, E], BF16, "b_wrow")
    wTblk = cx.sb([128, EC * 128], BF16, "b_wT")
    vstage = cx.sb([128, TOK], BF16, "b_vstage")
    v1t = [cx.sb([128, E], BF16, f"b_v1t{t}") for t in range(TOK // 128)]
    ps_w = cx.pp["w"]
    t_prev_w = None
    for wname, oname, scale in (("L1_attn_Wq", "q1", 0.125),
                                ("L1_attn_Wk", "k1", 1.0),
                                ("L1_attn_Wv", None, 1.0)):
        for ic in range(EC):
            tl = em.dma("pool", wrow[:], AP[wname][ts(ic), :],
                        deps=[t_prev_w, em.token("pe")])
            ttok = []
            for e in range(EC):
                ttok.append(transpose128(cx, wTblk[:, ts(e)], wrow[:, ts(e)], [tl]))
            t_prev_w = ttok[-1]
            mml = None
            for e in range(EC):
                for pc in range(2):
                    mml = em.op("pe", lambda t_, e=e, pc=pc: t_.matmul(
                        ps_w[:, ts(pc, 512)], wTblk[:, ts(e)],
                        xn1f[e][:, ts(pc, 512)], start=(e == 0), stop=(e == EC - 1)),
                        deps=(ttok + [cx.pdep(ps_w)]) if (e == 0 and pc == 0) else ())
            tev = em.op("act", lambda a, scale=scale: a.activation(
                vstage[:], ps_w[:], AF.Copy, scale=scale),
                deps=[mml, getattr(cx, "t_spill_prev", None)])
            cx.puse(ps_w, tev)
            if oname is not None:
                cx.t_spill_prev = em.dma("pool", outs[oname][ts(ic), :], vstage[:], deps=[tev])
                t_prev_w = tev
            else:
                vt = []
                for t in range(TOK // 128):
                    vt.append(transpose128(cx, v1t[t][:, ts(ic)], vstage[:, ts(t)], [tev]))
                t_prev_w = vt[-1]
    for t in range(TOK // 128):
        em.dma("pool", outs["v1"][ts(t), :], v1t[t][:], deps=[em.token("dve")])
    for e in range(EC):
        em.dma("pool", outs["xn1f"][ts(e), :], xn1f[e][:], deps=[em.token("dve")])
    em.barrier()


def build_launch_B():
    nc = bass.Bass("TRN2", target_bir_lowering=False)
    AP = declare_B_inputs(nc)
    outs = {
        "gsel": nc.dram_tensor("gsel", [TOK, 8], F32, kind="ExternalOutput").ap(),
        "q1": nc.dram_tensor("q1", [E, TOK], BF16, kind="ExternalOutput").ap(),
        "k1": nc.dram_tensor("k1", [E, TOK], BF16, kind="ExternalOutput").ap(),
        "v1": nc.dram_tensor("v1", [TOK, E], BF16, kind="ExternalOutput").ap(),
        "xn1f": nc.dram_tensor("xn1f", [E, TOK], BF16, kind="ExternalOutput").ap(),
    }
    em = Em(nc)
    cx = Ctx(nc, em)
    emit_consts(cx, AP)
    make_psum_pool(cx)
    emit_l1pre(cx, AP, outs)
    em.ops["sync"].append(("w", "pool", em.cnt["pool"]))
    em.ops["sync"].append(("w", "sync", em.cnt["sync"]))
    em.replay()
    return nc


def declare_C_inputs(nc):
    ap = {}
    def inp(n, sh, dt=F32):
        ap[n] = nc.dram_tensor(n, sh, dt, kind="ExternalInput").ap()
    inp("x1in", [TOK, E])
    inp("c_ident", [128, 128], BF16)
    inp("c_prot", [128, 128])
    inp("xn1f", [E, TOK], BF16)
    inp("gselT", [8, TOK])
    inp("q1", [E, TOK], BF16)
    inp("k_all", [E, B * S], BF16)
    inp("v_all", [B * S, E], BF16)
    inp("L1_attn_Wo", [E, E])
    inp("lam_q1", [64, 1]); inp("lam_k1", [64, 1])
    inp("lam_q2", [64, 1]); inp("lam_k2", [64, 1])
    inp("subln", [128, 1])
    inp("L1_ffn_wi_0", [MOE_E, E, E]); inp("L1_ffn_wi_1", [MOE_E, E, E])
    inp("L1_ffn_wo", [MOE_E, E, E])
    return ap


def emit_lambda(cx, AP):
    em = cx.em
    cx.neglam_col = cx.sb([128, 1], F32, "neglam")
    cx.slw_col = cx.sb([128, 1], F32, "slwcol")
    lv = cx.sb([64, 4], F32, "lamv")
    l11 = cx.sb([1, 4], F32, "lam_sc")
    subc = cx.sb([128, 1], F32, "subc")
    tloads = [em.dma("sync", lv[:, i:i + 1], AP[n])
              for i, n in enumerate(("lam_q1", "lam_k1", "lam_q2", "lam_k2"))]
    t_su = em.dma("sync", subc[:], AP["subln"])
    pr = cx.pp["row"]
    t1 = em.op("pe", lambda t: t.matmul(pr[0:1, 0:1], lv[:, 0:1], lv[:, 1:2],
                                        start=True, stop=True),
               deps=tloads + [cx.pdep(pr)])
    e1 = em.op("act", lambda a: a.activation(l11[:, 0:1], pr[0:1, 0:1], AF.Exp), deps=[t1])
    cx.puse(pr, e1)
    t2 = em.op("pe", lambda t: t.matmul(pr[0:1, 0:1], lv[:, 2:3], lv[:, 3:4],
                                        start=True, stop=True),
               deps=[e1, cx.pdep(pr)])
    e2 = em.op("act", lambda a: a.activation(l11[:, 1:2], pr[0:1, 0:1], AF.Exp), deps=[t2])
    cx.puse(pr, e2)
    d1 = em.op("dve", lambda v: v.tensor_tensor(l11[:, 2:3], l11[:, 1:2], l11[:, 0:1],
                                                ALU.subtract), deps=[e1, e2])
    d2 = em.op("dve", lambda v: v.tensor_scalar(l11[:, 2:3], l11[:, 2:3],
                                                LAMBDA_INIT, None, ALU.subtract),
               deps=[d1])
    t3 = em.op("pe", lambda t: t.matmul(cx.pp["c"][:, 0:1], cx.ones_rf[:], l11[:, 2:3],
                                        start=True, stop=True),
               deps=[d2, cx.t_ones, cx.pdep(cx.pp["c"])])
    d3 = em.op("dve", lambda v: v.tensor_copy(cx.neglam_col[:], cx.pp["c"][:, 0:1]),
               deps=[t3])
    cx.puse(cx.pp["c"], d3)
    cx.t_lam = d3
    cx.t_slw = em.op("dve", lambda v: v.tensor_scalar(
        cx.slw_col[:], subc[:], 1.0 - LAMBDA_INIT, None, ALU.mult), deps=[t_su])
    cx.free(lv, l11, subc)


MOE_PARANOID = False


def emit_moe(cx, AP, x2acc, xn1f_src, t_xn, grow_fn=None, t_w0=None):
    em = cx.em
    NT = TOK // 128
    grow = cx.sb([1, TOK], F32, "c_grow")
    gbc = cx.sb([128, TOK], F32, "c_gbc")
    wiblk = [cx.sb([128, EC * 128], BF16, f"c_wi{i}") for i in range(4)]
    silu_sb = cx.sb([128, 512], BF16, "c_silu")
    hmul = cx.sb([128, TOK], BF16, "c_hmul")
    h_fm = [cx.sb([128, TOK], BF16, f"c_hfm{m}") for m in range(EC)]
    wov = lambda m: wiblk[m // 4][:, (m % 4) * 512:(m % 4 + 1) * 512]
    pA, pC = cx.pp["a"], cx.pp["c"]
    pO, pR = cx.pp["o"], cx.pp["row"]
    ps_w = cx.pp["w"]
    t_wi_prev = [t_w0] * 4
    t_wo_prev = [t_w0] * EC
    for ex in range(MOE_E):
        if grow_fn is None:
            tg = em.dma("sync", grow[:], AP["gselT"][ex:ex + 1, :],
                        deps=[getattr(cx, "t_gbc", None)])
        else:
            tg = grow_fn(ex, grow, [getattr(cx, "t_gbc", None)])
        btoks = []
        for pc in range(2):
            tb = em.op("pe", lambda t, pc=pc: t.matmul(
                pA[:], cx.ones_rf[:], grow[:, ts(pc, 512)],
                start=True, stop=True), deps=[tg, cx.t_ones, cx.pdep(pA)])
            tc = em.op("dve", lambda v, pc=pc: v.tensor_copy(
                gbc[:, ts(pc, 512)], pA[:]), deps=[tb])
            cx.puse(pA, tc)
            btoks.append(tc)
        cx.t_gbc = btoks[-1]
        for m in range(EC):
            bi = (m % 2) * 2
            t0 = em.dma("pool", wiblk[bi][:].rearrange("p (dc m) -> p dc m", m=128),
                        AP["L1_ffn_wi_0"][ex, :, ts(m)].rearrange("(dc p) m -> p dc m", p=128),
                        deps=[t_wi_prev[bi]])
            t1 = em.dma("pool", wiblk[bi + 1][:].rearrange("p (dc m) -> p dc m", m=128),
                        AP["L1_ffn_wi_1"][ex, :, ts(m)].rearrange("(dc p) m -> p dc m", p=128),
                        deps=[t_wi_prev[bi + 1]])
            mml0 = [None, None]
            mml1 = [None, None]
            for pc in range(2):
                for e in range(EC):
                    mml0[pc] = em.op("pe", lambda t, e=e, pc=pc, bi=bi: t.matmul(
                        [pA, pC][pc][:], wiblk[bi][:, ts(e)],
                        xn1f_src[e][:, ts(pc, 512)], start=(e == 0), stop=(e == EC - 1)),
                        deps=([t0, t1, t_xn, cx.pdep([pA, pC][pc])] + btoks) if e == 0 else ())
                for e in range(EC):
                    mml1[pc] = em.op("pe", lambda t, e=e, pc=pc, bi=bi: t.matmul(
                        [pO, pR][pc][:], wiblk[bi + 1][:, ts(e)],
                        xn1f_src[e][:, ts(pc, 512)], start=(e == 0), stop=(e == EC - 1)),
                        deps=[t1, cx.pdep([pO, pR][pc])] if e == 0 else ())
            t_wi_prev[bi] = mml0[1]
            t_wi_prev[bi + 1] = mml1[1]
            if MOE_PARANOID:
                em.barrier()
            s2 = None
            for pc in range(2):
                s1 = em.op("act", lambda a, pc=pc: a.activation(
                    silu_sb[:], [pA, pC][pc][:], AF.Silu), deps=[mml0[pc], s2])
                cx.puse([pA, pC][pc], s1)
                s2 = em.op("dve", lambda v, pc=pc: v.tensor_tensor(
                    hmul[:, ts(pc, 512)], silu_sb[:], [pO, pR][pc][:], ALU.mult),
                    deps=[s1, mml1[pc]])
                cx.puse([pO, pR][pc], s2)
                em.op("dve", lambda v, m=m, pc=pc: v.tensor_tensor(
                    h_fm[m][:, ts(pc, 512)], hmul[:, ts(pc, 512)], gbc[:, ts(pc, 512)],
                    ALU.mult), deps=[s2, getattr(cx, "t_moe_out_prev", None)] + btoks)
        if MOE_PARANOID:
            em.barrier()
        t_h = em.token("dve")
        t_hw = em.token("pe")
        for quar in range(4):
            wo_toks = []
            for m in range(EC):
                wo_toks.append(em.dma("pool", wov(m),
                                      AP["L1_ffn_wo"][ex, ts(m), quar * 512:(quar + 1) * 512],
                                      deps=[t_wo_prev[m], t_hw]))
            for tc in range(NT):
                mml = None
                for m in range(EC):
                    mml = em.op("pe", lambda t, m=m, tc=tc: t.matmul(
                        ps_w[:, 0:512], h_fm[m][:, ts(tc)],
                        wov(m), start=(m == 0), stop=(m == EC - 1)),
                        deps=(wo_toks + [t_h, cx.pdep(ps_w)]) if m == 0 else ())
                tadd = em.op("dve", lambda v, tc=tc, quar=quar: v.tensor_tensor(
                    x2acc[tc][:, quar * 512:(quar + 1) * 512],
                    x2acc[tc][:, quar * 512:(quar + 1) * 512],
                    ps_w[:, 0:512], ALU.add), deps=[mml])
                cx.puse(ps_w, tadd)
            for m in range(EC):
                t_wo_prev[m] = em.token("pe")
        cx.t_moe_out_prev = em.token("pe")
        t_wi_prev = [cx.t_moe_out_prev] * 4
        if MOE_PARANOID:
            em.barrier()
        if hasattr(cx, "dbg_hfm") and ex == MOE_E - 1:
            for m in range(EC):
                em.dma("pool", cx.dbg_hfm[ts(m), :], h_fm[m][:], deps=[em.token("dve")])
            em.dma("pool", cx.dbg_gbc, gbc[:], deps=[em.token("dve")])
            em.barrier()
    cx.free(grow, gbc, *wiblk, silu_sb, hmul, *h_fm)
    em.barrier()


def emit_l1attn(cx, x2acc, q1_src, k_src, v_src, t_qkv):
    em = cx.em
    attnf = [cx.sb([128, TOK], BF16, f"c_attn{h}") for h in range(L1_H)]
    khb = cx.sb([128, S], BF16, "c_khb")
    vhb = cx.sb([128, S], BF16, "c_vhb")
    ksq = cx.sb([128, S], BF16, "c_ksq")
    srow = cx.sb([1, 512], F32, "c_srow")
    bnd = cx.sb([1, 8], F32, "c_bnd")
    bndc = cx.sb([128, 1], F32, "c_bndc")
    pT = cx.sb([128, 512], BF16, "c_pT")
    rl = cx.sb([1, 512], F32, "c_rl")
    of = cx.sb([128, 512], F32, "c_of")
    o1n = cx.sb([128, 512], F32, "c_o1n")
    o2n = cx.sb([128, 512], F32, "c_o2n")
    apre = cx.sb([128, 512], F32, "c_apre")
    asq = cx.sb([128, 512], BF16, "c_asq")
    pA, pC, pO, pR = cx.pp["a"], cx.pp["c"], cx.pp["o"], cx.pp["row"]
    NKC1 = S // 128
    t_kv_prev = None
    for h in range(L1_H):
        for b in range(B):
            tk = k_src(h, b, khb, [t_kv_prev])
            tv = v_src(h, b, vhb, [t_kv_prev])
            tq2 = em.op("act", lambda a, h=h, b=b: a.activation(
                ksq[:, 0:512], q1_src[h][:, b * OWN:(b + 1) * OWN], AF.Square),
                deps=[t_qkv, t_kv_prev])
            qb_toks = []
            for half in range(2):
                hp = slice(half * 64, (half + 1) * 64)
                tm1 = em.op("pe", lambda t, hp=hp: t.matmul(
                    pR[0:1, 0:512], cx.ones_cb[hp, :], ksq[hp, 0:512],
                    start=True, stop=True), deps=[tq2, cx.t_ones, cx.pdep(pR)])
                tr1 = em.op("dve", lambda v, half=half: v.tensor_reduce(
                    bnd[:, half:half + 1], pR[0:1, 0:512], AX.X, ALU.max), deps=[tm1])
                cx.puse(pR, tr1)
                qb_toks.append(tr1)
            tk2 = em.op("act", lambda a: a.activation(ksq[:], khb[:], AF.Square),
                        deps=[tk] + qb_toks)
            kb_toks = []
            for half in range(2):
                hp = slice(half * 64, (half + 1) * 64)
                mmk = None
                for pc in range(8):
                    mmk = em.op("pe", lambda t, hp=hp, pc=pc: t.matmul(
                        pR[0:1, 0:512], cx.ones_cb[hp, :], ksq[hp, ts(pc, 512)],
                        start=(pc == 0), stop=(pc == 7)),
                        deps=[tk2, cx.t_ones, cx.pdep(pR)] if pc == 0 else ())
                trk = em.op("dve", lambda v, half=half: v.tensor_reduce(
                    bnd[:, 2 + half:3 + half], pR[0:1, 0:512], AX.X, ALU.max), deps=[mmk])
                cx.puse(pR, trk)
                kb_toks.append(trk)
            sq1 = em.op("act", lambda a: a.activation(bnd[:, 4:6], bnd[:, 0:2], AF.Sqrt),
                        deps=qb_toks + kb_toks)
            sq2 = em.op("act", lambda a: a.activation(bnd[:, 6:8], bnd[:, 2:4], AF.Sqrt),
                        deps=[sq1])
            bb1 = em.op("dve", lambda v: v.tensor_tensor(bnd[:, 4:6], bnd[:, 4:6],
                                                         bnd[:, 6:8], ALU.mult), deps=[sq2])
            bb2 = em.op("dve", lambda v: v.tensor_scalar(bnd[:, 4:6], bnd[:, 4:6],
                                                         -1.02, 70.0, ALU.mult, ALU.add),
                        deps=[bb1])
            if DEBUG and hasattr(cx, "dbg_bnd"):
                em.dma("pool", cx.dbg_bnd[h * B + b: h * B + b + 1, :], bnd[:], deps=[bb2])
            ton = None
            for half in range(2):
                hp = slice(half * 64, (half + 1) * 64)
                tb = em.op("pe", lambda t, half=half: t.matmul(
                    pC[:, 0:1], cx.ones_rf[:], bnd[0:1, 4 + half:5 + half],
                    start=True, stop=True), deps=[bb2, cx.t_ones, cx.pdep(pC)])
                tbc = em.op("dve", lambda v: v.tensor_copy(bndc[:], pC[:, 0:1]), deps=[tb])
                cx.puse(pC, tbc)
                tl = tvm = None
                texp_prev = None
                for kc in range(NKC1):
                    tmm = em.op("pe", lambda t, kc=kc, hp=hp, h=h, b=b: t.matmul(
                        pA[:], khb[hp, ts(kc)], q1_src[h][hp, b * OWN:(b + 1) * OWN],
                        start=True, stop=True), deps=[tk, t_qkv, cx.pdep(pA)])
                    tex = em.op("act", lambda a: a.activation(
                        pT[:], pA[:], AF.Exp, bias=bndc[:]), deps=[tmm, tbc, texp_prev])
                    cx.puse(pA, tex)
                    tl = em.op("pe", lambda t, kc=kc: t.matmul(
                        pR[0:1, 0:512], cx.ones_cb[:], pT[:],
                        start=(kc == 0), stop=(kc == NKC1 - 1)),
                        deps=[tex] + ([cx.pdep(pR)] if kc == 0 else []))
                    tvm = em.op("pe", lambda t, kc=kc: t.matmul(
                        pO[:], vhb[:, ts(kc)], pT[:],
                        start=(kc == 0), stop=(kc == NKC1 - 1)),
                        deps=[tv] + ([cx.pdep(pO)] if kc == 0 else []))
                    texp_prev = tvm
                trc = em.op("dve", lambda v: v.reciprocal(rl[:], pR[0:1, 0:512]),
                            deps=[tl, tvm])
                cx.puse(pR, trc)
                if DEBUG and hasattr(cx, "dbg_rl"):
                    em.dma("pool", cx.dbg_rl[(h * B + b) * 2 + half: (h * B + b) * 2 + half + 1, :],
                           rl[:], deps=[trc])
                tb2 = em.op("pe", lambda t: t.matmul(pC[:], cx.ones_rf[:], rl[:],
                                                     start=True, stop=True),
                            deps=[trc, cx.pdep(pC)])
                tof = em.op("dve", lambda v: v.tensor_copy(of[:], pO[:]), deps=[tvm, tb2])
                cx.puse(pO, tof)
                dst = o1n if half == 0 else o2n
                ton = em.op("dve", lambda v, dst=dst: v.tensor_tensor(
                    dst[:], of[:], pC[:], ALU.mult), deps=[tof])
                cx.puse(pC, ton)
            a1 = em.op("dve", lambda v: v.tensor_scalar(
                apre[:], o2n[:], cx.neglam_col[:], None, ALU.mult),
                deps=[ton, cx.t_lam])
            a2 = em.op("dve", lambda v: v.tensor_tensor(apre[:], apre[:], o1n[:], ALU.add),
                       deps=[a1])
            a3 = em.op("act", lambda a: a.activation(asq[:], apre[:], AF.Square), deps=[a2])
            a4 = em.op("pe", lambda t: t.matmul(pR[0:1, 0:512], cx.ones_cb[:], asq[:],
                                                start=True, stop=True),
                       deps=[a3, cx.pdep(pR)])
            a5 = em.op("act", lambda a: a.activation(srow[:], pR[0:1, 0:512], AF.Sqrt,
                                                     scale=1.0 / 128.0, bias=cx.bias_ln[0:1, :]),
                       deps=[a4, cx.t_bias])
            cx.puse(pR, a5)
            a6 = em.op("dve", lambda v: v.reciprocal(srow[:], srow[:]), deps=[a5])
            a7 = em.op("pe", lambda t: t.matmul(pC[:], cx.ones_rf[:], srow[:],
                                                start=True, stop=True),
                       deps=[a6, cx.pdep(pC)])
            a8 = em.op("dve", lambda v: v.tensor_tensor(apre[:], apre[:], pC[:], ALU.mult),
                       deps=[a7])
            cx.puse(pC, a8)
            a9 = em.op("act", lambda a, h=h, b=b: a.activation(
                attnf[h][:, b * OWN:(b + 1) * OWN], apre[:], AF.Copy, scale=cx.slw_col[:]),
                deps=[a8, cx.t_slw])
            t_kv_prev = a9
    cx.free(khb, vhb, ksq, srow, bnd, bndc, pT, rl, of, o1n, o2n, apre, asq)
    return attnf


def emit_l1wo(cx, AP, x2acc, attnf):
    em = cx.em
    NT = TOK // 128
    woL = cx.sb([128, 512], BF16, "c_woL")
    woT = cx.sb([128, 4 * E], BF16, "c_woT")
    ps_w = cx.pp["w"]
    t_attn = em.token("act")
    t_prev = None
    for g in range(4):
        ttok = []
        for jr in range(EC):
            tl = em.dma("pool", woL[:], AP["L1_attn_Wo"][ts(jr), ts(g, 512)],
                        deps=[t_prev])
            for isub in range(4):
                ttok.append(transpose128(
                    cx, woT[:, isub * E + jr * 128: isub * E + (jr + 1) * 128],
                    woL[:, ts(isub)], [tl]))
            t_prev = ttok[-1]
        for tc in range(NT):
            for half in range(2):
                mml = None
                for isub in range(4):
                    ic = g * 4 + isub
                    for pc in range(2):
                        mml = em.op("pe", lambda t, ic=ic, isub=isub, tc=tc, pc=pc, half=half: t.matmul(
                            ps_w[:, ts(pc, 512)], attnf[ic][:, ts(tc)],
                            woT[:, isub * E + half * 1024 + pc * 512:
                                isub * E + half * 1024 + (pc + 1) * 512],
                            start=(isub == 0), stop=(isub == 3)),
                            deps=(ttok + [t_attn, cx.pdep(ps_w)]) if (isub == 0 and pc == 0) else ())
                tadd = em.op("dve", lambda v, tc=tc, half=half: v.tensor_tensor(
                    x2acc[tc][:, half * 1024:(half + 1) * 1024],
                    x2acc[tc][:, half * 1024:(half + 1) * 1024],
                    ps_w[:], ALU.add), deps=[mml])
                cx.puse(ps_w, tadd)
    cx.free(woL, woT)


def build_launch_C():
    nc = bass.Bass("TRN2", target_bir_lowering=False)
    AP = declare_C_inputs(nc)
    out = nc.dram_tensor("x2", [TOK, E], F32, kind="ExternalOutput").ap()
    em = Em(nc)
    cx = Ctx(nc, em)
    emit_consts(cx, AP)
    make_psum_pool(cx)
    NT = TOK // 128
    x2acc = [cx.sb([128, E], F32, f"x2acc{t}") for t in range(NT)]
    for t in range(NT):
        em.dma("sync", x2acc[t][:], AP["x1in"][ts(t), :])
    emit_lambda(cx, AP)
    xn1f = [cx.sb([128, TOK], BF16, f"c_xn1f{e}") for e in range(EC)]
    for e in range(EC):
        em.dma("sync", xn1f[e][:], AP["xn1f"][ts(e), :])
    t_xn = em.token("sync")
    emit_moe(cx, AP, x2acc, xn1f, t_xn)
    if DEBUG:
        dm = nc.dram_tensor("d_moe", [TOK, E], F32, kind="ExternalOutput").ap()
        for t in range(NT):
            em.dma("pool", dm[ts(t), :], x2acc[t][:], deps=[em.token("dve")])
        em.barrier()
    cx.free(*xn1f)
    q1f = [cx.sb([128, TOK], BF16, f"c_q1f{h}") for h in range(L1_H)]
    for h in range(L1_H):
        em.dma("sync", q1f[h][:], AP["q1"][ts(h), :])
    t_q = em.token("sync")

    def k_src(h, b, dst, deps):
        return cx.em.dma("sync", dst[:], AP["k_all"][ts(h), b * S:(b + 1) * S],
                         deps=deps)

    def v_src(h, b, dst, deps):
        return cx.em.dma("sync", dst[:].rearrange("p (kc d) -> p kc d", d=128),
                         AP["v_all"][b * S:(b + 1) * S, ts(h)]
                         .rearrange("(kc p) d -> p kc d", p=128), deps=deps)

    if DEBUG:
        cx.dbg_rl = nc.dram_tensor("d_rl", [64, 512], F32, kind="ExternalOutput").ap()
        cx.dbg_bnd = nc.dram_tensor("d_bnd", [64, 8], F32, kind="ExternalOutput").ap()
    attnf = emit_l1attn(cx, x2acc, q1f, k_src, v_src, t_q)
    if DEBUG:
        da = nc.dram_tensor("d_attnf", [E, TOK], BF16, kind="ExternalOutput").ap()
        for h in range(L1_H):
            em.dma("pool", da[ts(h), :], attnf[h][:], deps=[em.token("act")])
        em.barrier()
    emit_l1wo(cx, AP, x2acc, attnf)
    em.barrier()
    for t in range(NT):
        em.dma("sync", out[ts(t), :], x2acc[t][:], deps=[em.token("dve")])
    em.ops["sync"].append(("w", "sync", em.cnt["sync"]))
    em.replay()
    return nc



def declare_fused_inputs(nc):
    ap = declare_l0_inputs(nc)
    def inp(n, sh, dt=F32):
        ap[n] = nc.dram_tensor(n, sh, dt, kind="ExternalInput").ap()
    inp("L1_attn_Wq", [E, E]); inp("L1_attn_Wk", [E, E]); inp("L1_attn_Wv", [E, E])
    inp("L1_ffn_gate_kernel", [E, 8])
    inp("L1_attn_Wo", [E, E])
    inp("lam_q1", [64, 1]); inp("lam_k1", [64, 1])
    inp("lam_q2", [64, 1]); inp("lam_k2", [64, 1])
    inp("subln", [128, 1])
    inp("L1_ffn_wi_0", [MOE_E, E, E]); inp("L1_ffn_wi_1", [MOE_E, E, E])
    inp("L1_ffn_wo", [MOE_E, E, E])
    return ap


def emit_l1pre_fused(cx, AP, x1tiles, xn1f, gselT_sb, qin, kin, vin):
    em = cx.em
    NT = TOK // 128

    lof = [cx.sb([128, TOK], BF16, f"f_lof{e}") for e in range(EC)]
    xn32 = cx.sb([128, E], F32, "f_xn32")
    scr = cx.sb([128, E], F32, "f_scr")
    hibf = cx.sb([128, E], BF16, "f_hi")
    lobf = cx.sb([128, E], BF16, "f_lo")
    stats = cx.sb([128, 8], F32, "f_stat")
    g_hi = cx.sb([128, EC * 8], BF16, "f_ghi")
    g_lo = cx.sb([128, EC * 8], BF16, "f_glo")
    g_32 = cx.sb([128, EC * 8], F32, "f_g32")
    glog = cx.sb([128, 8], F32, "f_glog")
    gsc = cx.sb([128, 8], F32, "f_gsc")
    gse = cx.sb([128, 8], F32, "f_gse")
    ident32 = cx.sb([128, 128], F32, "f_id32")

    t_id32 = em.op("dve", lambda v: v.tensor_copy(ident32[:], cx.ident[:]),
                   deps=[cx.t_ident])
    t_g = em.dma("sync", g_32[:].rearrange("p (ec g) -> p ec g", g=8),
                 AP["L1_ffn_gate_kernel"].rearrange("(ec p) g -> p ec g", p=128))
    tg1 = em.op("dve", lambda v: v.tensor_copy(g_hi[:], g_32[:]), deps=[t_g])
    tg2 = em.op("dve", lambda v: v.tensor_tensor(g_32[:], g_32[:], g_hi[:], ALU.subtract),
                deps=[tg1])
    tg3 = em.op("dve", lambda v: v.tensor_copy(g_lo[:], g_32[:]), deps=[tg2])

    ps_g = cx.pp["a"]
    for t in range(NT):
        ln = layernorm_tile(cx, x1tiles[t], xn32, scr, stats, [cx.t_x1])
        c1 = em.op("dve", lambda v: v.tensor_copy(hibf[:], xn32[:]), deps=[ln])
        c2 = em.op("dve", lambda v: v.tensor_tensor(scr[:], xn32[:], hibf[:], ALU.subtract),
                   deps=[c1])
        c3 = em.op("dve", lambda v: v.tensor_copy(lobf[:], scr[:]), deps=[c2])
        last = None
        for e in range(EC):
            transpose128(cx, xn1f[e][:, ts(t)], hibf[:, ts(e)], [c1])
            last = transpose128(cx, lof[e][:, ts(t)], lobf[:, ts(e)], [c3])
        mml = None
        combos = [(e, src_t, gw) for e in range(EC)
                  for (src_t, gw) in ((xn1f, g_hi), (xn1f, g_lo), (lof, g_hi))]
        for idx, (e, src_t, gw) in enumerate(combos):
            mml = em.op("pe", lambda tt_, e=e, t=t, src_t=src_t, gw=gw, idx=idx: tt_.matmul(
                ps_g[:, 0:8], src_t[e][:, ts(t)], gw[:, e * 8:(e + 1) * 8],
                start=(idx == 0), stop=(idx == len(combos) - 1)),
                deps=[last, tg3, cx.pdep(ps_g)] if idx == 0 else ())
        tge = em.op("dve", lambda v: v.tensor_copy(glog[:], ps_g[:, 0:8]), deps=[mml])
        cx.puse(ps_g, tge)
        m1 = stats[:, 0:1]; m2 = stats[:, 1:2]; w1 = stats[:, 2:3]
        w2 = stats[:, 3:4]; dcol = stats[:, 4:5]
        g1 = em.op("dve", lambda v: v.tensor_reduce(m1, glog[:], AX.X, ALU.max), deps=[tge])
        g2 = em.op("dve", lambda v: v.tensor_scalar(gsc[:], glog[:], m1, None, ALU.is_ge),
                   deps=[g1])
        g3 = em.op("dve", lambda v: v.tensor_scalar(gse[:], gsc[:], -1e30, None, ALU.mult),
                   deps=[g2])
        g4 = em.op("dve", lambda v: v.tensor_tensor(gse[:], glog[:], gse[:], ALU.add),
                   deps=[g3])
        g5 = em.op("dve", lambda v: v.tensor_reduce(m2, gse[:], AX.X, ALU.max), deps=[g4])
        g6 = em.op("dve", lambda v: v.tensor_tensor(dcol, m1, m2, ALU.subtract), deps=[g5])
        g7 = em.op("act", lambda a: a.activation(w1, dcol, AF.Sigmoid), deps=[g6])
        g8 = em.op("dve", lambda v: v.tensor_scalar(w2, w1, -1.0, 1.0, ALU.mult, ALU.add),
                   deps=[g7])
        g9 = em.op("dve", lambda v: v.tensor_scalar(gsc[:], gsc[:], w1, None, ALU.mult),
                   deps=[g8])
        gA = em.op("dve", lambda v: v.tensor_scalar(gse[:], gse[:], m2, None, ALU.is_ge),
                   deps=[g9])
        gB = em.op("dve", lambda v: v.tensor_scalar(gse[:], gse[:], w2, None, ALU.mult),
                   deps=[gA])
        gC = em.op("dve", lambda v: v.tensor_tensor(gse[:], gse[:], gsc[:], ALU.add),
                   deps=[gB])
        tt = em.op("pe", lambda t_: t_.transpose(cx.pp["c"][0:8, 0:128], gse[:], ident32[:]),
                   deps=[gC, t_id32, cx.pdep(cx.pp["c"])])
        te = em.op("dve", lambda v, t=t: v.tensor_copy(gselT_sb[:, ts(t)],
                                                       cx.pp["c"][0:8, 0:128]), deps=[tt])
        cx.puse(cx.pp["c"], te)
    cx.free(*(lof + [xn32, scr, hibf, lobf, stats, g_hi, g_lo, g_32,
                     glog, gsc, gse, ident32]))

    wrow = cx.sb([128, E], BF16, "f_wrow")
    wTblk = cx.sb([128, EC * 128], BF16, "f_wT")
    vstage = cx.sb([128, TOK], BF16, "f_vstage")
    v1t = [cx.sb([128, E], BF16, f"f_v1t{t}") for t in range(NT)]
    ps_w = cx.pp["w"]
    t_prev_w = None
    kv_toks = []
    for wname, kind, scale in (("L1_attn_Wq", "q", 0.125),
                               ("L1_attn_Wk", "k", 1.0),
                               ("L1_attn_Wv", "v", 1.0)):
        dst_dram = {"q": qin, "k": kin, "v": None}[kind]
        for ic in range(EC):
            tl = em.dma("pool", wrow[:], AP[wname][ts(ic), :],
                        deps=[t_prev_w, em.token("pe")])
            ttok = []
            for e in range(EC):
                ttok.append(transpose128(cx, wTblk[:, ts(e)], wrow[:, ts(e)], [tl]))
            t_prev_w = ttok[-1]
            mml = None
            for e in range(EC):
                for pc in range(2):
                    mml = em.op("pe", lambda t_, e=e, pc=pc: t_.matmul(
                        ps_w[:, ts(pc, 512)], wTblk[:, ts(e)],
                        xn1f[e][:, ts(pc, 512)], start=(e == 0), stop=(e == EC - 1)),
                        deps=(ttok + [cx.pdep(ps_w)]) if (e == 0 and pc == 0) else ())
            tev = em.op("act", lambda a, scale=scale: a.activation(
                vstage[:], ps_w[:], AF.Copy, scale=scale),
                deps=[mml, getattr(cx, "t_spill_prev", None)])
            cx.puse(ps_w, tev)
            if kind in ("q", "k"):
                tok = em.dma("pool", dst_dram[ts(ic), :], vstage[:], deps=[tev])
                cx.t_spill_prev = tok
                if kind == "k":
                    kv_toks.append(tok)
                t_prev_w = tok
            else:
                vt = []
                for t in range(NT):
                    vt.append(transpose128(cx, v1t[t][:, ts(ic)], vstage[:, ts(t)], [tev]))
                t_prev_w = vt[-1]
    for t in range(NT):
        kv_toks.append(em.dma("pool", vin[ts(t), :], v1t[t][:], deps=[em.token("dve")]))
    cx.free(wrow, wTblk, vstage, *v1t)
    return kv_toks


def build_fused():
    nc = bass.Bass("TRN2", target_bir_lowering=False)
    AP = declare_fused_inputs(nc)
    out = nc.dram_tensor("x2", [TOK, E], F32, kind="ExternalOutput").ap()
    u_dram = nc.dram_tensor("u_scratch", [FFN_H, TOK], BF16, kind="Internal").ap()
    acc_dram = nc.dram_tensor("acc_scratch", [TOK, E], F32, kind="Internal").ap()
    qin = nc.dram_tensor("qin", [E, TOK], BF16, kind="Internal").ap()
    kin = nc.dram_tensor("kin", [E, TOK], BF16, kind="Internal").ap()
    vin = nc.dram_tensor("vin", [TOK, E], BF16, kind="Internal").ap()
    kg = nc.dram_tensor("kg", [NCORES * E, TOK], BF16, kind="Internal").ap()
    vg = nc.dram_tensor("vg", [NCORES * TOK, E], BF16, kind="Internal").ap()
    gselT_dram = nc.dram_tensor("gselT_d", [8, TOK], F32, kind="Internal").ap()
    em = Em(nc)
    cx = Ctx(nc, em)
    emit_consts(cx, AP)
    make_psum_pool(cx)
    emit_lambda(cx, AP)
    NT = TOK // 128

    xn_own = [cx.sb([128, TOK], BF16, f"xn_own{e}") for e in range(EC)]
    emit_l0(cx, AP, xn_own, u_dram, acc_dram)
    acc = [cx.sb([128, E], F32, f"accf{t}") for t in range(B * NQC)]
    emit_l0_ffn(cx, AP, xn_own, u_dram, acc_dram, acc)
    xtile = cx.sb([128, E], F32, "f_xres")
    tprev = None
    for b in range(B):
        for tc in range(NQC):
            i = b * NQC + tc
            tl = em.dma("sync", xtile[:], AP["xw"][b, WIN + tc * 128: WIN + (tc + 1) * 128, :],
                        deps=[tprev])
            tprev = em.op("dve", lambda v, i=i: v.tensor_tensor(
                acc[i][:], xtile[:], acc[i][:], ALU.add), deps=[tl])
    cx.t_x1 = tprev
    if DEBUG:
        dx1 = nc.dram_tensor("d_x1", [TOK, E], F32, kind="ExternalOutput").ap()
        for t in range(NT):
            em.dma("pool", dx1[ts(t), :], acc[t][:], deps=[cx.t_x1])
        em.barrier()
    cx.free(xtile)

    gselT_sb = cx.sb([8, TOK], F32, "f_gselT")
    xn1f = [cx.sb([128, TOK], BF16, f"f_xn1f{e}") for e in range(EC)]
    kv_toks = emit_l1pre_fused(cx, AP, acc, xn1f, gselT_sb, qin, kin, vin)
    t_gd = em.dma("pool", gselT_dram, gselT_sb[:], deps=[em.token("dve")])
    if DEBUG:
        dxn = nc.dram_tensor("d_xn1f", [E, TOK], BF16, kind="ExternalOutput").ap()
        for e in range(EC):
            em.dma("pool", dxn[ts(e), :], xn1f[e][:], deps=[em.token("dve")])
        dgs = nc.dram_tensor("d_gselT", [8, TOK], F32, kind="ExternalOutput").ap()
        em.dma("pool", dgs, gselT_sb[:], deps=[em.token("dve")])
        em.barrier()

    t_ck = em.op("pool", lambda g: g.collective_compute(
        "AllGather", ALU.bypass, replica_groups=[list(range(NCORES))],
        ins=[kin.opt()], outs=[kg.opt()]), deps=kv_toks, inc=1)
    t_cv = em.op("pool", lambda g: g.collective_compute(
        "AllGather", ALU.bypass, replica_groups=[list(range(NCORES))],
        ins=[vin.opt()], outs=[vg.opt()]), deps=[t_ck], inc=1)

    if DEBUG:
        cx.dbg_hfm = nc.dram_tensor("d_hfm", [E, TOK], BF16, kind="ExternalOutput").ap()
        cx.dbg_gbc = nc.dram_tensor("d_gbc", [128, TOK], F32, kind="ExternalOutput").ap()

    def grow_fn(ex, grow, deps):
        return em.dma("sync", grow[:], gselT_dram[ex:ex + 1, :],
                      deps=list(deps) + [t_gd])
    emit_moe(cx, AP, acc, xn1f, em.token("dve"), grow_fn=grow_fn, t_w0=t_cv)
    if DEBUG:
        dm = nc.dram_tensor("d_moe", [TOK, E], F32, kind="ExternalOutput").ap()
        for t in range(NT):
            em.dma("pool", dm[ts(t), :], acc[t][:], deps=[em.token("dve")])
        dxn2 = nc.dram_tensor("d_xn1f_post", [E, TOK], BF16, kind="ExternalOutput").ap()
        for e in range(EC):
            em.dma("pool", dxn2[ts(e), :], xn1f[e][:], deps=[em.token("dve")])
        dgd = nc.dram_tensor("d_gselTd", [8, TOK], F32, kind="ExternalOutput").ap()
        em.dma("pool", dgd, gselT_dram, deps=[em.token("dve")])
        em.barrier()
    cx.free(*xn1f)

    t_gate = em.token("pe")
    q1f = [cx.sb([128, TOK], BF16, f"f_q1f{h}") for h in range(L1_H)]
    for h in range(L1_H):
        em.dma("sync", q1f[h][:], qin[ts(h), :], deps=[em.token("pool")])
    t_q = em.token("sync")

    def k_src(h, b, dst, deps):
        tok = None
        for r in range(NCORES):
            tok = em.dma("sync", dst[:, ts(r, 512)],
                         kg[r * E + h * 128: r * E + (h + 1) * 128,
                            b * OWN:(b + 1) * OWN],
                         deps=list(deps) + [t_ck, t_gate])
        return tok

    def v_src(h, b, dst, deps):
        tok = None
        for r in range(NCORES):
            tok = em.dma("sync",
                         dst[:, ts(r, 512)].rearrange("p (kc d) -> p kc d", d=128),
                         vg[r * TOK + b * OWN: r * TOK + (b + 1) * OWN, ts(h)]
                         .rearrange("(kc p) d -> p kc d", p=128),
                         deps=list(deps) + [t_cv, t_gate])
        return tok

    attnf = emit_l1attn(cx, acc, q1f, k_src, v_src, t_q)
    if DEBUG:
        da = nc.dram_tensor("d_attnf", [E, TOK], BF16, kind="ExternalOutput").ap()
        for h in range(L1_H):
            em.dma("pool", da[ts(h), :], attnf[h][:], deps=[em.token("act")])
        em.barrier()
    emit_l1wo(cx, AP, acc, attnf)
    em.barrier()
    for t in range(NT):
        em.dma("sync", out[ts(t), :], acc[t][:], deps=[em.token("dve")])
    em.ops["sync"].append(("w", "sync", em.cnt["sync"]))
    em.replay()
    return nc



FUSED = True


def kernel(**inputs):
    I = {k: np.asarray(v) for k, v in inputs.items()}
    if not FUSED:
        return _kernel_3launch(I)
    nc = build_fused()
    in_maps = []
    for c in range(NCORES):
        m = prep_core_inputs(c, I)
        m.update({
            "L1_attn_Wq": I["L1_attn_Wq"], "L1_attn_Wk": I["L1_attn_Wk"],
            "L1_attn_Wv": I["L1_attn_Wv"], "L1_attn_Wo": I["L1_attn_Wo"],
            "L1_ffn_gate_kernel": I["L1_ffn_gate_kernel"],
            "lam_q1": I["L1_attn_lambda_q1"].reshape(64, 1),
            "lam_k1": I["L1_attn_lambda_k1"].reshape(64, 1),
            "lam_q2": I["L1_attn_lambda_q2"].reshape(64, 1),
            "lam_k2": I["L1_attn_lambda_k2"].reshape(64, 1),
            "subln": I["L1_attn_subln_weight"].reshape(128, 1),
            "L1_ffn_wi_0": I["L1_ffn_wi_0"], "L1_ffn_wi_1": I["L1_ffn_wi_1"],
            "L1_ffn_wo": I["L1_ffn_wo"],
        })
        in_maps.append(m)
    res = run_bass_kernel_spmd(nc, in_maps, core_ids=list(range(NCORES)))
    out = np.zeros((B, S, E), np.float32)
    for c in range(NCORES):
        r = np.asarray(res.results[c]["x2"], np.float32)
        for b in range(B):
            out[b, c * OWN:(c + 1) * OWN, :] = r[b * OWN:(b + 1) * OWN, :]
    return out


def _kernel_3launch(I):
    ident = np.eye(128, dtype=np.float32).astype(BFNP)
    prot = np.eye(128, dtype=np.float32)[:, list(range(64, 128)) + list(range(64))].copy()

    ncA = build_launch_A()
    in_maps = [prep_core_inputs(c, I) for c in range(NCORES)]
    resA = run_bass_kernel_spmd(ncA, in_maps, core_ids=list(range(NCORES)))
    x1c = [np.asarray(resA.results[c]["x1"]) for c in range(NCORES)]

    ncB = build_launch_B()
    in_mapsB = [{
        "x1in": x1c[c], "c_ident": ident, "c_prot": prot,
        "L1_attn_Wq": I["L1_attn_Wq"], "L1_attn_Wk": I["L1_attn_Wk"],
        "L1_attn_Wv": I["L1_attn_Wv"],
        "L1_ffn_gate_kernel": I["L1_ffn_gate_kernel"],
    } for c in range(NCORES)]
    resB = run_bass_kernel_spmd(ncB, in_mapsB, core_ids=list(range(NCORES)))

    k_all = np.zeros((E, B * S), dtype=BFNP)
    v_all = np.zeros((B * S, E), dtype=BFNP)
    for c in range(NCORES):
        k1 = np.asarray(resB.results[c]["k1"])
        v1 = np.asarray(resB.results[c]["v1"])
        if k1.dtype.itemsize == 2 and k1.dtype != BFNP:
            k1 = k1.view(BFNP); v1 = v1.view(BFNP)
        for b in range(B):
            k_all[:, b * S + c * OWN:(b * S + (c + 1) * OWN)] = k1[:, b * OWN:(b + 1) * OWN]
            v_all[b * S + c * OWN:(b * S + (c + 1) * OWN), :] = v1[b * OWN:(b + 1) * OWN, :]

    ncC = build_launch_C()
    in_mapsC = []
    for c in range(NCORES):
        q1 = np.asarray(resB.results[c]["q1"])
        xn1f = np.asarray(resB.results[c]["xn1f"])
        if q1.dtype.itemsize == 2 and q1.dtype != BFNP:
            q1 = q1.view(BFNP); xn1f = xn1f.view(BFNP)
        gsel = np.asarray(resB.results[c]["gsel"], np.float32)
        in_mapsC.append({
            "x1in": x1c[c], "c_ident": ident, "c_prot": prot,
            "xn1f": xn1f, "gselT": np.ascontiguousarray(gsel.T),
            "q1": q1, "k_all": k_all, "v_all": v_all,
            "L1_attn_Wo": I["L1_attn_Wo"],
            "lam_q1": I["L1_attn_lambda_q1"].reshape(64, 1),
            "lam_k1": I["L1_attn_lambda_k1"].reshape(64, 1),
            "lam_q2": I["L1_attn_lambda_q2"].reshape(64, 1),
            "lam_k2": I["L1_attn_lambda_k2"].reshape(64, 1),
            "subln": I["L1_attn_subln_weight"].reshape(128, 1),
            "L1_ffn_wi_0": I["L1_ffn_wi_0"], "L1_ffn_wi_1": I["L1_ffn_wi_1"],
            "L1_ffn_wo": I["L1_ffn_wo"],
        })
    resC = run_bass_kernel_spmd(ncC, in_mapsC, core_ids=list(range(NCORES)))

    out = np.zeros((B, S, E), np.float32)
    for c in range(NCORES):
        r = np.asarray(resC.results[c]["x2"], np.float32)
        for b in range(B):
            out[b, c * OWN:(c + 1) * OWN, :] = r[b * OWN:(b + 1) * OWN, :]
    return out


def prep_core_inputs(core, I):
    q0 = core * OWN
    x = I["L0_attn_hidden_states"]
    xw = np.zeros((B, WROWS, E), np.float32)
    lo, hi = q0 - WIN, q0 + OWN + WIN
    slo, shi = max(lo, 0), min(hi, S)
    xw[:, slo - lo:shi - lo, :] = x[:, slo:shi, :]
    pos = np.clip(np.arange(lo, hi), 0, S - 1)
    cosw, sinw = _rope_tables_np(pos)
    return {
        "xw": np.ascontiguousarray(xw),
        "c_ident": np.eye(128, dtype=np.float32).astype(BFNP),
        "c_prot": np.eye(128, dtype=np.float32)[:, list(range(64, 128)) + list(range(64))].copy(),
        "c_cosw": cosw, "c_sinw": sinw,
        "c_l0mask": l0_mask_values(core),
        "qscale": np.ascontiguousarray(I["L0_attn_query_scale"].reshape(128, 1)),
        "kscale": np.ascontiguousarray(I["L0_attn_key_scale"].reshape(128, 1)),
        "L0_attn_wq": I["L0_attn_wq"], "L0_attn_wk": I["L0_attn_wk"],
        "L0_attn_wv": I["L0_attn_wv"], "L0_attn_wo": I["L0_attn_wo"],
        "L0_ffn_W_fc": I["L0_ffn_W_fc"], "L0_ffn_W_proj": I["L0_ffn_W_proj"],
    }
